# version 03
# baseline (speedup 1.0000x reference)
import numpy as np
from collections import defaultdict
from contextlib import ExitStack

import concourse.bass as bass
import concourse.mybir as mybir
from concourse.bass_utils import run_bass_kernel_spmd

F32 = mybir.dt.float32
BF16 = mybir.dt.bfloat16
AF = mybir.ActivationFunctionType
ALU = mybir.AluOpType
AX = mybir.AxisListType

DM = 2048
KC = DM // 128
NHEAD = 16
DH = 128
HSH = 8192
HR = 1024
NE = 8
NT = 512
CH = 128
CPT = NT // CH
LN_EPS = 1e-5
ATT_EPS = 1e-6


class CFG:
    def __init__(self, n_cores, batch, seqlen):
        self.NC = n_cores
        self.B = batch
        self.S = seqlen
        self.TCORE = batch * seqlen // n_cores
        assert self.TCORE % NT == 0
        self.NTILES = self.TCORE // NT
        self.NCHUNK = self.TCORE // CH
        self.HPS = n_cores // batch
        assert seqlen % (self.HPS * CH) == 0


class Buf:
    __slots__ = ("w", "r")
    def __init__(self):
        self.w = {}
        self.r = {}


def _merge(dst, tok):
    for s, v in tok.items():
        if dst.get(s, -1) < v:
            dst[s] = v


class Sched:
    def __init__(self):
        self.ops = []
        self.cnt = defaultdict(int)

    def op(self, eng, fn, reads=(), writes=(), extra=None, inc=None, inc_n=1):
        waits = {}
        if extra:
            _merge(waits, extra)
        for b in reads:
            _merge(waits, b.w)
        for b in writes:
            _merge(waits, b.w)
            _merge(waits, b.r)
        sem = inc if inc is not None else eng
        self.cnt[sem] += inc_n
        tok = {sem: self.cnt[sem]}
        self.ops.append((eng, fn, waits, sem, inc_n))
        for b in reads:
            _merge(b.r, tok)
        for b in writes:
            b.w = dict(tok)
            b.r = {}
        return tok


def build_nc(cfg: CFG):
    nc = bass.Bass("TRN2", target_bir_lowering=False, debug=False)
    T = cfg.NTILES

    def ein(name, shape, dt=F32):
        return nc.dram_tensor(name, shape, dt, kind="ExternalInput").ap()

    U = KC * 256
    x_in = ein("x", [T, 128, KC * NT])
    sw1p = ein("sw1p", [HSH // 256, 128, U])
    sw2p = ein("sw2p", [HSH // 256, 128, U])
    sw3p = ein("sw3p", [KC // 2, 4, 128, U])
    w12p1 = ein("w12p1", [NE, HR // 256, 128, U])
    w12p2 = ein("w12p2", [NE, HR // 256, 128, U])
    w3p = ein("w3p", [KC // 2, 4, 128, U])
    wqp = ein("wqp", [KC // 2, 128, U])
    wkp = ein("wkp", [KC // 2, 128, U])
    wvp = ein("wvp", [KC // 2, 128, U])
    wop = ein("wop", [KC // 2, 128, U])
    routp = ein("routp", [128, KC, NE])
    ebbc_in = ein("ebbc", [128, NE])
    bq_in = ein("bq", [128, KC])
    bk_in = ein("bk", [128, KC])
    bv_in = ein("bv", [128, KC])
    bo_in = ein("bo", [128, KC])
    iden_b_in = ein("iden_b", [128, 128], BF16)
    iden_f_in = ein("iden_f", [128, 128])
    ones1_in = ein("ones1", [1, 128])
    ones1b_in = ein("ones1b", [1, 128], BF16)
    colones_in = ein("colones", [128, 1], BF16)
    triu_in = ein("triu", [128, 128])
    prevm_in = ein("prevm", [128, cfg.NC])

    y_out = nc.dram_tensor("y", [T, 128, KC * NT], F32, kind="ExternalOutput").ap()

    def idram(name, shape, dt):
        return nc.dram_tensor(name, shape, dt).ap()

    sw1b = idram("sw1b", [HSH // 256, 128, U], BF16)
    sw2b = idram("sw2b", [HSH // 256, 128, U], BF16)
    sw3b = idram("sw3b", [KC // 2, 4, 128, U], BF16)
    w12b1 = idram("w12b1", [NE, HR // 256, 128, U], BF16)
    w12b2 = idram("w12b2", [NE, HR // 256, 128, U], BF16)
    w3b = idram("w3b", [KC // 2, 4, 128, U], BF16)
    wqb = idram("wqb", [KC // 2, 128, U], BF16)
    wkb = idram("wkb", [KC // 2, 128, U], BF16)
    wvb = idram("wvb", [KC // 2, 128, U], BF16)
    wob = idram("wob", [KC // 2, 128, U], BF16)

    x1_d = idram("x1_d", [T, 128, KC * NT], F32)
    qf_d = idram("qf_d", [T, 128, KC * NT], BF16)
    kf_d = idram("kf_d", [T, 128, KC * NT], BF16)
    v_d = idram("v_d", [cfg.NCHUNK, CH, DM], BF16)
    cc_in = idram("cc_in", [128, NHEAD * DH + NHEAD], F32)
    cc_out = idram("cc_out", [cfg.NC * 128, NHEAD * DH + NHEAD], F32)

    es = ExitStack()
    sb = lambda n, c, dt: es.enter_context(nc.sbuf_tensor(n, [128, c], dt))

    xs = sb("xs", KC * NT, F32)
    ys = sb("ys", KC * NT, BF16)
    hs = sb("hs", 2 * KC * NT, BF16)
    accs = sb("accs", KC * NT, F32)
    combs = sb("combs", NE * NT, BF16)
    vfm = sb("vfm", NHEAD * CH, BF16)
    atb = vfm
    wg = sb("wg", 2 * KC * 256, BF16)
    wg2 = sb("wg2", 2 * KC * 256, BF16)
    wo3 = sb("wo3", 2 * KC * 256, BF16)
    kftm = sb("kftm", NHEAD * CH, BF16)
    attntm = sb("attntm", NHEAD * CH, BF16)
    kvs = sb("kvs", NHEAD * DH, F32)
    kvb = sb("kvb", NHEAD * DH, BF16)
    ksums = sb("ksums", NHEAD, F32)
    ksumb = sb("ksumb", NHEAD, BF16)
    routs = sb("routs", 2 * KC * NE, BF16)
    bqs = sb("bqs", KC, F32)
    bks = sb("bks", KC, F32)
    bvs = sb("bvs", KC, F32)
    bos = sb("bos", KC, F32)
    ebbc = sb("ebbcS", NE, F32)
    iden_b = sb("iden_bS", 128, BF16)
    iden_f = sb("iden_fS", 128, F32)
    colones = sb("colonesS", 1, BF16)
    triu = sb("triuS", 128, F32)
    prevm = sb("prevmS", cfg.NC, F32)
    stm = sb("stm", 64, F32)
    zs = sb("zs", NHEAD, F32)
    zinv = sb("zinv", NHEAD, F32)
    ones1 = es.enter_context(nc.sbuf_tensor("ones1S", [1, 128], F32))
    ones1b = es.enter_context(nc.sbuf_tensor("ones1bS", [1, 128], BF16))
    rowb = es.enter_context(nc.sbuf_tensor("rowb", [1, 3 * NT], F32))
    scor = es.enter_context(nc.sbuf_tensor("scor", [NE, NT], F32))
    combr = es.enter_context(nc.sbuf_tensor("combr", [1, NE * CH], BF16))
    attntm_silu = attntm
    silu2 = kftm

    P = [es.enter_context(nc.psum_tensor(f"P{i}", [128, 512], F32)) for i in range(6)]
    PT = es.enter_context(nc.psum_tensor("PT", [128, 2048], BF16))

    NDSEM = 6
    sem_names = (["pe", "act", "dve", "coll"]
                 + [f"sq{i}" for i in range(NDSEM)]
                 + [f"gp{i}" for i in range(NDSEM)]
                 + [f"aq{i}" for i in range(NDSEM)])
    sems = {n: es.enter_context(nc.semaphore(f"s_{n}")) for n in sem_names}

    S = Sched()
    dma_ring = {"sq": 0, "gp": 0, "aq": 0}

    b_xs, b_ys, b_hsA, b_hsB, b_accs, b_vfm = (Buf() for _ in range(6))
    b_atb = b_vfm
    b_combs = [Buf(), Buf()]
    b_wg = [Buf(), Buf()]
    b_wg2 = [Buf(), Buf()]
    b_wo3 = [Buf(), Buf()]
    b_att_sl = [Buf(), Buf()]
    b_kftm_sl = [Buf(), Buf()]
    b_kvs, b_kvb, b_ksums, b_ksumb = Buf(), Buf(), Buf(), Buf()
    b_stm, b_row, b_scor, b_combr, b_z, b_zinv = (Buf() for _ in range(6))
    b_routs, b_rtmp = Buf(), Buf()
    b_P = [Buf() for _ in range(6)]
    b_PT = [Buf(), Buf()]
    b_x1d = [Buf() for _ in range(T)]
    b_qfd = [Buf() for _ in range(T)]
    b_kfd = [Buf() for _ in range(T)]
    b_vd = [Buf() for _ in range(cfg.NCHUNK)]
    b_ccin, b_ccout = Buf(), Buf()
    b_wcvt = defaultdict(Buf)

    HSB = KC * NT

    def opp(eng, fn, reads=(), writes=(), extra=None, inc=None, inc_n=1):
        return S.op(eng, fn, reads=reads, writes=writes, extra=extra, inc=inc, inc_n=inc_n)

    def dma(queue, out_ap, in_ap, reads=(), writes=(), extra=None):
        i = dma_ring[queue]
        dma_ring[queue] = (i + 1) % NDSEM
        sem = f"{queue}{i}"
        slot_free = {sem: S.cnt[sem]}
        if extra:
            slot_free = dict(slot_free)
            _merge(slot_free, extra)
        return opp(queue, lambda e: e.dma_start(out=out_ap, in_=in_ap),
                   reads=reads, writes=writes, extra=slot_free, inc=sem, inc_n=16)

    def mm(out_ap, lhsT_ap, rhs_ap, start, stop, reads=(), writes=(), extra=None):
        return opp("pe", lambda e, o=out_ap, l=lhsT_ap, r=rhs_ap, st=start, sp=stop:
                   e.matmul(o, l, r, start=st, stop=sp),
                   reads=reads, writes=writes, extra=extra)

    def act(fn, reads=(), writes=(), extra=None):
        return opp("act", fn, reads=reads, writes=writes, extra=extra)

    def dve(fn, reads=(), writes=(), extra=None):
        return opp("dve", fn, reads=reads, writes=writes, extra=extra)

    def pe(fn, reads=(), writes=(), extra=None):
        return opp("pe", fn, reads=reads, writes=writes, extra=extra)

    tok_consts = {}
    for dst, srcap in ((bqs, bq_in), (bks, bk_in), (bvs, bv_in), (bos, bo_in),
                       (ebbc, ebbc_in), (iden_b, iden_b_in), (iden_f, iden_f_in),
                       (ones1, ones1_in), (ones1b, ones1b_in), (colones, colones_in),
                       (triu, triu_in), (prevm, prevm_in)):
        _merge(tok_consts, dma("sq", dst[:, :], srcap[:, :], writes=[Buf()]))

    RW = KC * NE
    dma("sq", accs[:, 0:RW], routp.rearrange("p k e -> p (k e)"), writes=[b_accs])
    dve(lambda e: e.tensor_copy(routs[:, 0:RW], accs[:, 0:RW]),
        reads=[b_accs], writes=[b_routs])
    dve(lambda e: e.tensor_copy(accs[:, RW:2 * RW], routs[:, 0:RW]),
        reads=[b_routs, b_accs], writes=[b_accs])
    dve(lambda e: e.tensor_sub(accs[:, RW:2 * RW], accs[:, 0:RW], accs[:, RW:2 * RW]),
        reads=[b_accs], writes=[b_accs])
    dve(lambda e: e.tensor_copy(routs[:, RW:2 * RW], accs[:, RW:2 * RW]),
        reads=[b_accs], writes=[b_routs])

    def cvt(key, out_ap, in_ap):
        return dma("gp", out_ap, in_ap, writes=[b_wcvt[key]])

    for blk in range(16):
        cvt(("sw1b", blk), sw1b[blk], sw1p[blk])
        cvt(("sw2b", blk), sw2b[blk], sw2p[blk])
    for mp in range(KC // 2):
        for u_ in (0, 1):
            cvt(("sw3b", mp, u_), sw3b[mp, u_], sw3p[mp, u_])
    for blk in range(16, 32):
        cvt(("sw1b", blk), sw1b[blk], sw1p[blk])
        cvt(("sw2b", blk), sw2b[blk], sw2p[blk])
    for mp in range(KC // 2):
        for u_ in (2, 3):
            cvt(("sw3b", mp, u_), sw3b[mp, u_], sw3p[mp, u_])
    for e_ in range(4):
        for jp in range(HR // 256):
            cvt(("w12b1", e_, jp), w12b1[e_, jp], w12p1[e_, jp])
            cvt(("w12b2", e_, jp), w12b2[e_, jp], w12p2[e_, jp])
    for mp in range(KC // 2):
        for u_ in (0, 1):
            cvt(("w3b", mp, u_), w3b[mp, u_], w3p[mp, u_])
    for e_ in range(4, NE):
        for jp in range(HR // 256):
            cvt(("w12b1", e_, jp), w12b1[e_, jp], w12p1[e_, jp])
            cvt(("w12b2", e_, jp), w12b2[e_, jp], w12p2[e_, jp])
    for mp in range(KC // 2):
        for u_ in (2, 3):
            cvt(("w3b", mp, u_), w3b[mp, u_], w3p[mp, u_])
    for nm, dst, src in (("wqb", wqb, wqp), ("wkb", wkb, wkp), ("wvb", wvb, wvp)):
        for jp in range(KC // 2):
            cvt((nm, jp), dst[jp], src[jp])
    for mp in range(KC // 2):
        cvt(("wob", mp), wob[mp], wop[mp])

    def arena_tile(arena, slot, kc, c0, w):
        off = slot * KC * 256 + kc * 256 + c0
        return arena[:, off:off + w]

    slot_state = {}

    def stream_unit(arena, b_arena, key, dram_slice, queue):
        k_ = id(arena)
        s_ = slot_state.get(k_, 0)
        slot_state[k_] = s_ + 1
        slot = s_ % 2
        tok = dma(queue, arena[:, slot * U: (slot + 1) * U], dram_slice,
                  reads=[b_wcvt[key]], writes=[b_arena[slot]])
        return slot, tok

    def emit_ln(src_buf, src_sb):
        for k in range(KC):
            dve(lambda e, k=k: e.tensor_copy(hs[:, k * NT:(k + 1) * NT],
                                             src_sb[:, k * NT:(k + 1) * NT]),
                reads=[src_buf], writes=[b_hsA])
            act(lambda e, k=k: e.activation(hs[:, HSB + k * NT:HSB + (k + 1) * NT],
                                            hs[:, k * NT:(k + 1) * NT], AF.Square),
                reads=[b_hsA], writes=[b_hsB])
        for k in range(KC):
            mm(P[3][0:1, 0:NT], colones[:, :], hs[:, k * NT:(k + 1) * NT],
               start=(k == 0), stop=(k == KC - 1), reads=[b_hsA], writes=[b_P[3]],
               extra=tok_consts)
        for k in range(KC):
            mm(P[4][0:1, 0:NT], colones[:, :], hs[:, HSB + k * NT:HSB + (k + 1) * NT],
               start=(k == 0), stop=(k == KC - 1), reads=[b_hsB], writes=[b_P[4]])
        r = rowb
        dve(lambda e: e.tensor_scalar_mul(r[:, 0:NT], P[3][0:1, 0:NT], 1.0 / DM),
            reads=[b_P[3]], writes=[b_row])
        dve(lambda e: e.tensor_scalar_mul(r[:, NT:2 * NT], P[4][0:1, 0:NT], 1.0 / DM),
            reads=[b_P[4]], writes=[b_row])
        dve(lambda e: e.tensor_mul(r[:, 2 * NT:3 * NT], r[:, 0:NT], r[:, 0:NT]),
            reads=[b_row], writes=[b_row])
        dve(lambda e: e.tensor_sub(r[:, NT:2 * NT], r[:, NT:2 * NT], r[:, 2 * NT:3 * NT]),
            reads=[b_row], writes=[b_row])
        dve(lambda e: e.tensor_scalar_add(r[:, NT:2 * NT], r[:, NT:2 * NT], LN_EPS),
            reads=[b_row], writes=[b_row])
        act(lambda e: e.activation(r[:, NT:2 * NT], r[:, NT:2 * NT], AF.Ln),
            reads=[b_row], writes=[b_row])
        dve(lambda e: e.tensor_scalar_mul(r[:, NT:2 * NT], r[:, NT:2 * NT], -0.5),
            reads=[b_row], writes=[b_row])
        act(lambda e: e.activation(r[:, NT:2 * NT], r[:, NT:2 * NT], AF.Exp),
            reads=[b_row], writes=[b_row])
        dve(lambda e: e.tensor_mul(r[:, 2 * NT:3 * NT], r[:, 0:NT], r[:, NT:2 * NT]),
            reads=[b_row], writes=[b_row])
        mm(P[4][:, 0:NT], ones1[:, :], r[:, NT:2 * NT], start=True, stop=True,
           reads=[b_row], writes=[b_P[4]], extra=tok_consts)
        mm(P[5][:, 0:NT], ones1[:, :], r[:, 2 * NT:3 * NT], start=True, stop=True,
           reads=[b_row], writes=[b_P[5]])
        for k in range(KC):
            dve(lambda e, k=k: e.tensor_mul(accs[:, 0:NT], src_sb[:, k * NT:(k + 1) * NT],
                                            P[4][:, 0:NT]),
                reads=[src_buf, b_P[4]], writes=[b_accs])
            dve(lambda e, k=k: e.tensor_sub(ys[:, k * NT:(k + 1) * NT], accs[:, 0:NT],
                                            P[5][:, 0:NT]),
                reads=[b_accs, b_P[5]], writes=[b_ys])

    def emit_router():
        for k in range(KC):
            mm(P[0][0:NE, 0:NT], routs[:, k * NE:(k + 1) * NE], ys[:, k * NT:(k + 1) * NT],
               start=(k == 0), stop=False, reads=[b_routs, b_ys], writes=[b_P[0]])
        for k in range(KC):
            mm(P[0][0:NE, 0:NT], routs[:, KC * NE + k * NE:KC * NE + (k + 1) * NE],
               ys[:, k * NT:(k + 1) * NT],
               start=False, stop=(k == KC - 1), reads=[b_routs, b_ys], writes=[b_P[0]])
        dve(lambda e: e.tensor_scalar_mul(scor[:, :], P[0][0:NE, 0:NT], 0.5),
            reads=[b_P[0]], writes=[b_scor])
        act(lambda e: e.activation(scor[:, :], scor[:, :], AF.Tanh),
            reads=[b_scor], writes=[b_scor])
        dve(lambda e: e.tensor_scalar(scor[:, :], scor[:, :], 0.5, 0.5, ALU.mult, ALU.add),
            reads=[b_scor], writes=[b_scor])
        for c in range(CPT):
            pe(lambda e, c=c: e.transpose(P[1][:, 0:NE], scor[:, c * CH:(c + 1) * CH],
                                          iden_f[0:NE, 0:NE]),
               reads=[b_scor], writes=[b_P[1]], extra=tok_consts)
            t = stm
            dve(lambda e: e.tensor_copy(t[:, 0:8], P[1][:, 0:NE]),
                reads=[b_P[1]], writes=[b_stm])
            dve(lambda e: e.tensor_add(t[:, 8:16], t[:, 0:8], ebbc[:, :]),
                reads=[b_stm], writes=[b_stm], extra=tok_consts)
            dve(lambda e: e.tensor_reduce(t[:, 16:17], t[:, 8:16], AX.X, ALU.max),
                reads=[b_stm], writes=[b_stm])
            dve(lambda e: e.tensor_scalar(t[:, 24:32], t[:, 8:16], t[:, 16:17], None, ALU.is_ge),
                reads=[b_stm], writes=[b_stm])
            dve(lambda e: e.scalar_tensor_tensor(t[:, 32:40], t[:, 24:32], -1e9, t[:, 8:16],
                                                 ALU.mult, ALU.add),
                reads=[b_stm], writes=[b_stm])
            dve(lambda e: e.tensor_reduce(t[:, 17:18], t[:, 32:40], AX.X, ALU.max),
                reads=[b_stm], writes=[b_stm])
            dve(lambda e: e.tensor_scalar(t[:, 40:48], t[:, 32:40], t[:, 17:18], None, ALU.is_ge),
                reads=[b_stm], writes=[b_stm])
            dve(lambda e: e.tensor_add(t[:, 40:48], t[:, 40:48], t[:, 24:32]),
                reads=[b_stm], writes=[b_stm])
            dve(lambda e: e.tensor_mul(t[:, 48:56], t[:, 0:8], t[:, 40:48]),
                reads=[b_stm], writes=[b_stm])
            dve(lambda e: e.tensor_reduce(t[:, 18:19], t[:, 48:56], AX.X, ALU.add),
                reads=[b_stm], writes=[b_stm])
            dve(lambda e: e.tensor_scalar_add(t[:, 18:19], t[:, 18:19], 1e-9),
                reads=[b_stm], writes=[b_stm])
            dve(lambda e: e.reciprocal(t[:, 19:20], t[:, 18:19]),
                reads=[b_stm], writes=[b_stm])
            dve(lambda e: e.tensor_scalar(t[:, 56:64], t[:, 48:56], t[:, 19:20], None, ALU.mult),
                reads=[b_stm], writes=[b_stm])
            for e_ in range(NE):
                pb = 1 + (e_ % 2)
                pe(lambda e, e_=e_, pb=pb: e.transpose(
                    P[pb][0:1, 0:CH], stm[:, 56 + e_:57 + e_], iden_f[:, :]),
                   reads=[b_stm], writes=[b_P[pb]])
                dve(lambda e, e_=e_, pb=pb: e.tensor_copy(
                    combr[0:1, e_ * CH:(e_ + 1) * CH], P[pb][0:1, 0:CH]),
                    reads=[b_P[pb]], writes=[b_combr])
            for e_ in range(NE):
                pb = 1 + (e_ % 2)
                mm(P[pb][:, 0:CH], ones1b[:, :], combr[0:1, e_ * CH:(e_ + 1) * CH],
                   start=True, stop=True, reads=[b_combr], writes=[b_P[pb]])
                act(lambda e, e_=e_, pb=pb, c=c: e.activation(
                    combs[:, e_ * NT + c * CH: e_ * NT + (c + 1) * CH],
                    P[pb][:, 0:CH], AF.Copy),
                    reads=[b_P[pb]], writes=[b_combs[0], b_combs[1]])

    def emit_ffn(dest_is_out, tile):
        for half in range(2):
            for i in range(HSH // 2 // 128):
                hc = half * 32 + i
                blk = hc // 2
                if hc % 2 == 0:
                    s1, tok1 = stream_unit(wg, b_wg, ("sw1b", blk), sw1b[blk], "sq")
                    s2, tok2 = stream_unit(wg2, b_wg2, ("sw2b", blk), sw2b[blk], "aq")
                    emit_ffn.cur = (s1, s2)
                s1, s2 = emit_ffn.cur
                pg = P[2 * (i % 2)]
                pu = P[2 * (i % 2) + 1]
                for k in range(KC):
                    mm(pg[:, 0:NT], arena_tile(wg, s1, k, (hc % 2) * 128, 128),
                       ys[:, k * NT:(k + 1) * NT], start=(k == 0), stop=(k == KC - 1),
                       reads=[b_wg[s1], b_ys], writes=[b_P[2 * (i % 2)]])
                for k in range(KC):
                    mm(pu[:, 0:NT], arena_tile(wg2, s2, k, (hc % 2) * 128, 128),
                       ys[:, k * NT:(k + 1) * NT], start=(k == 0), stop=(k == KC - 1),
                       reads=[b_wg2[s2], b_ys], writes=[b_P[2 * (i % 2) + 1]])
                hreg = b_hsA if i < 16 else b_hsB
                hview = hs[:, i * NT: (i + 1) * NT] if i < 16 else \
                    hs[:, HSB + (i - 16) * NT: HSB + (i - 15) * NT]
                act(lambda e, pg=pg, i=i: e.activation(
                    attntm_silu[:, (i % 2) * 1024:(i % 2) * 1024 + NT],
                    pg[:, 0:NT], AF.Silu),
                    reads=[b_P[2 * (i % 2)]], writes=[b_att_sl[i % 2]])
                dve(lambda e, pu=pu, hv=hview, i=i: e.tensor_mul(
                    hv, attntm_silu[:, (i % 2) * 1024:(i % 2) * 1024 + NT], pu[:, 0:NT]),
                    reads=[b_att_sl[i % 2], b_P[2 * (i % 2) + 1]], writes=[hreg])
            for mp in range(KC // 2):
                sus = []
                for uu in range(2):
                    su, _ = stream_unit(wo3, b_wo3, ("sw3b", mp, half * 2 + uu),
                                        sw3b[mp, half * 2 + uu], "sq" if uu == 0 else "aq")
                    sus.append(su)
                for m in (2 * mp, 2 * mp + 1):
                    po = P[4 + (m % 2)]
                    for kk in range(32):
                        hv = hs[:, kk * NT:(kk + 1) * NT] if kk < 16 else \
                            hs[:, HSB + (kk - 16) * NT:HSB + (kk - 15) * NT]
                        hreg = b_hsA if kk < 16 else b_hsB
                        su = sus[kk // 16]
                        mm(po[:, 0:NT], arena_tile(wo3, su, kk % 16, (m % 2) * 128, 128),
                           hv, start=(kk == 0), stop=(kk == 31),
                           reads=[b_wo3[su], hreg], writes=[b_P[4 + (m % 2)]])
                    if half == 0:
                        dve(lambda e, m=m, po=po: e.tensor_copy(
                            accs[:, m * NT:(m + 1) * NT], po[:, 0:NT]),
                            reads=[b_P[4 + (m % 2)]], writes=[b_accs])
                    else:
                        dve(lambda e, m=m, po=po: e.tensor_add(
                            accs[:, m * NT:(m + 1) * NT], accs[:, m * NT:(m + 1) * NT],
                            po[:, 0:NT]),
                            reads=[b_P[4 + (m % 2)], b_accs], writes=[b_accs])
        for half in range(2):
            for ei in range(4):
                e_ = half * 4 + ei
                for jp in range(HR // 256):
                    s1, _ = stream_unit(wg, b_wg, ("w12b1", e_, jp), w12b1[e_, jp], "sq")
                    s2, _ = stream_unit(wg2, b_wg2, ("w12b2", e_, jp), w12b2[e_, jp], "aq")
                    for jj in range(2):
                        j2 = jp * 2 + jj
                        ph1 = P[2 * (j2 % 2)]
                        ph2 = P[2 * (j2 % 2) + 1]
                        for k in range(KC):
                            mm(ph1[:, 0:NT], arena_tile(wg, s1, k, jj * 128, 128),
                               ys[:, k * NT:(k + 1) * NT], start=(k == 0), stop=(k == KC - 1),
                               reads=[b_wg[s1], b_ys], writes=[b_P[2 * (j2 % 2)]])
                        for k in range(KC):
                            mm(ph2[:, 0:NT], arena_tile(wg2, s2, k, jj * 128, 128),
                               ys[:, k * NT:(k + 1) * NT], start=(k == 0), stop=(k == KC - 1),
                               reads=[b_wg2[s2], b_ys], writes=[b_P[2 * (j2 % 2) + 1]])
                        kk = ei * 8 + j2
                        hv = hs[:, kk * NT:(kk + 1) * NT] if kk < 16 else \
                            hs[:, HSB + (kk - 16) * NT:HSB + (kk - 15) * NT]
                        hreg = b_hsA if kk < 16 else b_hsB
                        act(lambda e, ph1=ph1, j2=j2: e.activation(
                            attntm_silu[:, (j2 % 2) * 1024:(j2 % 2) * 1024 + NT],
                            ph1[:, 0:NT], AF.Silu),
                            reads=[b_P[2 * (j2 % 2)]], writes=[b_att_sl[j2 % 2]])
                        dve(lambda e, ph2=ph2, j2=j2: e.tensor_mul(
                            silu2[:, (j2 % 2) * 1024:(j2 % 2) * 1024 + NT],
                            attntm_silu[:, (j2 % 2) * 1024:(j2 % 2) * 1024 + NT], ph2[:, 0:NT]),
                            reads=[b_att_sl[j2 % 2], b_P[2 * (j2 % 2) + 1]], writes=[b_kftm_sl[j2 % 2]])
                        dve(lambda e, hv=hv, j2=j2, e_=e_: e.tensor_mul(
                            hv, silu2[:, (j2 % 2) * 1024:(j2 % 2) * 1024 + NT],
                            combs[:, e_ * NT:(e_ + 1) * NT]),
                            reads=[b_kftm_sl[j2 % 2], b_combs[0], b_combs[1]], writes=[hreg])
            for mp in range(KC // 2):
                sus = []
                for uu in range(2):
                    su, _ = stream_unit(wo3, b_wo3, ("w3b", mp, half * 2 + uu),
                                        w3b[mp, half * 2 + uu], "sq" if uu == 0 else "aq")
                    sus.append(su)
                for m in (2 * mp, 2 * mp + 1):
                    po = P[4 + (m % 2)]
                    for kk in range(32):
                        hv = hs[:, kk * NT:(kk + 1) * NT] if kk < 16 else \
                            hs[:, HSB + (kk - 16) * NT:HSB + (kk - 15) * NT]
                        hreg = b_hsA if kk < 16 else b_hsB
                        su = sus[kk // 16]
                        mm(po[:, 0:NT], arena_tile(wo3, su, kk % 16, (m % 2) * 128, 128),
                           hv, start=(kk == 0), stop=(kk == 31),
                           reads=[b_wo3[su], hreg], writes=[b_P[4 + (m % 2)]])
                    dve(lambda e, m=m, po=po: e.tensor_add(
                        accs[:, m * NT:(m + 1) * NT], accs[:, m * NT:(m + 1) * NT],
                        po[:, 0:NT]),
                        reads=[b_P[4 + (m % 2)], b_accs], writes=[b_accs])


    def emit_eluplus1(psum, b_psum, bias_col, dst_view, dst_buf):
        a = accs
        act(lambda e: e.activation(a[:, 0:NT], psum[:, 0:NT], AF.Relu, bias=bias_col),
            reads=[b_psum], writes=[b_accs], extra=tok_consts)
        act(lambda e: e.activation(a[:, NT:2 * NT], psum[:, 0:NT], AF.Identity, bias=bias_col),
            reads=[b_psum], writes=[b_accs])
        dve(lambda e: e.tensor_sub(a[:, NT:2 * NT], a[:, NT:2 * NT], a[:, 0:NT]),
            reads=[b_accs], writes=[b_accs])
        act(lambda e: e.activation(a[:, 2 * NT:3 * NT], a[:, NT:2 * NT], AF.Exp),
            reads=[b_accs], writes=[b_accs])
        dve(lambda e: e.tensor_add(dst_view, a[:, 0:NT], a[:, 2 * NT:3 * NT]),
            reads=[b_accs], writes=[dst_buf])

    for t in range(T):
        dma("sq", xs[:, :], x_in[t], writes=[b_xs])
        emit_ln(b_xs, xs)
        emit_router()
        emit_ffn(False, t)
        for m in range(KC):
            dve(lambda e, m=m: e.scalar_tensor_tensor(
                xs[:, m * NT:(m + 1) * NT], accs[:, m * NT:(m + 1) * NT], 0.5,
                xs[:, m * NT:(m + 1) * NT], ALU.mult, ALU.add),
                reads=[b_accs, b_xs], writes=[b_xs])
        dma("gp", x1_d[t], xs[:, :], reads=[b_xs], writes=[b_x1d[t]])
        emit_ln(b_xs, xs)

        for which, wsrc, nmb, bias_sb, dstoff, dstbuf in (
                (0, wqb, "wqb", bqs, 0, b_hsA), (1, wkb, "wkb", bks, HSB, b_hsB)):
            for jp in range(KC // 2):
                s1, _ = stream_unit(wg, b_wg, (nmb, jp), wsrc[jp], "sq")
                for jj in range(2):
                    j = jp * 2 + jj
                    pq = P[2 + (j % 2)]
                    for k in range(KC):
                        mm(pq[:, 0:NT], arena_tile(wg, s1, k, jj * 128, 128),
                           ys[:, k * NT:(k + 1) * NT], start=(k == 0), stop=(k == KC - 1),
                           reads=[b_wg[s1], b_ys], writes=[b_P[2 + (j % 2)]])
                    emit_eluplus1(pq, b_P[2 + (j % 2)], bias_sb[:, j:j + 1],
                                  hs[:, dstoff + j * NT:dstoff + (j + 1) * NT], dstbuf)
        dma("gp", qf_d[t], hs[:, 0:HSB], reads=[b_hsA], writes=[b_qfd[t]])
        dma("gp", kf_d[t], hs[:, HSB:2 * HSB], reads=[b_hsB], writes=[b_kfd[t]])

        for cp in range(CPT // 2):
            for vh in range(2):
                for jp in range(4):
                    gj = vh * 4 + jp
                    s1, _ = stream_unit(wg2, b_wg2, ("wvb", gj), wvb[gj], "aq")
                    for jj in range(2):
                        j = gj * 2 + jj
                        lj = jp * 2 + jj
                        pv = P[2 + (j % 2)]
                        for k in range(KC):
                            mm(pv[:, 0:256], arena_tile(wg2, s1, k, jj * 128, 128),
                               ys[:, k * NT + cp * 256: k * NT + cp * 256 + 256],
                               start=(k == 0), stop=(k == KC - 1),
                               reads=[b_wg2[s1], b_ys], writes=[b_P[2 + (j % 2)]])
                        act(lambda e, pv=pv, j=j, lj=lj: e.activation(
                            vfm[:, lj * 256:(lj + 1) * 256], pv[:, 0:256], AF.Identity,
                            bias=bvs[:, j:j + 1]),
                            reads=[b_P[2 + (j % 2)]], writes=[b_vfm], extra=tok_consts)
                for ci in range(2):
                    c = cp * 2 + ci
                    vslot = c % 2
                    for lj in range(8):
                        pe(lambda e, lj=lj, ci=ci: e.transpose(
                            PT[:, ci * 1024 + lj * 128: ci * 1024 + (lj + 1) * 128],
                            vfm[:, lj * 256 + ci * 128: lj * 256 + ci * 128 + 128],
                            iden_b[:, :]),
                           reads=[b_vfm], writes=[b_PT[ci]])
                    dve(lambda e, ci=ci, vslot=vslot, vh=vh: e.tensor_copy(
                        combs[:, vslot * 2048 + vh * 1024: vslot * 2048 + (vh + 1) * 1024],
                        PT[:, ci * 1024: (ci + 1) * 1024]),
                        reads=[b_PT[ci]], writes=[b_combs[vslot]])
            for ci in range(2):
                c = cp * 2 + ci
                g = t * CPT + c
                vslot = c % 2
                dma("gp", v_d[g][:, :], combs[:, vslot * 2048: vslot * 2048 + DM],
                    reads=[b_combs[vslot]], writes=[b_vd[g]])
                for h in range(NHEAD):
                    pt = h // 8
                    pe(lambda e, h=h, c=c, pt=pt: e.transpose(
                        PT[:, pt * 1024 + (h % 8) * 128: pt * 1024 + (h % 8) * 128 + 128],
                        hs[:, HSB + h * NT + c * CH: HSB + h * NT + c * CH + 128],
                        iden_b[:, :]),
                       reads=[b_hsB], writes=[b_PT[pt]])
                    if h % 8 == 7:
                        dve(lambda e, pt=pt, h=h: e.tensor_copy(
                            kftm[:, (h - 7) * 128:(h + 1) * 128],
                            PT[:, pt * 1024: pt * 1024 + 1024]),
                            reads=[b_PT[pt]], writes=[b_kftm_sl[pt]])
                for h in range(NHEAD):
                    pb = 4 + (h // 4) % 2
                    mm(P[pb][:, (h % 4) * 128:(h % 4) * 128 + 128],
                       kftm[:, h * 128:(h + 1) * 128],
                       combs[:, vslot * 2048 + h * 128: vslot * 2048 + (h + 1) * 128],
                       start=True, stop=True,
                       reads=[b_kftm_sl[h // 8], b_combs[vslot]], writes=[b_P[pb]])
                    if h % 4 == 3:
                        if t == 0 and c == 0:
                            dve(lambda e, pb=pb, h=h: e.tensor_copy(
                                kvs[:, (h - 3) * 128:(h + 1) * 128], P[pb][:, 0:512]),
                                reads=[b_P[pb]], writes=[b_kvs])
                        else:
                            dve(lambda e, pb=pb, h=h: e.tensor_add(
                                kvs[:, (h - 3) * 128:(h + 1) * 128],
                                kvs[:, (h - 3) * 128:(h + 1) * 128], P[pb][:, 0:512]),
                                reads=[b_P[pb], b_kvs], writes=[b_kvs])
                kf3 = hs[:, HSB:2 * HSB].rearrange("p (j n) -> p j n", j=NHEAD)[:, :, c * CH:(c + 1) * CH]
                if t == 0 and c == 0:
                    dve(lambda e, kf3=kf3: e.tensor_reduce(ksums[:, :], kf3, AX.X, ALU.add),
                        reads=[b_hsB], writes=[b_ksums])
                else:
                    dve(lambda e, kf3=kf3: e.tensor_reduce(stm[:, 0:NHEAD], kf3, AX.X, ALU.add),
                        reads=[b_hsB], writes=[b_stm])
                    dve(lambda e: e.tensor_add(ksums[:, :], ksums[:, :], stm[:, 0:NHEAD]),
                        reads=[b_stm, b_ksums], writes=[b_ksums])

    dma("gp", cc_in[:, 0:NHEAD * DH], kvs[:, :], reads=[b_kvs], writes=[b_ccin])
    dma("gp", cc_in[:, NHEAD * DH:], ksums[:, :], reads=[b_ksums], writes=[b_ccin])
    opp("gp", lambda e: e.collective_compute(
        "AllGather", ALU.bypass, replica_groups=[list(range(cfg.NC))],
        ins=[cc_in[:, :]], outs=[cc_out[:, :]]),
        reads=[b_ccin], writes=[b_ccout], inc="coll")
    CCW = NHEAD * DH + NHEAD
    b_gslot = [Buf(), Buf()]
    for r in range(cfg.NC):
        sl = r % 2
        gview = accs[:, sl * CCW: sl * CCW + CCW]
        dma("sq", gview, cc_out[r * 128:(r + 1) * 128, :],
            reads=[b_ccout], writes=[b_gslot[sl]])
        if r == 0:
            dve(lambda e, gv=gview, r=r: e.tensor_scalar(
                kvs[:, :], gv[:, 0:NHEAD * DH], prevm[:, r:r + 1], None, ALU.mult),
                reads=[b_gslot[sl]], writes=[b_kvs], extra=tok_consts)
            dve(lambda e, gv=gview, r=r: e.tensor_scalar(
                ksums[:, :], gv[:, NHEAD * DH:CCW], prevm[:, r:r + 1], None, ALU.mult),
                reads=[b_gslot[sl]], writes=[b_ksums])
        else:
            dve(lambda e, gv=gview, r=r: e.scalar_tensor_tensor(
                kvs[:, :], gv[:, 0:NHEAD * DH], prevm[:, r:r + 1], kvs[:, :],
                ALU.mult, ALU.add),
                reads=[b_gslot[sl], b_kvs], writes=[b_kvs])
            dve(lambda e, gv=gview, r=r: e.scalar_tensor_tensor(
                ksums[:, :], gv[:, NHEAD * DH:CCW], prevm[:, r:r + 1], ksums[:, :],
                ALU.mult, ALU.add),
                reads=[b_gslot[sl], b_ksums], writes=[b_ksums])
    dve(lambda e: e.tensor_copy(kvb[:, :], kvs[:, :]), reads=[b_kvs], writes=[b_kvb])
    dve(lambda e: e.tensor_copy(ksumb[:, :], ksums[:, :]), reads=[b_ksums], writes=[b_ksumb])

    for t in range(T):
        dma("gp", xs[:, :], x1_d[t], reads=[b_x1d[t]], writes=[b_xs])
        dma("gp", hs[:, 0:HSB], qf_d[t], reads=[b_qfd[t]], writes=[b_hsA])
        dma("gp", hs[:, HSB:2 * HSB], kf_d[t], reads=[b_kfd[t]], writes=[b_hsB])
        for c in range(CPT):
            g = t * CPT + c
            vslot = c % 2
            dma("sq", combs[:, vslot * 4 * NT: vslot * 4 * NT + DM], v_d[g][:, :],
                reads=[b_vd[g]], writes=[b_combs[vslot]])
            qf_h = lambda h: hs[:, h * NT + c * CH: h * NT + c * CH + 128]
            kf_h = lambda h: hs[:, HSB + h * NT + c * CH: HSB + h * NT + c * CH + 128]
            v_h = lambda h: combs[:, vslot * 4 * NT + h * 128: vslot * 4 * NT + (h + 1) * 128]
            for h in range(NHEAD):
                pt = h // 8
                pe(lambda e, h=h, pt=pt, src=kf_h(h): e.transpose(
                    PT[:, pt * 1024 + (h % 8) * 128: pt * 1024 + (h % 8) * 128 + 128],
                    src, iden_b[:, :]), reads=[b_hsB], writes=[b_PT[pt]])
                if h % 8 == 7:
                    dve(lambda e, pt=pt, h=h: e.tensor_copy(
                        kftm[:, (h - 7) * 128:(h + 1) * 128], PT[:, pt * 1024: pt * 1024 + 1024]),
                        reads=[b_PT[pt]], writes=[b_kftm_sl[pt]])
            for h in range(NHEAD):
                pa = P[h % 2]
                mm(pa[:, 0:128], kf_h(h), qf_h(h), start=True, stop=True,
                   reads=[b_hsA, b_hsB], writes=[b_P[h % 2]])
                dve(lambda e, pa=pa, h=h: e.tensor_mul(
                    atb[:, h * 128:(h + 1) * 128], pa[:, 0:128], triu[:, :]),
                    reads=[b_P[h % 2]], writes=[b_atb], extra=tok_consts)
                mm(P[3][:, h:h + 1], qf_h(h), ksumb[:, h:h + 1], start=True, stop=False,
                   reads=[b_hsA, b_ksumb], writes=[b_P[3]])
                mm(P[3][:, h:h + 1], atb[:, h * 128:(h + 1) * 128], colones[:, :],
                   start=False, stop=True, reads=[b_atb], writes=[b_P[3]])
            dve(lambda e: e.tensor_scalar_add(zs[:, :], P[3][:, 0:NHEAD], ATT_EPS),
                reads=[b_P[3]], writes=[b_z])
            dve(lambda e: e.reciprocal(zinv[:, :], zs[:, :]), reads=[b_z], writes=[b_zinv])
            for h in range(NHEAD):
                pb = 4 + (h // 4) % 2
                psl = P[pb][:, (h % 4) * 128:(h % 4) * 128 + 128]
                mm(psl, qf_h(h), kvb[:, h * 128:(h + 1) * 128], start=True, stop=False,
                   reads=[b_hsA, b_kvb], writes=[b_P[pb]])
                mm(psl, atb[:, h * 128:(h + 1) * 128], v_h(h), start=False, stop=True,
                   reads=[b_atb, b_combs[vslot]], writes=[b_P[pb]])
                act(lambda e, psl=psl, h=h: e.activation(
                    attntm[:, h * 128:(h + 1) * 128], psl, AF.Copy, scale=zinv[:, h:h + 1]),
                    reads=[b_P[pb], b_zinv], writes=[b_att_sl[h // 8]])
            for h in range(NHEAD):
                pt = h // 8
                pe(lambda e, h=h, pt=pt: e.transpose(
                    PT[:, pt * 1024 + (h % 8) * 128: pt * 1024 + (h % 8) * 128 + 128],
                    attntm[:, h * 128:(h + 1) * 128], iden_b[:, :]),
                   reads=[b_att_sl[h // 8]], writes=[b_PT[pt]])
                if h % 8 == 7:
                    for hh in range(h - 7, h + 1):
                        dve(lambda e, pt=pt, hh=hh, c=c: e.tensor_copy(
                            ys[:, hh * NT + c * CH: hh * NT + c * CH + 128],
                            PT[:, pt * 1024 + (hh % 8) * 128: pt * 1024 + (hh % 8) * 128 + 128]),
                            reads=[b_PT[pt]], writes=[b_ys])
            for h in range(NHEAD):
                pb = 4 + (h // 4) % 2
                mm(P[pb][:, (h % 4) * 128:(h % 4) * 128 + 128],
                   kftm[:, h * 128:(h + 1) * 128], v_h(h), start=True, stop=True,
                   reads=[b_kftm_sl[h // 8], b_combs[vslot]], writes=[b_P[pb]])
                if h % 4 == 3:
                    dve(lambda e, pb=pb, h=h: e.tensor_add(
                        kvs[:, (h - 3) * 128:(h + 1) * 128],
                        kvs[:, (h - 3) * 128:(h + 1) * 128], P[pb][:, 0:512]),
                        reads=[b_P[pb], b_kvs], writes=[b_kvs])
            dve(lambda e: e.tensor_copy(kvb[:, :], kvs[:, :]), reads=[b_kvs], writes=[b_kvb])
            kf3 = hs[:, HSB:2 * HSB].rearrange("p (j n) -> p j n", j=NHEAD)[:, :, c * CH:(c + 1) * CH]
            dve(lambda e, kf3=kf3: e.tensor_reduce(stm[:, 0:NHEAD], kf3, AX.X, ALU.add),
                reads=[b_hsB], writes=[b_stm])
            dve(lambda e: e.tensor_add(ksums[:, :], ksums[:, :], stm[:, 0:NHEAD]),
                reads=[b_stm, b_ksums], writes=[b_ksums])
            dve(lambda e: e.tensor_copy(ksumb[:, :], ksums[:, :]), reads=[b_ksums], writes=[b_ksumb])
        for mp in range(KC // 2):
            su, _ = stream_unit(wo3, b_wo3, ("wob", mp), wob[mp],
                                "sq" if mp % 2 == 0 else "aq")
            for m in (2 * mp, 2 * mp + 1):
                po = P[4 + (m % 2)]
                for j in range(KC):
                    mm(po[:, 0:NT], arena_tile(wo3, su, j, (m % 2) * 128, 128),
                       ys[:, j * NT:(j + 1) * NT], start=(j == 0), stop=(j == KC - 1),
                       reads=[b_wo3[su], b_ys], writes=[b_P[4 + (m % 2)]])
                dve(lambda e, m=m, po=po: e.scalar_tensor_tensor(
                    xs[:, m * NT:(m + 1) * NT], po[:, 0:NT], bos[:, m:m + 1],
                    xs[:, m * NT:(m + 1) * NT], ALU.add, ALU.add),
                    reads=[b_P[4 + (m % 2)], b_xs], writes=[b_xs])
        emit_ln(b_xs, xs)
        emit_router()
        emit_ffn(True, t)
        for m in range(KC):
            dve(lambda e, m=m: e.scalar_tensor_tensor(
                accs[:, m * NT:(m + 1) * NT], accs[:, m * NT:(m + 1) * NT], 0.5,
                xs[:, m * NT:(m + 1) * NT], ALU.mult, ALU.add),
                reads=[b_accs, b_xs], writes=[b_accs])
        dma("gp", y_out[t], accs[:, :], reads=[b_accs], writes=[Buf()])

    with es:
        blk = es.enter_context(nc.Block())

        def make_runner(names, drain=()):
            names = set(names)
            def run(e):
                last = defaultdict(int)
                for (oe, fn, waits, inc_sem, inc_n) in S.ops:
                    if oe not in names:
                        continue
                    for s_, v_ in waits.items():
                        if oe == "pe" and s_ == "pe":
                            continue
                        if v_ > last[s_]:
                            e.wait_ge(sems[s_], v_)
                            last[s_] = v_
                    ins = fn(e)
                    ins.then_inc(sems[inc_sem], inc_n)
                for d in drain:
                    if S.cnt[d]:
                        e.wait_ge(sems[d], S.cnt[d])
            return run

        blk.tensor(make_runner(("pe",)))
        blk.scalar(make_runner(("act", "aq")))
        blk.vector(make_runner(("dve",)))
        blk.sync(make_runner(("sq",)))
        blk.gpsimd(make_runner(("gp",), drain=[f"gp{i}" for i in range(NDSEM)]))
    return nc


def _prep_inputs(cfg: CFG, I):
    f32 = lambda a: np.ascontiguousarray(np.asarray(a), dtype=np.float32)
    d = {}
    sw1 = np.asarray(I["L0_ffn_sw1"], dtype=np.float32)
    sw2 = np.asarray(I["L0_ffn_sw2"], dtype=np.float32)
    sw3 = np.asarray(I["L0_ffn_sw3"], dtype=np.float32)
    w12 = np.asarray(I["L0_ffn_w12"], dtype=np.float32)
    w3 = np.asarray(I["L0_ffn_w3"], dtype=np.float32)
    for nm, w in (("sw1p", sw1), ("sw2p", sw2)):
        d[nm] = f32(w.reshape(32, 256, 16, 128).transpose(0, 3, 2, 1)
                    .reshape(32, 128, 4096))
    d["sw3p"] = f32(sw3.reshape(8, 256, 4, 16, 128).transpose(0, 2, 4, 3, 1)
                    .reshape(8, 4, 128, 4096))
    d["w12p1"] = f32(w12[:, :HR, :].reshape(NE, 4, 256, 16, 128)
                     .transpose(0, 1, 4, 3, 2).reshape(NE, 4, 128, 4096))
    d["w12p2"] = f32(w12[:, HR:, :].reshape(NE, 4, 256, 16, 128)
                     .transpose(0, 1, 4, 3, 2).reshape(NE, 4, 128, 4096))
    d["w3p"] = f32(w3.transpose(0, 2, 1).reshape(8, 8, 128, 8, 256)
                   .transpose(3, 0, 1, 2, 4).reshape(8, 4, 16, 128, 256)
                   .transpose(0, 1, 3, 2, 4).reshape(8, 4, 128, 4096))
    for nm, key in (("wqp", "L0_attn_Wq"), ("wkp", "L0_attn_Wk"),
                    ("wvp", "L0_attn_Wv"), ("wop", "L0_attn_Wo")):
        w = np.asarray(I[key], dtype=np.float32)
        d[nm] = f32(w.reshape(16, 128, 8, 256).transpose(2, 1, 0, 3)
                    .reshape(8, 128, 4096))
    rt = f32(np.asarray(I["L0_ffn_router_w"]).T)
    d["routp"] = f32(rt.reshape(KC, 128, NE).transpose(1, 0, 2))
    d["ebbc"] = f32(np.broadcast_to(np.asarray(I["L0_ffn_expert_bias"]), (128, NE)))
    for nm, key in (("bq", "L0_attn_bq"), ("bk", "L0_attn_bk"),
                    ("bv", "L0_attn_bv"), ("bo", "L0_attn_bo")):
        d[nm] = f32(np.asarray(I[key]).reshape(KC, 128).T)
    import ml_dtypes
    d["iden_b"] = np.eye(128, dtype=ml_dtypes.bfloat16)
    d["iden_f"] = np.eye(128, dtype=np.float32)
    d["ones1"] = np.ones((1, 128), dtype=np.float32)
    d["ones1b"] = np.ones((1, 128), dtype=ml_dtypes.bfloat16)
    d["colones"] = np.ones((128, 1), dtype=ml_dtypes.bfloat16)
    d["triu"] = np.triu(np.ones((128, 128), dtype=np.float32))
    return d


def make_in_maps(cfg: CFG, inputs):
    shared = _prep_inputs(cfg, inputs)
    x = np.asarray(inputs["L0_attn_x"], dtype=np.float32)
    in_maps = []
    for c in range(cfg.NC):
        b = c // cfg.HPS
        h = c % cfg.HPS
        sl = x[b, h * cfg.TCORE:(h + 1) * cfg.TCORE, :]
        xt = np.ascontiguousarray(
            sl.reshape(cfg.NTILES, NT, KC, 128).transpose(0, 3, 2, 1)
            .reshape(cfg.NTILES, 128, KC * NT))
        pm = np.zeros((128, cfg.NC), dtype=np.float32)
        for r in range(cfg.NC):
            if r // cfg.HPS == c // cfg.HPS and r < c:
                pm[:, r] = 1.0
        m = dict(shared)
        m["x"] = xt
        m["prevm"] = pm
        in_maps.append(m)
    return in_maps


def assemble_out(cfg: CFG, x_shape, per_core_y):
    out = np.empty(x_shape, dtype=np.float32)
    for c in range(cfg.NC):
        b = c // cfg.HPS
        h = c % cfg.HPS
        yt = np.asarray(per_core_y[c])
        yt = (yt.reshape(cfg.NTILES, 128, KC, NT).transpose(0, 3, 2, 1)
              .reshape(cfg.TCORE, DM))
        out[b, h * cfg.TCORE:(h + 1) * cfg.TCORE, :] = yt
    return out


def run(cfg: CFG, inputs, **kw):
    in_maps = make_in_maps(cfg, inputs)
    nc = build_nc(cfg)
    res = run_bass_kernel_spmd(nc, in_maps, core_ids=list(range(cfg.NC)), **kw)
    x = np.asarray(inputs["L0_attn_x"])
    out = assemble_out(cfg, x.shape, [res.results[c]["y"] for c in range(cfg.NC)])
    return out, res


def kernel(**inputs):
    cfg = CFG(8, 4, 8192)
    out, _ = run(cfg, inputs)
    return out


# version 09
# speedup vs baseline: 1.0372x; 1.0372x over previous
import numpy as np
from collections import defaultdict
from contextlib import ExitStack

import concourse.bass as bass
import concourse.mybir as mybir
from concourse.bass_utils import run_bass_kernel_spmd

F32 = mybir.dt.float32
BF16 = mybir.dt.bfloat16
AF = mybir.ActivationFunctionType
ALU = mybir.AluOpType
AX = mybir.AxisListType

DM = 2048
KC = DM // 128
NHEAD = 16
DH = 128
HSH = 8192
HR = 1024
NE = 8
NT = 512
CH = 128
CPT = NT // CH
LN_EPS = 1e-5
ATT_EPS = 1e-6


class CFG:
    def __init__(self, n_cores, batch, seqlen):
        self.NC = n_cores
        self.B = batch
        self.S = seqlen
        self.TCORE = batch * seqlen // n_cores
        assert self.TCORE % NT == 0
        self.NTILES = self.TCORE // NT
        self.NCHUNK = self.TCORE // CH
        self.HPS = n_cores // batch
        assert seqlen % (self.HPS * CH) == 0


class Buf:
    __slots__ = ("w", "r")
    def __init__(self):
        self.w = {}
        self.r = {}


def _merge(dst, tok):
    for s, v in tok.items():
        if dst.get(s, -1) < v:
            dst[s] = v


_TRACE_SRC = False


class Sched:
    def __init__(self):
        self.ops = []
        self.cnt = defaultdict(int)
        self.src = []

    def op(self, eng, fn, reads=(), writes=(), extra=None, inc=None, inc_n=1):
        if _TRACE_SRC:
            import sys as _s
            f = _s._getframe(1)
            skip = ("op", "opp", "mm", "dma", "act", "dve", "pe", "cvt", "stream_unit")
            helper = None
            while f.f_code.co_name in skip and f.f_back is not None:
                helper = f.f_code.co_name
                f = f.f_back
            self.src.append((helper, f.f_code.co_name, f.f_lineno))
        waits = {}
        if extra:
            _merge(waits, extra)
        for b in reads:
            _merge(waits, b.w)
        for b in writes:
            _merge(waits, b.w)
            _merge(waits, b.r)
        sem = inc if inc is not None else eng
        self.cnt[sem] += inc_n
        tok = {sem: self.cnt[sem]}
        self.ops.append((eng, fn, waits, sem, inc_n))
        for b in reads:
            _merge(b.r, tok)
        for b in writes:
            b.w = dict(tok)
            b.r = {}
        return tok


def build_nc(cfg: CFG):
    nc = bass.Bass("TRN2", target_bir_lowering=False, debug=False)
    T = cfg.NTILES

    def ein(name, shape, dt=F32):
        return nc.dram_tensor(name, shape, dt, kind="ExternalInput").ap()

    U = KC * 256
    x_in = ein("x", [T, 128, KC * NT])
    sw1p = ein("sw1p", [HSH // 256, 128, U])
    sw2p = ein("sw2p", [HSH // 256, 128, U])
    sw3p = ein("sw3p", [KC // 2, 4, 128, U])
    w12p1 = ein("w12p1", [NE, HR // 256, 128, U])
    w12p2 = ein("w12p2", [NE, HR // 256, 128, U])
    w3p = ein("w3p", [KC // 2, 4, 128, U])
    wqp = ein("wqp", [KC // 2, 128, U])
    wkp = ein("wkp", [KC // 2, 128, U])
    wvp = ein("wvp", [KC // 2, 128, U])
    wop = ein("wop", [KC // 2, 128, U])
    routp = ein("routp", [128, KC, NE])
    ebbc_in = ein("ebbc", [128, NE])
    bq_in = ein("bq", [128, KC])
    bk_in = ein("bk", [128, KC])
    bv_in = ein("bv", [128, KC])
    bo_in = ein("bo", [128, KC])
    iden_b_in = ein("iden_b", [128, 128], BF16)
    iden_f_in = ein("iden_f", [128, 128])
    ones1_in = ein("ones1", [1, 128])
    ones1b_in = ein("ones1b", [1, 128], BF16)
    colones_in = ein("colones", [128, 1], BF16)
    triu_in = ein("triu", [128, 128])
    triu16_in = ein("triu16", [128, NHEAD * CH], BF16)
    prevm_in = ein("prevm", [128, cfg.NC])

    y_out = nc.dram_tensor("y", [T, 128, KC * NT], F32, kind="ExternalOutput").ap()

    def idram(name, shape, dt):
        return nc.dram_tensor(name, shape, dt).ap()

    sw1b = idram("sw1b", [HSH // 256, 128, U], BF16)
    sw2b = idram("sw2b", [HSH // 256, 128, U], BF16)
    sw3b = idram("sw3b", [KC // 2, 4, 128, U], BF16)
    w12b1 = idram("w12b1", [NE, HR // 256, 128, U], BF16)
    w12b2 = idram("w12b2", [NE, HR // 256, 128, U], BF16)
    w3b = idram("w3b", [KC // 2, 4, 128, U], BF16)
    wqb = idram("wqb", [KC // 2, 128, U], BF16)
    wkb = idram("wkb", [KC // 2, 128, U], BF16)
    wvb = idram("wvb", [KC // 2, 128, U], BF16)
    wob = idram("wob", [KC // 2, 128, U], BF16)

    x1_d = idram("x1_d", [T, 128, KC * NT], F32)
    qf_d = idram("qf_d", [T, 128, KC * NT], BF16)
    kf_d = idram("kf_d", [T, 128, KC * NT], BF16)
    v_d = idram("v_d", [cfg.NCHUNK, CH, DM], BF16)
    cc_in = idram("cc_in", [128, NHEAD * DH + NHEAD], F32)
    cc_out = idram("cc_out", [cfg.NC * 128, NHEAD * DH + NHEAD], F32)

    es = ExitStack()
    sb = lambda n, c, dt: es.enter_context(nc.sbuf_tensor(n, [128, c], dt))

    xs = sb("xs", KC * NT, F32)
    ys = sb("ys", KC * NT, BF16)
    hs = sb("hs", 2 * KC * NT, BF16)
    accs = sb("accs", 3 * NT, F32)
    combs = sb("combs", NE * NT, BF16)
    vfm = sb("vfm", NHEAD * CH, BF16)
    atb = vfm
    wg = sb("wg", 3 * KC * 256, BF16)
    wg2 = sb("wg2", 3 * KC * 256, BF16)
    wo3 = sb("wo3", 2 * KC * 256, BF16)
    kftm = sb("kftm", NHEAD * CH, BF16)
    attntm = sb("attntm", NHEAD * CH, BF16)
    kvs = sb("kvs", NHEAD * DH, F32)
    kvb = sb("kvb", NHEAD * DH, BF16)
    ksums = sb("ksums", NHEAD, F32)
    ksumb = sb("ksumb", NHEAD, BF16)
    routs = sb("routs", 2 * KC * NE, BF16)
    bqs = sb("bqs", KC, F32)
    bks = sb("bks", KC, F32)
    bvs = sb("bvs", KC, F32)
    bos = sb("bos", KC, F32)
    ebbc = sb("ebbcS", NE, F32)
    iden_b = sb("iden_bS", 128, BF16)
    iden_f = sb("iden_fS", 128, F32)
    colones = sb("colonesS", 1, BF16)
    triu = sb("triuS", 128, F32)
    triu16 = sb("triu16S", NHEAD * CH, BF16)
    prevm = sb("prevmS", cfg.NC, F32)
    stm = sb("stm", 64, F32)
    zs = sb("zs", NHEAD, F32)
    zinv = sb("zinv", NHEAD, F32)
    ones1 = es.enter_context(nc.sbuf_tensor("ones1S", [1, 128], F32))
    ones1b = es.enter_context(nc.sbuf_tensor("ones1bS", [1, 128], BF16))
    rowb = es.enter_context(nc.sbuf_tensor("rowb", [1, 3 * NT], F32))
    scor = es.enter_context(nc.sbuf_tensor("scor", [NE, NT], F32))
    combr = es.enter_context(nc.sbuf_tensor("combr", [1, NE * CH], BF16))
    attntm_silu = attntm
    silu2 = kftm

    P = [es.enter_context(nc.psum_tensor(f"P{i}", [128, 512], F32)) for i in range(6)]
    PT = es.enter_context(nc.psum_tensor("PT", [128, 2048], BF16))

    NDSEM = 6
    sem_names = (["pe", "act", "dve", "coll"]
                 + [f"sq{i}" for i in range(NDSEM)]
                 + [f"gp{i}" for i in range(NDSEM)]
                 + [f"aq{i}" for i in range(NDSEM)])
    sems = {n: es.enter_context(nc.semaphore(f"s_{n}")) for n in sem_names}

    S = Sched()
    dma_ring = {"sq": 0, "gp": 0, "aq": 0}

    b_xs, b_ys, b_hsA, b_hsB, b_accs, b_vfm = (Buf() for _ in range(6))
    b_atb = b_vfm
    b_combs = [Buf(), Buf()]
    b_wg = [Buf(), Buf(), Buf()]
    b_wg2 = [Buf(), Buf(), Buf()]
    b_wo3 = [Buf(), Buf()]
    b_att_sl = [Buf(), Buf()]
    b_kftm_sl = [Buf(), Buf()]
    b_kvs, b_kvb, b_ksums, b_ksumb = Buf(), Buf(), Buf(), Buf()
    b_stm, b_row, b_scor, b_combr, b_z, b_zinv = (Buf() for _ in range(6))
    b_routs, b_rtmp = Buf(), Buf()
    b_P = [Buf() for _ in range(6)]
    b_PT = [Buf(), Buf()]
    b_x1d = [Buf() for _ in range(T)]
    b_qfd = [Buf() for _ in range(T)]
    b_kfd = [Buf() for _ in range(T)]
    b_vd = [Buf() for _ in range(cfg.NCHUNK)]
    b_ccin, b_ccout = Buf(), Buf()
    b_wcvt = defaultdict(Buf)

    HSB = KC * NT

    def opp(eng, fn, reads=(), writes=(), extra=None, inc=None, inc_n=1):
        return S.op(eng, fn, reads=reads, writes=writes, extra=extra, inc=inc, inc_n=inc_n)

    def dma(queue, out_ap, in_ap, reads=(), writes=(), extra=None):
        i = dma_ring[queue]
        dma_ring[queue] = (i + 1) % NDSEM
        sem = f"{queue}{i}"
        slot_free = {sem: S.cnt[sem]}
        if extra:
            slot_free = dict(slot_free)
            _merge(slot_free, extra)
        return opp(queue, lambda e: e.dma_start(out=out_ap, in_=in_ap),
                   reads=reads, writes=writes, extra=slot_free, inc=sem, inc_n=16)

    def mm(out_ap, lhsT_ap, rhs_ap, start, stop, reads=(), writes=(), extra=None, sgc=False):
        return opp("pe", lambda e, o=out_ap, l=lhsT_ap, r=rhs_ap, st=start, sp=stop, g=sgc:
                   e.matmul(o, l, r, start=st, stop=sp, skip_group_check=g),
                   reads=reads, writes=writes, extra=extra)

    def act(fn, reads=(), writes=(), extra=None):
        return opp("act", fn, reads=reads, writes=writes, extra=extra)

    def dve(fn, reads=(), writes=(), extra=None):
        return opp("dve", fn, reads=reads, writes=writes, extra=extra)

    def pe(fn, reads=(), writes=(), extra=None):
        return opp("pe", fn, reads=reads, writes=writes, extra=extra)

    tok_consts = {}
    for dst, srcap in ((bqs, bq_in), (bks, bk_in), (bvs, bv_in), (bos, bo_in),
                       (ebbc, ebbc_in), (iden_b, iden_b_in), (iden_f, iden_f_in),
                       (ones1, ones1_in), (ones1b, ones1b_in), (colones, colones_in),
                       (triu, triu_in), (triu16, triu16_in), (prevm, prevm_in)):
        _merge(tok_consts, dma("sq", dst[:, :], srcap[:, :], writes=[Buf()]))

    RW = KC * NE
    dma("sq", accs[:, 0:RW], routp.rearrange("p k e -> p (k e)"), writes=[b_accs])
    dve(lambda e: e.tensor_copy(routs[:, 0:RW], accs[:, 0:RW]),
        reads=[b_accs], writes=[b_routs])
    dve(lambda e: e.tensor_copy(accs[:, RW:2 * RW], routs[:, 0:RW]),
        reads=[b_routs, b_accs], writes=[b_accs])
    dve(lambda e: e.tensor_sub(accs[:, RW:2 * RW], accs[:, 0:RW], accs[:, RW:2 * RW]),
        reads=[b_accs], writes=[b_accs])
    dve(lambda e: e.tensor_copy(routs[:, RW:2 * RW], accs[:, RW:2 * RW]),
        reads=[b_accs], writes=[b_routs])

    def cvt(key, out_ap, in_ap):
        return dma("gp", out_ap, in_ap, writes=[b_wcvt[key]])

    for blk in range(16):
        cvt(("sw1b", blk), sw1b[blk], sw1p[blk])
        cvt(("sw2b", blk), sw2b[blk], sw2p[blk])
    for mp in range(KC // 2):
        for u_ in (0, 1):
            cvt(("sw3b", mp, u_), sw3b[mp, u_], sw3p[mp, u_])
    for blk in range(16, 32):
        cvt(("sw1b", blk), sw1b[blk], sw1p[blk])
        cvt(("sw2b", blk), sw2b[blk], sw2p[blk])
    for mp in range(KC // 2):
        for u_ in (2, 3):
            cvt(("sw3b", mp, u_), sw3b[mp, u_], sw3p[mp, u_])
    for e_ in range(4):
        for jp in range(HR // 256):
            cvt(("w12b1", e_, jp), w12b1[e_, jp], w12p1[e_, jp])
            cvt(("w12b2", e_, jp), w12b2[e_, jp], w12p2[e_, jp])
    for mp in range(KC // 2):
        for u_ in (0, 1):
            cvt(("w3b", mp, u_), w3b[mp, u_], w3p[mp, u_])
    for e_ in range(4, NE):
        for jp in range(HR // 256):
            cvt(("w12b1", e_, jp), w12b1[e_, jp], w12p1[e_, jp])
            cvt(("w12b2", e_, jp), w12b2[e_, jp], w12p2[e_, jp])
    for mp in range(KC // 2):
        for u_ in (2, 3):
            cvt(("w3b", mp, u_), w3b[mp, u_], w3p[mp, u_])
    for nm, dst, src in (("wqb", wqb, wqp), ("wkb", wkb, wkp), ("wvb", wvb, wvp)):
        for jp in range(KC // 2):
            cvt((nm, jp), dst[jp], src[jp])
    for mp in range(KC // 2):
        cvt(("wob", mp), wob[mp], wop[mp])

    def arena_tile(arena, slot, kc, c0, w):
        off = slot * KC * 256 + kc * 256 + c0
        return arena[:, off:off + w]

    slot_state = {}

    def stream_unit(arena, b_arena, key, dram_slice, queue):
        k_ = id(arena)
        s_ = slot_state.get(k_, 0)
        slot_state[k_] = s_ + 1
        slot = s_ % len(b_arena)
        tok = dma(queue, arena[:, slot * U: (slot + 1) * U], dram_slice,
                  reads=[b_wcvt[key]], writes=[b_arena[slot]])
        return slot, tok

    def emit_ln(src_buf, src_sb):
        for k in range(KC):
            dve(lambda e, k=k: e.tensor_copy(hs[:, k * NT:(k + 1) * NT],
                                             src_sb[:, k * NT:(k + 1) * NT]),
                reads=[src_buf], writes=[b_hsA])
            act(lambda e, k=k: e.activation(hs[:, HSB + k * NT:HSB + (k + 1) * NT],
                                            hs[:, k * NT:(k + 1) * NT], AF.Square),
                reads=[b_hsA], writes=[b_hsB])
        for k in range(KC):
            mm(P[3][0:1, 0:NT], colones[:, :], hs[:, k * NT:(k + 1) * NT],
               start=(k == 0), stop=(k == KC - 1), reads=[b_hsA], writes=[b_P[3]],
               extra=tok_consts)
        for k in range(KC):
            mm(P[4][0:1, 0:NT], colones[:, :], hs[:, HSB + k * NT:HSB + (k + 1) * NT],
               start=(k == 0), stop=(k == KC - 1), reads=[b_hsB], writes=[b_P[4]])
        r = rowb
        dve(lambda e: e.tensor_scalar_mul(r[:, 0:NT], P[3][0:1, 0:NT], 1.0 / DM),
            reads=[b_P[3]], writes=[b_row])
        dve(lambda e: e.tensor_scalar_mul(r[:, NT:2 * NT], P[4][0:1, 0:NT], 1.0 / DM),
            reads=[b_P[4]], writes=[b_row])
        dve(lambda e: e.tensor_mul(r[:, 2 * NT:3 * NT], r[:, 0:NT], r[:, 0:NT]),
            reads=[b_row], writes=[b_row])
        dve(lambda e: e.tensor_sub(r[:, NT:2 * NT], r[:, NT:2 * NT], r[:, 2 * NT:3 * NT]),
            reads=[b_row], writes=[b_row])
        dve(lambda e: e.tensor_scalar_add(r[:, NT:2 * NT], r[:, NT:2 * NT], LN_EPS),
            reads=[b_row], writes=[b_row])
        act(lambda e: e.activation(r[:, NT:2 * NT], r[:, NT:2 * NT], AF.Ln),
            reads=[b_row], writes=[b_row])
        dve(lambda e: e.tensor_scalar_mul(r[:, NT:2 * NT], r[:, NT:2 * NT], -0.5),
            reads=[b_row], writes=[b_row])
        act(lambda e: e.activation(r[:, NT:2 * NT], r[:, NT:2 * NT], AF.Exp),
            reads=[b_row], writes=[b_row])
        dve(lambda e: e.tensor_mul(r[:, 2 * NT:3 * NT], r[:, 0:NT], r[:, NT:2 * NT]),
            reads=[b_row], writes=[b_row])
        mm(P[4][:, 0:NT], ones1[:, :], r[:, NT:2 * NT], start=True, stop=True,
           reads=[b_row], writes=[b_P[4]], extra=tok_consts)
        mm(P[5][:, 0:NT], ones1[:, :], r[:, 2 * NT:3 * NT], start=True, stop=True,
           reads=[b_row], writes=[b_P[5]])
        for k in range(KC):
            dve(lambda e, k=k: e.tensor_mul(accs[:, 0:NT], src_sb[:, k * NT:(k + 1) * NT],
                                            P[4][:, 0:NT]),
                reads=[src_buf, b_P[4]], writes=[b_accs])
            dve(lambda e, k=k: e.tensor_sub(ys[:, k * NT:(k + 1) * NT], accs[:, 0:NT],
                                            P[5][:, 0:NT]),
                reads=[b_accs, b_P[5]], writes=[b_ys])

    def emit_router():
        for k in range(KC):
            mm(P[0][0:NE, 0:NT], routs[:, k * NE:(k + 1) * NE], ys[:, k * NT:(k + 1) * NT],
               start=(k == 0), stop=False, reads=[b_routs, b_ys], writes=[b_P[0]])
        for k in range(KC):
            mm(P[0][0:NE, 0:NT], routs[:, KC * NE + k * NE:KC * NE + (k + 1) * NE],
               ys[:, k * NT:(k + 1) * NT],
               start=False, stop=(k == KC - 1), reads=[b_routs, b_ys], writes=[b_P[0]])
        dve(lambda e: e.tensor_scalar_mul(scor[:, :], P[0][0:NE, 0:NT], 0.5),
            reads=[b_P[0]], writes=[b_scor])
        act(lambda e: e.activation(scor[:, :], scor[:, :], AF.Tanh),
            reads=[b_scor], writes=[b_scor])
        dve(lambda e: e.tensor_scalar(scor[:, :], scor[:, :], 0.5, 0.5, ALU.mult, ALU.add),
            reads=[b_scor], writes=[b_scor])
        for c in range(CPT):
            pe(lambda e, c=c: e.transpose(P[1][:, 0:NE], scor[:, c * CH:(c + 1) * CH],
                                          iden_f[0:NE, 0:NE]),
               reads=[b_scor], writes=[b_P[1]], extra=tok_consts)
            t = stm
            dve(lambda e: e.tensor_copy(t[:, 0:8], P[1][:, 0:NE]),
                reads=[b_P[1]], writes=[b_stm])
            dve(lambda e: e.tensor_add(t[:, 8:16], t[:, 0:8], ebbc[:, :]),
                reads=[b_stm], writes=[b_stm], extra=tok_consts)
            dve(lambda e: e.tensor_reduce(t[:, 16:17], t[:, 8:16], AX.X, ALU.max),
                reads=[b_stm], writes=[b_stm])
            dve(lambda e: e.tensor_scalar(t[:, 24:32], t[:, 8:16], t[:, 16:17], None, ALU.is_ge),
                reads=[b_stm], writes=[b_stm])
            dve(lambda e: e.scalar_tensor_tensor(t[:, 32:40], t[:, 24:32], -1e9, t[:, 8:16],
                                                 ALU.mult, ALU.add),
                reads=[b_stm], writes=[b_stm])
            dve(lambda e: e.tensor_reduce(t[:, 17:18], t[:, 32:40], AX.X, ALU.max),
                reads=[b_stm], writes=[b_stm])
            dve(lambda e: e.tensor_scalar(t[:, 40:48], t[:, 32:40], t[:, 17:18], None, ALU.is_ge),
                reads=[b_stm], writes=[b_stm])
            dve(lambda e: e.tensor_add(t[:, 40:48], t[:, 40:48], t[:, 24:32]),
                reads=[b_stm], writes=[b_stm])
            dve(lambda e: e.tensor_mul(t[:, 48:56], t[:, 0:8], t[:, 40:48]),
                reads=[b_stm], writes=[b_stm])
            dve(lambda e: e.tensor_reduce(t[:, 18:19], t[:, 48:56], AX.X, ALU.add),
                reads=[b_stm], writes=[b_stm])
            dve(lambda e: e.tensor_scalar_add(t[:, 18:19], t[:, 18:19], 1e-9),
                reads=[b_stm], writes=[b_stm])
            dve(lambda e: e.reciprocal(t[:, 19:20], t[:, 18:19]),
                reads=[b_stm], writes=[b_stm])
            dve(lambda e: e.tensor_scalar(t[:, 56:64], t[:, 48:56], t[:, 19:20], None, ALU.mult),
                reads=[b_stm], writes=[b_stm])
            for e_ in range(NE):
                pb = 1 + (e_ % 2)
                pe(lambda e, e_=e_, pb=pb: e.transpose(
                    P[pb][0:1, 0:CH], stm[:, 56 + e_:57 + e_], iden_f[:, :]),
                   reads=[b_stm], writes=[b_P[pb]])
                dve(lambda e, e_=e_, pb=pb: e.tensor_copy(
                    combr[0:1, e_ * CH:(e_ + 1) * CH], P[pb][0:1, 0:CH]),
                    reads=[b_P[pb]], writes=[b_combr])
            for e_ in range(NE):
                pb = 1 + (e_ % 2)
                mm(P[pb][:, 0:CH], ones1b[:, :], combr[0:1, e_ * CH:(e_ + 1) * CH],
                   start=True, stop=True, reads=[b_combr], writes=[b_P[pb]])
                act(lambda e, e_=e_, pb=pb, c=c: e.activation(
                    combs[:, e_ * NT + c * CH: e_ * NT + (c + 1) * CH],
                    P[pb][:, 0:CH], AF.Copy),
                    reads=[b_P[pb]], writes=[b_combs[0], b_combs[1]])

    def emit_ffn(dest_is_out, tile):
        for half in range(2):
            for i in range(HSH // 2 // 128):
                hc = half * 32 + i
                blk = hc // 2
                if hc % 2 == 0:
                    s1, tok1 = stream_unit(wg, b_wg, ("sw1b", blk), sw1b[blk], "sq")
                    s2, tok2 = stream_unit(wg2, b_wg2, ("sw2b", blk), sw2b[blk], "aq")
                    emit_ffn.cur = (s1, s2)
                s1, s2 = emit_ffn.cur
                pg = P[2 * (i % 2)]
                pu = P[2 * (i % 2) + 1]
                for k in range(KC):
                    mm(pg[:, 0:NT], arena_tile(wg, s1, k, (hc % 2) * 128, 128),
                       ys[:, k * NT:(k + 1) * NT], start=(k == 0), stop=(k == KC - 1),
                       reads=[b_wg[s1], b_ys], writes=[b_P[2 * (i % 2)]])
                for k in range(KC):
                    mm(pu[:, 0:NT], arena_tile(wg2, s2, k, (hc % 2) * 128, 128),
                       ys[:, k * NT:(k + 1) * NT], start=(k == 0), stop=(k == KC - 1),
                       reads=[b_wg2[s2], b_ys], writes=[b_P[2 * (i % 2) + 1]])
                hreg = b_hsA if i < 16 else b_hsB
                hview = hs[:, i * NT: (i + 1) * NT] if i < 16 else \
                    hs[:, HSB + (i - 16) * NT: HSB + (i - 15) * NT]
                act(lambda e, pg=pg, i=i: e.activation(
                    attntm_silu[:, (i % 2) * 1024:(i % 2) * 1024 + NT],
                    pg[:, 0:NT], AF.Silu),
                    reads=[b_P[2 * (i % 2)]], writes=[b_att_sl[i % 2]])
                dve(lambda e, pu=pu, hv=hview, i=i: e.tensor_mul(
                    hv, attntm_silu[:, (i % 2) * 1024:(i % 2) * 1024 + NT], pu[:, 0:NT]),
                    reads=[b_att_sl[i % 2], b_P[2 * (i % 2) + 1]], writes=[hreg])
            for mp in range(KC // 2):
                sus = []
                for uu in range(2):
                    su, _ = stream_unit(wo3, b_wo3, ("sw3b", mp, half * 2 + uu),
                                        sw3b[mp, half * 2 + uu], "sq" if uu == 0 else "aq")
                    sus.append(su)
                for m in (2 * mp, 2 * mp + 1):
                    po = P[4 + (m % 2)]
                    for kk in range(32):
                        hv = hs[:, kk * NT:(kk + 1) * NT] if kk < 16 else \
                            hs[:, HSB + (kk - 16) * NT:HSB + (kk - 15) * NT]
                        hreg = b_hsA if kk < 16 else b_hsB
                        su = sus[kk // 16]
                        mm(po[:, 0:NT], arena_tile(wo3, su, kk % 16, (m % 2) * 128, 128),
                           hv, start=(kk == 0), stop=(kk == 31),
                           reads=[b_wo3[su], hreg], writes=[b_P[4 + (m % 2)]])
                    dve(lambda e, m=m, po=po: e.scalar_tensor_tensor(
                        xs[:, m * NT:(m + 1) * NT], po[:, 0:NT], 0.5,
                        xs[:, m * NT:(m + 1) * NT], ALU.mult, ALU.add),
                        reads=[b_P[4 + (m % 2)], b_xs], writes=[b_xs])
        for half in range(2):
            for ei in range(4):
                e_ = half * 4 + ei
                for jp in range(HR // 256):
                    s1, _ = stream_unit(wg, b_wg, ("w12b1", e_, jp), w12b1[e_, jp], "sq")
                    s2, _ = stream_unit(wg2, b_wg2, ("w12b2", e_, jp), w12b2[e_, jp], "aq")
                    for jj in range(2):
                        j2 = jp * 2 + jj
                        ph1 = P[2 * (j2 % 2)]
                        ph2 = P[2 * (j2 % 2) + 1]
                        for k in range(KC):
                            mm(ph1[:, 0:NT], arena_tile(wg, s1, k, jj * 128, 128),
                               ys[:, k * NT:(k + 1) * NT], start=(k == 0), stop=(k == KC - 1),
                               reads=[b_wg[s1], b_ys], writes=[b_P[2 * (j2 % 2)]])
                        for k in range(KC):
                            mm(ph2[:, 0:NT], arena_tile(wg2, s2, k, jj * 128, 128),
                               ys[:, k * NT:(k + 1) * NT], start=(k == 0), stop=(k == KC - 1),
                               reads=[b_wg2[s2], b_ys], writes=[b_P[2 * (j2 % 2) + 1]])
                        kk = ei * 8 + j2
                        hv = hs[:, kk * NT:(kk + 1) * NT] if kk < 16 else \
                            hs[:, HSB + (kk - 16) * NT:HSB + (kk - 15) * NT]
                        hreg = b_hsA if kk < 16 else b_hsB
                        act(lambda e, ph1=ph1, j2=j2: e.activation(
                            attntm_silu[:, (j2 % 2) * 1024:(j2 % 2) * 1024 + NT],
                            ph1[:, 0:NT], AF.Silu),
                            reads=[b_P[2 * (j2 % 2)]], writes=[b_att_sl[j2 % 2]])
                        dve(lambda e, ph2=ph2, j2=j2: e.tensor_mul(
                            silu2[:, (j2 % 2) * 1024:(j2 % 2) * 1024 + NT],
                            attntm_silu[:, (j2 % 2) * 1024:(j2 % 2) * 1024 + NT], ph2[:, 0:NT]),
                            reads=[b_att_sl[j2 % 2], b_P[2 * (j2 % 2) + 1]], writes=[b_kftm_sl[j2 % 2]])
                        dve(lambda e, hv=hv, j2=j2, e_=e_: e.tensor_mul(
                            hv, silu2[:, (j2 % 2) * 1024:(j2 % 2) * 1024 + NT],
                            combs[:, e_ * NT:(e_ + 1) * NT]),
                            reads=[b_kftm_sl[j2 % 2], b_combs[0], b_combs[1]], writes=[hreg])
            for mp in range(KC // 2):
                sus = []
                for uu in range(2):
                    su, _ = stream_unit(wo3, b_wo3, ("w3b", mp, half * 2 + uu),
                                        w3b[mp, half * 2 + uu], "sq" if uu == 0 else "aq")
                    sus.append(su)
                for m in (2 * mp, 2 * mp + 1):
                    po = P[4 + (m % 2)]
                    for kk in range(32):
                        hv = hs[:, kk * NT:(kk + 1) * NT] if kk < 16 else \
                            hs[:, HSB + (kk - 16) * NT:HSB + (kk - 15) * NT]
                        hreg = b_hsA if kk < 16 else b_hsB
                        su = sus[kk // 16]
                        mm(po[:, 0:NT], arena_tile(wo3, su, kk % 16, (m % 2) * 128, 128),
                           hv, start=(kk == 0), stop=(kk == 31),
                           reads=[b_wo3[su], hreg], writes=[b_P[4 + (m % 2)]])
                    dve(lambda e, m=m, po=po: e.scalar_tensor_tensor(
                        xs[:, m * NT:(m + 1) * NT], po[:, 0:NT], 0.5,
                        xs[:, m * NT:(m + 1) * NT], ALU.mult, ALU.add),
                        reads=[b_P[4 + (m % 2)], b_xs], writes=[b_xs])


    def emit_eluplus1(psum, b_psum, bias_col, dst_view, dst_buf):
        a = accs
        act(lambda e: e.activation(a[:, 0:NT], psum[:, 0:NT], AF.Relu, bias=bias_col),
            reads=[b_psum], writes=[b_accs], extra=tok_consts)
        act(lambda e: e.activation(a[:, NT:2 * NT], psum[:, 0:NT], AF.Identity, bias=bias_col),
            reads=[b_psum], writes=[b_accs])
        dve(lambda e: e.tensor_sub(a[:, NT:2 * NT], a[:, NT:2 * NT], a[:, 0:NT]),
            reads=[b_accs], writes=[b_accs])
        act(lambda e: e.activation(a[:, 2 * NT:3 * NT], a[:, NT:2 * NT], AF.Exp),
            reads=[b_accs], writes=[b_accs])
        dve(lambda e: e.tensor_add(dst_view, a[:, 0:NT], a[:, 2 * NT:3 * NT]),
            reads=[b_accs], writes=[dst_buf])

    for t in range(T):
        dma("sq", xs[:, :], x_in[t], writes=[b_xs])
        emit_ln(b_xs, xs)
        emit_router()
        emit_ffn(False, t)
        dma("gp", x1_d[t], xs[:, :], reads=[b_xs], writes=[b_x1d[t]])
        emit_ln(b_xs, xs)

        for which, wsrc, nmb, bias_sb, dstoff, dstbuf in (
                (0, wqb, "wqb", bqs, 0, b_hsA), (1, wkb, "wkb", bks, HSB, b_hsB)):
            for jp in range(KC // 2):
                s1, _ = stream_unit(wg, b_wg, (nmb, jp), wsrc[jp], "sq")
                for jj in range(2):
                    j = jp * 2 + jj
                    pq = P[2 + (j % 2)]
                    for k in range(KC):
                        mm(pq[:, 0:NT], arena_tile(wg, s1, k, jj * 128, 128),
                           ys[:, k * NT:(k + 1) * NT], start=(k == 0), stop=(k == KC - 1),
                           reads=[b_wg[s1], b_ys], writes=[b_P[2 + (j % 2)]])
                    emit_eluplus1(pq, b_P[2 + (j % 2)], bias_sb[:, j:j + 1],
                                  hs[:, dstoff + j * NT:dstoff + (j + 1) * NT], dstbuf)
        dma("gp", qf_d[t], hs[:, 0:HSB], reads=[b_hsA], writes=[b_qfd[t]])
        dma("gp", kf_d[t], hs[:, HSB:2 * HSB], reads=[b_hsB], writes=[b_kfd[t]])

        for cp in range(CPT // 2):
            for vh in range(2):
                for jp in range(4):
                    gj = vh * 4 + jp
                    s1, _ = stream_unit(wg2, b_wg2, ("wvb", gj), wvb[gj], "sq")
                    for jj in range(2):
                        j = gj * 2 + jj
                        lj = jp * 2 + jj
                        pv = P[2 + (j % 2)]
                        for k in range(KC):
                            mm(pv[:, 0:256], arena_tile(wg2, s1, k, jj * 128, 128),
                               ys[:, k * NT + cp * 256: k * NT + cp * 256 + 256],
                               start=(k == 0), stop=(k == KC - 1),
                               reads=[b_wg2[s1], b_ys], writes=[b_P[2 + (j % 2)]])
                        act(lambda e, pv=pv, j=j, lj=lj: e.activation(
                            vfm[:, lj * 256:(lj + 1) * 256], pv[:, 0:256], AF.Identity,
                            bias=bvs[:, j:j + 1]),
                            reads=[b_P[2 + (j % 2)]], writes=[b_vfm], extra=tok_consts)
                for ci in range(2):
                    c = cp * 2 + ci
                    vslot = c % 2
                    for lj in range(8):
                        pe(lambda e, lj=lj, ci=ci: e.transpose(
                            PT[:, ci * 1024 + lj * 128: ci * 1024 + (lj + 1) * 128],
                            vfm[:, lj * 256 + ci * 128: lj * 256 + ci * 128 + 128],
                            iden_b[:, :]),
                           reads=[b_vfm], writes=[b_PT[ci]])
                    dve(lambda e, ci=ci, vslot=vslot, vh=vh: e.tensor_copy(
                        combs[:, vslot * 2048 + vh * 1024: vslot * 2048 + (vh + 1) * 1024],
                        PT[:, ci * 1024: (ci + 1) * 1024]),
                        reads=[b_PT[ci]], writes=[b_combs[vslot]])
            for ci in range(2):
                c = cp * 2 + ci
                g = t * CPT + c
                vslot = c % 2
                dma("gp", v_d[g][:, :], combs[:, vslot * 2048: vslot * 2048 + DM],
                    reads=[b_combs[vslot]], writes=[b_vd[g]])
                for h in range(NHEAD):
                    pt = h // 8
                    pe(lambda e, h=h, c=c, pt=pt: e.transpose(
                        PT[:, pt * 1024 + (h % 8) * 128: pt * 1024 + (h % 8) * 128 + 128],
                        hs[:, HSB + h * NT + c * CH: HSB + h * NT + c * CH + 128],
                        iden_b[:, :]),
                       reads=[b_hsB], writes=[b_PT[pt]])
                    if h % 8 == 7:
                        dve(lambda e, pt=pt, h=h: e.tensor_copy(
                            kftm[:, (h - 7) * 128:(h + 1) * 128],
                            PT[:, pt * 1024: pt * 1024 + 1024]),
                            reads=[b_PT[pt]], writes=[b_kftm_sl[pt]])
                for h in range(NHEAD):
                    pb = (0, 1, 4, 5)[h // 4]
                    mm(P[pb][:, (h % 4) * 128:(h % 4) * 128 + 128],
                       kftm[:, h * 128:(h + 1) * 128],
                       combs[:, vslot * 2048 + h * 128: vslot * 2048 + (h + 1) * 128],
                       start=True, stop=True,
                       reads=[b_kftm_sl[h // 8], b_combs[vslot]], writes=[b_P[pb]])
                for g4 in range(4):
                    pb = (0, 1, 4, 5)[g4]
                    if t == 0 and c == 0:
                        dve(lambda e, pb=pb, g4=g4: e.tensor_copy(
                            kvs[:, g4 * 512:(g4 + 1) * 512], P[pb][:, 0:512]),
                            reads=[b_P[pb]], writes=[b_kvs])
                    else:
                        dve(lambda e, pb=pb, g4=g4: e.tensor_add(
                            kvs[:, g4 * 512:(g4 + 1) * 512],
                            kvs[:, g4 * 512:(g4 + 1) * 512], P[pb][:, 0:512]),
                            reads=[b_P[pb], b_kvs], writes=[b_kvs])
                if c == CPT - 1:
                    kf3 = hs[:, HSB:2 * HSB].rearrange("p (j n) -> p j n", j=NHEAD)
                    if t == 0:
                        dve(lambda e, kf3=kf3: e.tensor_reduce(ksums[:, :], kf3, AX.X, ALU.add),
                            reads=[b_hsB], writes=[b_ksums])
                    else:
                        dve(lambda e, kf3=kf3: e.tensor_reduce(stm[:, 0:NHEAD], kf3, AX.X, ALU.add),
                            reads=[b_hsB], writes=[b_stm])
                        dve(lambda e: e.tensor_add(ksums[:, :], ksums[:, :], stm[:, 0:NHEAD]),
                            reads=[b_stm, b_ksums], writes=[b_ksums])

    dma("gp", hs[:, HSB:2 * HSB], kf_d[0], reads=[b_kfd[0]], writes=[b_hsB])
    dma("gp", hs[:, 0:HSB], qf_d[0], reads=[b_qfd[0]], writes=[b_hsA])
    dma("gp", cc_in[:, 0:NHEAD * DH], kvs[:, :], reads=[b_kvs], writes=[b_ccin])
    dma("gp", cc_in[:, NHEAD * DH:], ksums[:, :], reads=[b_ksums], writes=[b_ccin])
    opp("gp", lambda e: e.collective_compute(
        "AllGather", ALU.bypass, replica_groups=[list(range(cfg.NC))],
        ins=[cc_in[:, :]], outs=[cc_out[:, :]]),
        reads=[b_ccin], writes=[b_ccout], inc="coll")
    CCW = NHEAD * DH + NHEAD
    b_gslot = [Buf(), Buf()]
    for r in range(cfg.NC):
        sl = r % 2
        gview = xs[:, sl * CCW: sl * CCW + CCW]
        dma("sq", gview, cc_out[r * 128:(r + 1) * 128, :],
            reads=[b_ccout], writes=[b_gslot[sl], b_xs])
        if r == 0:
            dve(lambda e, gv=gview, r=r: e.tensor_scalar(
                kvs[:, :], gv[:, 0:NHEAD * DH], prevm[:, r:r + 1], None, ALU.mult),
                reads=[b_gslot[sl], b_xs], writes=[b_kvs], extra=tok_consts)
            dve(lambda e, gv=gview, r=r: e.tensor_scalar(
                ksums[:, :], gv[:, NHEAD * DH:CCW], prevm[:, r:r + 1], None, ALU.mult),
                reads=[b_gslot[sl], b_xs], writes=[b_ksums])
        else:
            dve(lambda e, gv=gview, r=r: e.scalar_tensor_tensor(
                kvs[:, :], gv[:, 0:NHEAD * DH], prevm[:, r:r + 1], kvs[:, :],
                ALU.mult, ALU.add),
                reads=[b_gslot[sl], b_kvs, b_xs], writes=[b_kvs])
            dve(lambda e, gv=gview, r=r: e.scalar_tensor_tensor(
                ksums[:, :], gv[:, NHEAD * DH:CCW], prevm[:, r:r + 1], ksums[:, :],
                ALU.mult, ALU.add),
                reads=[b_gslot[sl], b_ksums, b_xs], writes=[b_ksums])
    dve(lambda e: e.tensor_copy(kvb[:, :], kvs[:, :]), reads=[b_kvs], writes=[b_kvb])
    dve(lambda e: e.tensor_copy(ksumb[:, :], ksums[:, :]), reads=[b_ksums], writes=[b_ksumb])

    for t in range(T):
        if t > 0:
            dma("gp", hs[:, HSB:2 * HSB], kf_d[t], reads=[b_kfd[t]], writes=[b_hsB])
            dma("gp", hs[:, 0:HSB], qf_d[t], reads=[b_qfd[t]], writes=[b_hsA])
        dma("gp", xs[:, :], x1_d[t], reads=[b_x1d[t]], writes=[b_xs])
        for c in range(CPT):
            g = t * CPT + c
            vslot = c % 2
            dma("gp", combs[:, vslot * 4 * NT: vslot * 4 * NT + DM], v_d[g][:, :],
                reads=[b_vd[g]], writes=[b_combs[vslot]])
            qf_h = lambda h: hs[:, h * NT + c * CH: h * NT + c * CH + 128]
            kf_h = lambda h: hs[:, HSB + h * NT + c * CH: HSB + h * NT + c * CH + 128]
            v_h = lambda h: combs[:, vslot * 4 * NT + h * 128: vslot * 4 * NT + (h + 1) * 128]
            for h in range(NHEAD):
                pb = (0, 1, 4, 5)[h // 4]
                mm(P[pb][:, (h % 4) * 128:(h % 4) * 128 + 128], kf_h(h), qf_h(h),
                   start=True, stop=True, reads=[b_hsA, b_hsB], writes=[b_P[pb]])
            for g4 in range(4):
                pb = (0, 1, 4, 5)[g4]
                dve(lambda e, pb=pb, g4=g4: e.tensor_mul(
                    atb[:, g4 * 512:(g4 + 1) * 512], P[pb][:, 0:512],
                    triu16[:, 0:512]),
                    reads=[b_P[pb]], writes=[b_atb], extra=tok_consts)
            for h in range(NHEAD):
                pt = h // 8
                pe(lambda e, h=h, pt=pt, src=kf_h(h): e.transpose(
                    PT[:, pt * 1024 + (h % 8) * 128: pt * 1024 + (h % 8) * 128 + 128],
                    src, iden_b[:, :]), reads=[b_hsB], writes=[b_PT[pt]])
                if h % 8 == 7:
                    dve(lambda e, pt=pt, h=h: e.tensor_copy(
                        kftm[:, (h - 7) * 128:(h + 1) * 128], PT[:, pt * 1024: pt * 1024 + 1024]),
                        reads=[b_PT[pt]], writes=[b_kftm_sl[pt]])
            for h in range(NHEAD):
                mm(P[3][:, h:h + 1], qf_h(h), ksumb[:, h:h + 1], start=True, stop=False,
                   reads=[b_hsA, b_ksumb], writes=[b_P[3]])
                mm(P[3][:, h:h + 1], atb[:, h * 128:(h + 1) * 128], colones[:, :],
                   start=False, stop=True, reads=[b_atb], writes=[b_P[3]])
            dve(lambda e: e.tensor_scalar_add(zs[:, :], P[3][:, 0:NHEAD], ATT_EPS),
                reads=[b_P[3]], writes=[b_z])
            dve(lambda e: e.reciprocal(zinv[:, :], zs[:, :]), reads=[b_z], writes=[b_zinv])
            for h in range(NHEAD):
                pb = 4 + (h // 4) % 2
                psl = P[pb][:, (h % 4) * 128:(h % 4) * 128 + 128]
                mm(psl, qf_h(h), kvb[:, h * 128:(h + 1) * 128], start=True, stop=False,
                   reads=[b_hsA, b_kvb], writes=[b_P[pb]])
                mm(psl, atb[:, h * 128:(h + 1) * 128], v_h(h), start=False, stop=True,
                   reads=[b_atb, b_combs[vslot]], writes=[b_P[pb]])
                act(lambda e, psl=psl, h=h: e.activation(
                    attntm[:, h * 128:(h + 1) * 128], psl, AF.Copy, scale=zinv[:, h:h + 1]),
                    reads=[b_P[pb], b_zinv], writes=[b_att_sl[h // 8]])
            for h in range(NHEAD):
                pt = h // 8
                pe(lambda e, h=h, pt=pt: e.transpose(
                    PT[:, pt * 1024 + (h % 8) * 128: pt * 1024 + (h % 8) * 128 + 128],
                    attntm[:, h * 128:(h + 1) * 128], iden_b[:, :]),
                   reads=[b_att_sl[h // 8]], writes=[b_PT[pt]])
                if h % 8 == 7:
                    for hh in range(h - 7, h + 1):
                        dve(lambda e, pt=pt, hh=hh, c=c: e.tensor_copy(
                            ys[:, hh * NT + c * CH: hh * NT + c * CH + 128],
                            PT[:, pt * 1024 + (hh % 8) * 128: pt * 1024 + (hh % 8) * 128 + 128]),
                            reads=[b_PT[pt]], writes=[b_ys])
            for h in range(NHEAD):
                pb = 4 + (h // 4) % 2
                mm(P[pb][:, (h % 4) * 128:(h % 4) * 128 + 128],
                   kftm[:, h * 128:(h + 1) * 128], v_h(h), start=True, stop=True,
                   reads=[b_kftm_sl[h // 8], b_combs[vslot]], writes=[b_P[pb]])
                if h % 4 == 3:
                    dve(lambda e, pb=pb, h=h: e.tensor_add(
                        kvs[:, (h - 3) * 128:(h + 1) * 128],
                        kvs[:, (h - 3) * 128:(h + 1) * 128], P[pb][:, 0:512]),
                        reads=[b_P[pb], b_kvs], writes=[b_kvs])
            dve(lambda e: e.tensor_copy(kvb[:, :], kvs[:, :]), reads=[b_kvs], writes=[b_kvb])
            kf3 = hs[:, HSB:2 * HSB].rearrange("p (j n) -> p j n", j=NHEAD)[:, :, c * CH:(c + 1) * CH]
            dve(lambda e, kf3=kf3: e.tensor_reduce(stm[:, 0:NHEAD], kf3, AX.X, ALU.add),
                reads=[b_hsB], writes=[b_stm])
            dve(lambda e: e.tensor_add(ksums[:, :], ksums[:, :], stm[:, 0:NHEAD]),
                reads=[b_stm, b_ksums], writes=[b_ksums])
            dve(lambda e: e.tensor_copy(ksumb[:, :], ksums[:, :]), reads=[b_ksums], writes=[b_ksumb])
        for mp in range(KC // 2):
            su, _ = stream_unit(wo3, b_wo3, ("wob", mp), wob[mp],
                                "sq" if mp % 2 == 0 else "aq")
            for m in (2 * mp, 2 * mp + 1):
                po = P[4 + (m % 2)]
                for j in range(KC):
                    mm(po[:, 0:NT], arena_tile(wo3, su, j, (m % 2) * 128, 128),
                       ys[:, j * NT:(j + 1) * NT], start=(j == 0), stop=(j == KC - 1),
                       reads=[b_wo3[su], b_ys], writes=[b_P[4 + (m % 2)]])
                dve(lambda e, m=m, po=po: e.scalar_tensor_tensor(
                    xs[:, m * NT:(m + 1) * NT], po[:, 0:NT], bos[:, m:m + 1],
                    xs[:, m * NT:(m + 1) * NT], ALU.add, ALU.add),
                    reads=[b_P[4 + (m % 2)], b_xs], writes=[b_xs])
        emit_ln(b_xs, xs)
        emit_router()
        emit_ffn(True, t)
        dma("aq", y_out[t], xs[:, :], reads=[b_xs], writes=[Buf()])

    with es:
        blk = es.enter_context(nc.Block())

        def make_runner(names, drain=()):
            names = set(names)
            def run(e):
                last = defaultdict(int)
                for (oe, fn, waits, inc_sem, inc_n) in S.ops:
                    if oe not in names:
                        continue
                    for s_, v_ in waits.items():
                        if oe == "pe" and s_ == "pe":
                            continue
                        if v_ > last[s_]:
                            e.wait_ge(sems[s_], v_)
                            last[s_] = v_
                    ins = fn(e)
                    ins.then_inc(sems[inc_sem], inc_n)
                for d in drain:
                    if S.cnt[d]:
                        e.wait_ge(sems[d], S.cnt[d])
            return run

        blk.tensor(make_runner(("pe",)))
        blk.scalar(make_runner(("act", "aq"), drain=[f"aq{i}" for i in range(NDSEM)]))
        blk.vector(make_runner(("dve",)))
        blk.sync(make_runner(("sq",)))
        blk.gpsimd(make_runner(("gp",), drain=[f"gp{i}" for i in range(NDSEM)]))
    build_nc.last_sched = S
    return nc


def _prep_inputs(cfg: CFG, I):
    f32 = lambda a: np.ascontiguousarray(np.asarray(a), dtype=np.float32)
    d = {}
    sw1 = np.asarray(I["L0_ffn_sw1"], dtype=np.float32)
    sw2 = np.asarray(I["L0_ffn_sw2"], dtype=np.float32)
    sw3 = np.asarray(I["L0_ffn_sw3"], dtype=np.float32)
    w12 = np.asarray(I["L0_ffn_w12"], dtype=np.float32)
    w3 = np.asarray(I["L0_ffn_w3"], dtype=np.float32)
    for nm, w in (("sw1p", sw1), ("sw2p", sw2)):
        d[nm] = f32(w.reshape(32, 256, 16, 128).transpose(0, 3, 2, 1)
                    .reshape(32, 128, 4096))
    d["sw3p"] = f32(sw3.reshape(8, 256, 4, 16, 128).transpose(0, 2, 4, 3, 1)
                    .reshape(8, 4, 128, 4096))
    d["w12p1"] = f32(w12[:, :HR, :].reshape(NE, 4, 256, 16, 128)
                     .transpose(0, 1, 4, 3, 2).reshape(NE, 4, 128, 4096))
    d["w12p2"] = f32(w12[:, HR:, :].reshape(NE, 4, 256, 16, 128)
                     .transpose(0, 1, 4, 3, 2).reshape(NE, 4, 128, 4096))
    d["w3p"] = f32(w3.transpose(0, 2, 1).reshape(8, 8, 128, 8, 256)
                   .transpose(3, 0, 1, 2, 4).reshape(8, 4, 16, 128, 256)
                   .transpose(0, 1, 3, 2, 4).reshape(8, 4, 128, 4096))
    for nm, key in (("wqp", "L0_attn_Wq"), ("wkp", "L0_attn_Wk"),
                    ("wvp", "L0_attn_Wv"), ("wop", "L0_attn_Wo")):
        w = np.asarray(I[key], dtype=np.float32)
        d[nm] = f32(w.reshape(16, 128, 8, 256).transpose(2, 1, 0, 3)
                    .reshape(8, 128, 4096))
    rt = f32(np.asarray(I["L0_ffn_router_w"]).T)
    d["routp"] = f32(rt.reshape(KC, 128, NE).transpose(1, 0, 2))
    d["ebbc"] = f32(np.broadcast_to(np.asarray(I["L0_ffn_expert_bias"]), (128, NE)))
    for nm, key in (("bq", "L0_attn_bq"), ("bk", "L0_attn_bk"),
                    ("bv", "L0_attn_bv"), ("bo", "L0_attn_bo")):
        d[nm] = f32(np.asarray(I[key]).reshape(KC, 128).T)
    import ml_dtypes
    d["iden_b"] = np.eye(128, dtype=ml_dtypes.bfloat16)
    d["iden_f"] = np.eye(128, dtype=np.float32)
    d["ones1"] = np.ones((1, 128), dtype=np.float32)
    d["ones1b"] = np.ones((1, 128), dtype=ml_dtypes.bfloat16)
    d["colones"] = np.ones((128, 1), dtype=ml_dtypes.bfloat16)
    d["triu"] = np.triu(np.ones((128, 128), dtype=np.float32))
    d["triu16"] = np.tile(d["triu"], (1, NHEAD)).astype(ml_dtypes.bfloat16)
    return d


def make_in_maps(cfg: CFG, inputs):
    shared = _prep_inputs(cfg, inputs)
    x = np.asarray(inputs["L0_attn_x"], dtype=np.float32)
    in_maps = []
    for c in range(cfg.NC):
        b = c // cfg.HPS
        h = c % cfg.HPS
        sl = x[b, h * cfg.TCORE:(h + 1) * cfg.TCORE, :]
        xt = np.ascontiguousarray(
            sl.reshape(cfg.NTILES, NT, KC, 128).transpose(0, 3, 2, 1)
            .reshape(cfg.NTILES, 128, KC * NT))
        pm = np.zeros((128, cfg.NC), dtype=np.float32)
        for r in range(cfg.NC):
            if r // cfg.HPS == c // cfg.HPS and r < c:
                pm[:, r] = 1.0
        m = dict(shared)
        m["x"] = xt
        m["prevm"] = pm
        in_maps.append(m)
    return in_maps


def assemble_out(cfg: CFG, x_shape, per_core_y):
    out = np.empty(x_shape, dtype=np.float32)
    for c in range(cfg.NC):
        b = c // cfg.HPS
        h = c % cfg.HPS
        yt = np.asarray(per_core_y[c])
        yt = (yt.reshape(cfg.NTILES, 128, KC, NT).transpose(0, 3, 2, 1)
              .reshape(cfg.TCORE, DM))
        out[b, h * cfg.TCORE:(h + 1) * cfg.TCORE, :] = yt
    return out


def run(cfg: CFG, inputs, **kw):
    in_maps = make_in_maps(cfg, inputs)
    nc = build_nc(cfg)
    res = run_bass_kernel_spmd(nc, in_maps, core_ids=list(range(cfg.NC)), **kw)
    x = np.asarray(inputs["L0_attn_x"])
    out = assemble_out(cfg, x.shape, [res.results[c]["y"] for c in range(cfg.NC)])
    return out, res


def kernel(**inputs):
    cfg = CFG(8, 4, 8192)
    out, _ = run(cfg, inputs)
    return out


# version 10
# speedup vs baseline: 1.0657x; 1.0275x over previous
import numpy as np
from collections import defaultdict
from contextlib import ExitStack

import concourse.bass as bass
import concourse.mybir as mybir
from concourse.bass_utils import run_bass_kernel_spmd

F32 = mybir.dt.float32
BF16 = mybir.dt.bfloat16
AF = mybir.ActivationFunctionType
ALU = mybir.AluOpType
AX = mybir.AxisListType

DM = 2048
KC = DM // 128
NHEAD = 16
DH = 128
HSH = 8192
HR = 1024
NE = 8
NT = 512
CH = 128
CPT = NT // CH
LN_EPS = 1e-5
ATT_EPS = 1e-6


class CFG:
    def __init__(self, n_cores, batch, seqlen):
        self.NC = n_cores
        self.B = batch
        self.S = seqlen
        self.TCORE = batch * seqlen // n_cores
        assert self.TCORE % NT == 0
        self.NTILES = self.TCORE // NT
        self.NCHUNK = self.TCORE // CH
        self.HPS = n_cores // batch
        assert seqlen % (self.HPS * CH) == 0


class Buf:
    __slots__ = ("w", "r")
    def __init__(self):
        self.w = {}
        self.r = {}


def _merge(dst, tok):
    for s, v in tok.items():
        if dst.get(s, -1) < v:
            dst[s] = v


_TRACE_SRC = False


class Sched:
    def __init__(self):
        self.ops = []
        self.cnt = defaultdict(int)
        self.src = []

    def op(self, eng, fn, reads=(), writes=(), extra=None, inc=None, inc_n=1):
        if _TRACE_SRC:
            import sys as _s
            f = _s._getframe(1)
            skip = ("op", "opp", "mm", "dma", "act", "dve", "pe", "cvt", "stream_unit")
            helper = None
            while f.f_code.co_name in skip and f.f_back is not None:
                helper = f.f_code.co_name
                f = f.f_back
            self.src.append((helper, f.f_code.co_name, f.f_lineno))
        waits = {}
        if extra:
            _merge(waits, extra)
        for b in reads:
            _merge(waits, b.w)
        for b in writes:
            _merge(waits, b.w)
            _merge(waits, b.r)
        sem = inc if inc is not None else eng
        self.cnt[sem] += inc_n
        tok = {sem: self.cnt[sem]}
        self.ops.append((eng, fn, waits, sem, inc_n))
        for b in reads:
            _merge(b.r, tok)
        for b in writes:
            b.w = dict(tok)
            b.r = {}
        return tok


def build_nc(cfg: CFG):
    nc = bass.Bass("TRN2", target_bir_lowering=False, debug=False)
    T = cfg.NTILES

    def ein(name, shape, dt=F32):
        return nc.dram_tensor(name, shape, dt, kind="ExternalInput").ap()

    U = KC * 256
    x_in = ein("x", [T, 128, KC * NT])
    sw1p = ein("sw1p", [HSH // 256, 128, U])
    sw2p = ein("sw2p", [HSH // 256, 128, U])
    sw3p = ein("sw3p", [KC // 2, 4, 128, U])
    w12p1 = ein("w12p1", [NE, HR // 256, 128, U])
    w12p2 = ein("w12p2", [NE, HR // 256, 128, U])
    w3p = ein("w3p", [KC // 2, 4, 128, U])
    wqp = ein("wqp", [KC // 2, 128, U])
    wkp = ein("wkp", [KC // 2, 128, U])
    wvp = ein("wvp", [KC // 2, 128, U])
    wop = ein("wop", [KC // 2, 128, U])
    routp = ein("routp", [128, KC, NE])
    ebbc_in = ein("ebbc", [128, NE])
    bq_in = ein("bq", [128, KC])
    bk_in = ein("bk", [128, KC])
    bv_in = ein("bv", [128, KC])
    bo_in = ein("bo", [128, KC])
    iden_b_in = ein("iden_b", [128, 128], BF16)
    iden_f_in = ein("iden_f", [128, 128])
    ones1_in = ein("ones1", [1, 128])
    ones1b_in = ein("ones1b", [1, 128], BF16)
    colones_in = ein("colones", [128, 1], BF16)
    triu_in = ein("triu", [128, 128])
    triu16_in = ein("triu16", [128, NHEAD * CH], BF16)
    prevm_in = ein("prevm", [128, 2])

    y_out = nc.dram_tensor("y", [T, 128, KC * NT], F32, kind="ExternalOutput").ap()

    def idram(name, shape, dt):
        return nc.dram_tensor(name, shape, dt).ap()

    sw1b = idram("sw1b", [HSH // 256, 128, U], BF16)
    sw2b = idram("sw2b", [HSH // 256, 128, U], BF16)
    sw3b = idram("sw3b", [KC // 2, 4, 128, U], BF16)
    w12b1 = idram("w12b1", [NE, HR // 256, 128, U], BF16)
    w12b2 = idram("w12b2", [NE, HR // 256, 128, U], BF16)
    w3b = idram("w3b", [KC // 2, 4, 128, U], BF16)
    wqb = idram("wqb", [KC // 2, 128, U], BF16)
    wkb = idram("wkb", [KC // 2, 128, U], BF16)
    wvb = idram("wvb", [KC // 2, 128, U], BF16)
    wob = idram("wob", [KC // 2, 128, U], BF16)

    x1_d = idram("x1_d", [T, 128, KC * NT], F32)
    qf_d = idram("qf_d", [T, 128, KC * NT], BF16)
    kf_d = idram("kf_d", [T, 128, KC * NT], BF16)
    v_d = idram("v_d", [cfg.NCHUNK, CH, DM], BF16)
    cc_in = idram("cc_in", [128, NHEAD * DH + NHEAD], F32)
    cc_out = idram("cc_out", [2 * 128, NHEAD * DH + NHEAD], F32)

    es = ExitStack()
    sb = lambda n, c, dt: es.enter_context(nc.sbuf_tensor(n, [128, c], dt))

    xs = sb("xs", KC * NT, F32)
    ys = sb("ys", KC * NT, BF16)
    hs = sb("hs", 2 * KC * NT, BF16)
    accs = sb("accs", 4 * NT, F32)
    combs = sb("combs", NE * NT, BF16)
    vfm = sb("vfm", NHEAD * CH, BF16)
    atb = vfm
    wg = sb("wg", 3 * KC * 256, BF16)
    wg2 = sb("wg2", 3 * KC * 256, BF16)
    wo3 = sb("wo3", 2 * KC * 256, BF16)
    kftm = sb("kftm", NHEAD * CH, BF16)
    attntm = sb("attntm", NHEAD * CH, BF16)
    kvs = sb("kvs", NHEAD * DH, F32)
    kvb = sb("kvb", NHEAD * DH, BF16)
    ksums = sb("ksums", NHEAD, F32)
    ksumb = sb("ksumb", NHEAD, BF16)
    routs = sb("routs", 2 * KC * NE, BF16)
    bqs = sb("bqs", KC, F32)
    bks = sb("bks", KC, F32)
    bvs = sb("bvs", KC, F32)
    bos = sb("bos", KC, F32)
    ebbc = sb("ebbcS", NE, F32)
    iden_b = sb("iden_bS", 128, BF16)
    iden_f = sb("iden_fS", 128, F32)
    colones = sb("colonesS", 1, BF16)
    triu = sb("triuS", 128, F32)
    triu16 = sb("triu16S", NHEAD * CH, BF16)
    prevm = sb("prevmS", 2, F32)
    stm = sb("stm", 64, F32)
    zs = sb("zs", NHEAD, F32)
    zinv = sb("zinv", NHEAD, F32)
    ones1 = es.enter_context(nc.sbuf_tensor("ones1S", [1, 128], F32))
    ones1b = es.enter_context(nc.sbuf_tensor("ones1bS", [1, 128], BF16))
    rowb = es.enter_context(nc.sbuf_tensor("rowb", [1, 3 * NT], F32))
    scor = es.enter_context(nc.sbuf_tensor("scor", [NE, NT], F32))
    combr = es.enter_context(nc.sbuf_tensor("combr", [1, NE * CH], BF16))
    attntm_silu = attntm
    silu2 = kftm

    P = [es.enter_context(nc.psum_tensor(f"P{i}", [128, 512], F32)) for i in range(6)]
    PT = es.enter_context(nc.psum_tensor("PT", [128, 2048], BF16))

    NDSEM = 6
    sem_names = (["pe", "act", "dve", "coll"]
                 + [f"sq{i}" for i in range(NDSEM)]
                 + [f"gp{i}" for i in range(NDSEM)]
                 + [f"aq{i}" for i in range(NDSEM)])
    sems = {n: es.enter_context(nc.semaphore(f"s_{n}")) for n in sem_names}

    S = Sched()
    dma_ring = {"sq": 0, "gp": 0, "aq": 0}

    b_xs, b_ys, b_hsA, b_hsB, b_accs, b_vfm = (Buf() for _ in range(6))
    b_atb = b_vfm
    b_combs = [Buf(), Buf()]
    b_wg = [Buf(), Buf(), Buf()]
    b_wg2 = [Buf(), Buf(), Buf()]
    b_wo3 = [Buf(), Buf()]
    b_att_sl = [Buf(), Buf()]
    b_kftm_sl = [Buf(), Buf()]
    b_kvs, b_kvb, b_ksums, b_ksumb = Buf(), Buf(), Buf(), Buf()
    b_stm, b_row, b_scor, b_combr, b_z, b_zinv = (Buf() for _ in range(6))
    b_routs, b_rtmp = Buf(), Buf()
    b_P = [Buf() for _ in range(6)]
    b_PT = [Buf(), Buf()]
    b_x1d = [Buf() for _ in range(T)]
    b_qfd = [Buf() for _ in range(T)]
    b_kfd = [Buf() for _ in range(T)]
    b_vd = [Buf() for _ in range(cfg.NCHUNK)]
    b_ccin, b_ccout = Buf(), Buf()
    b_wcvt = defaultdict(Buf)

    HSB = KC * NT

    def opp(eng, fn, reads=(), writes=(), extra=None, inc=None, inc_n=1):
        return S.op(eng, fn, reads=reads, writes=writes, extra=extra, inc=inc, inc_n=inc_n)

    def dma(queue, out_ap, in_ap, reads=(), writes=(), extra=None):
        i = dma_ring[queue]
        dma_ring[queue] = (i + 1) % NDSEM
        sem = f"{queue}{i}"
        slot_free = {sem: S.cnt[sem]}
        if extra:
            slot_free = dict(slot_free)
            _merge(slot_free, extra)
        return opp(queue, lambda e: e.dma_start(out=out_ap, in_=in_ap),
                   reads=reads, writes=writes, extra=slot_free, inc=sem, inc_n=16)

    def mm(out_ap, lhsT_ap, rhs_ap, start, stop, reads=(), writes=(), extra=None, sgc=False):
        return opp("pe", lambda e, o=out_ap, l=lhsT_ap, r=rhs_ap, st=start, sp=stop, g=sgc:
                   e.matmul(o, l, r, start=st, stop=sp, skip_group_check=g),
                   reads=reads, writes=writes, extra=extra)

    def act(fn, reads=(), writes=(), extra=None):
        return opp("act", fn, reads=reads, writes=writes, extra=extra)

    def dve(fn, reads=(), writes=(), extra=None):
        return opp("dve", fn, reads=reads, writes=writes, extra=extra)

    def pe(fn, reads=(), writes=(), extra=None):
        return opp("pe", fn, reads=reads, writes=writes, extra=extra)

    tok_consts = {}
    for dst, srcap in ((bqs, bq_in), (bks, bk_in), (bvs, bv_in), (bos, bo_in),
                       (ebbc, ebbc_in), (iden_b, iden_b_in), (iden_f, iden_f_in),
                       (ones1, ones1_in), (ones1b, ones1b_in), (colones, colones_in),
                       (triu, triu_in), (triu16, triu16_in), (prevm, prevm_in)):
        _merge(tok_consts, dma("sq", dst[:, :], srcap[:, :], writes=[Buf()]))

    RW = KC * NE
    dma("sq", accs[:, 0:RW], routp.rearrange("p k e -> p (k e)"), writes=[b_accs])
    dve(lambda e: e.tensor_copy(routs[:, 0:RW], accs[:, 0:RW]),
        reads=[b_accs], writes=[b_routs])
    dve(lambda e: e.tensor_copy(accs[:, RW:2 * RW], routs[:, 0:RW]),
        reads=[b_routs, b_accs], writes=[b_accs])
    dve(lambda e: e.tensor_sub(accs[:, RW:2 * RW], accs[:, 0:RW], accs[:, RW:2 * RW]),
        reads=[b_accs], writes=[b_accs])
    dve(lambda e: e.tensor_copy(routs[:, RW:2 * RW], accs[:, RW:2 * RW]),
        reads=[b_accs], writes=[b_routs])

    def cvt(key, out_ap, in_ap):
        return dma("gp", out_ap, in_ap, writes=[b_wcvt[key]])

    for blk in range(16):
        cvt(("sw1b", blk), sw1b[blk], sw1p[blk])
        cvt(("sw2b", blk), sw2b[blk], sw2p[blk])
    for mp in range(KC // 2):
        for u_ in (0, 1):
            cvt(("sw3b", mp, u_), sw3b[mp, u_], sw3p[mp, u_])
    for blk in range(16, 32):
        cvt(("sw1b", blk), sw1b[blk], sw1p[blk])
        cvt(("sw2b", blk), sw2b[blk], sw2p[blk])
    for mp in range(KC // 2):
        for u_ in (2, 3):
            cvt(("sw3b", mp, u_), sw3b[mp, u_], sw3p[mp, u_])
    for e_ in range(4):
        for jp in range(HR // 256):
            cvt(("w12b1", e_, jp), w12b1[e_, jp], w12p1[e_, jp])
            cvt(("w12b2", e_, jp), w12b2[e_, jp], w12p2[e_, jp])
    for mp in range(KC // 2):
        for u_ in (0, 1):
            cvt(("w3b", mp, u_), w3b[mp, u_], w3p[mp, u_])
    for e_ in range(4, NE):
        for jp in range(HR // 256):
            cvt(("w12b1", e_, jp), w12b1[e_, jp], w12p1[e_, jp])
            cvt(("w12b2", e_, jp), w12b2[e_, jp], w12p2[e_, jp])
    for mp in range(KC // 2):
        for u_ in (2, 3):
            cvt(("w3b", mp, u_), w3b[mp, u_], w3p[mp, u_])
    for nm, dst, src in (("wqb", wqb, wqp), ("wkb", wkb, wkp), ("wvb", wvb, wvp)):
        for jp in range(KC // 2):
            cvt((nm, jp), dst[jp], src[jp])
    for mp in range(KC // 2):
        cvt(("wob", mp), wob[mp], wop[mp])

    def arena_tile(arena, slot, kc, c0, w):
        off = slot * KC * 256 + kc * 256 + c0
        return arena[:, off:off + w]

    slot_state = {}

    def stream_unit(arena, b_arena, key, dram_slice, queue):
        k_ = id(arena)
        s_ = slot_state.get(k_, 0)
        slot_state[k_] = s_ + 1
        slot = s_ % len(b_arena)
        tok = dma(queue, arena[:, slot * U: (slot + 1) * U], dram_slice,
                  reads=[b_wcvt[key]], writes=[b_arena[slot]])
        return slot, tok

    def emit_ln(src_buf, src_sb):
        dve(lambda e: e.tensor_copy(hs[:, 0:HSB], src_sb[:, :]),
            reads=[src_buf], writes=[b_hsA])
        act(lambda e: e.activation(hs[:, HSB:2 * HSB], hs[:, 0:HSB], AF.Square),
            reads=[b_hsA], writes=[b_hsB])
        for k in range(KC):
            mm(P[3][0:1, 0:NT], colones[:, :], hs[:, k * NT:(k + 1) * NT],
               start=(k == 0), stop=(k == KC - 1), reads=[b_hsA], writes=[b_P[3]],
               extra=tok_consts)
        for k in range(KC):
            mm(P[4][0:1, 0:NT], colones[:, :], hs[:, HSB + k * NT:HSB + (k + 1) * NT],
               start=(k == 0), stop=(k == KC - 1), reads=[b_hsB], writes=[b_P[4]])
        r = rowb
        dve(lambda e: e.tensor_scalar_mul(r[:, 0:NT], P[3][0:1, 0:NT], 1.0 / DM),
            reads=[b_P[3]], writes=[b_row])
        dve(lambda e: e.tensor_scalar_mul(r[:, NT:2 * NT], P[4][0:1, 0:NT], 1.0 / DM),
            reads=[b_P[4]], writes=[b_row])
        dve(lambda e: e.tensor_mul(r[:, 2 * NT:3 * NT], r[:, 0:NT], r[:, 0:NT]),
            reads=[b_row], writes=[b_row])
        dve(lambda e: e.tensor_sub(r[:, NT:2 * NT], r[:, NT:2 * NT], r[:, 2 * NT:3 * NT]),
            reads=[b_row], writes=[b_row])
        dve(lambda e: e.tensor_scalar_add(r[:, NT:2 * NT], r[:, NT:2 * NT], LN_EPS),
            reads=[b_row], writes=[b_row])
        act(lambda e: e.activation(r[:, NT:2 * NT], r[:, NT:2 * NT], AF.Ln),
            reads=[b_row], writes=[b_row])
        dve(lambda e: e.tensor_scalar_mul(r[:, NT:2 * NT], r[:, NT:2 * NT], -0.5),
            reads=[b_row], writes=[b_row])
        act(lambda e: e.activation(r[:, NT:2 * NT], r[:, NT:2 * NT], AF.Exp),
            reads=[b_row], writes=[b_row])
        mm(P[4][:, 0:NT], ones1[:, :], r[:, 0:NT], start=True, stop=True,
           reads=[b_row], writes=[b_P[4]], extra=tok_consts)
        mm(P[5][:, 0:NT], ones1[:, :], r[:, NT:2 * NT], start=True, stop=True,
           reads=[b_row], writes=[b_P[5]])
        for g in range(4):
            sl = slice(g * 4 * NT, (g + 1) * 4 * NT)
            x4 = src_sb[:, sl].rearrange("p (k n) -> p k n", k=4)
            e4 = accs[:, 0:4 * NT].rearrange("p (k n) -> p k n", k=4)
            y4 = ys[:, sl].rearrange("p (k n) -> p k n", k=4)
            m_bc = P[4][:, None, 0:NT].broadcast_to([128, 4, NT])
            a_bc = P[5][:, None, 0:NT].broadcast_to([128, 4, NT])
            dve(lambda e, x4=x4, e4=e4, m_bc=m_bc: e.tensor_tensor(
                e4, x4, m_bc, ALU.subtract),
                reads=[src_buf, b_P[4]], writes=[b_accs])
            dve(lambda e, e4=e4, y4=y4, a_bc=a_bc: e.tensor_tensor(
                y4, e4, a_bc, ALU.mult),
                reads=[b_accs, b_P[5]], writes=[b_ys])

    def emit_router():
        for k in range(KC):
            mm(P[0][0:NE, 0:NT], routs[:, k * NE:(k + 1) * NE], ys[:, k * NT:(k + 1) * NT],
               start=(k == 0), stop=False, reads=[b_routs, b_ys], writes=[b_P[0]])
        for k in range(KC):
            mm(P[0][0:NE, 0:NT], routs[:, KC * NE + k * NE:KC * NE + (k + 1) * NE],
               ys[:, k * NT:(k + 1) * NT],
               start=False, stop=(k == KC - 1), reads=[b_routs, b_ys], writes=[b_P[0]])
        dve(lambda e: e.tensor_scalar_mul(scor[:, :], P[0][0:NE, 0:NT], 0.5),
            reads=[b_P[0]], writes=[b_scor])
        act(lambda e: e.activation(scor[:, :], scor[:, :], AF.Tanh),
            reads=[b_scor], writes=[b_scor])
        dve(lambda e: e.tensor_scalar(scor[:, :], scor[:, :], 0.5, 0.5, ALU.mult, ALU.add),
            reads=[b_scor], writes=[b_scor])
        for c in range(CPT):
            pe(lambda e, c=c: e.transpose(P[1][:, 0:NE], scor[:, c * CH:(c + 1) * CH],
                                          iden_f[0:NE, 0:NE]),
               reads=[b_scor], writes=[b_P[1]], extra=tok_consts)
            t = stm
            dve(lambda e: e.tensor_copy(t[:, 0:8], P[1][:, 0:NE]),
                reads=[b_P[1]], writes=[b_stm])
            dve(lambda e: e.tensor_add(t[:, 8:16], t[:, 0:8], ebbc[:, :]),
                reads=[b_stm], writes=[b_stm], extra=tok_consts)
            dve(lambda e: e.tensor_reduce(t[:, 16:17], t[:, 8:16], AX.X, ALU.max),
                reads=[b_stm], writes=[b_stm])
            dve(lambda e: e.tensor_scalar(t[:, 24:32], t[:, 8:16], t[:, 16:17], None, ALU.is_ge),
                reads=[b_stm], writes=[b_stm])
            dve(lambda e: e.scalar_tensor_tensor(t[:, 32:40], t[:, 24:32], -1e9, t[:, 8:16],
                                                 ALU.mult, ALU.add),
                reads=[b_stm], writes=[b_stm])
            dve(lambda e: e.tensor_reduce(t[:, 17:18], t[:, 32:40], AX.X, ALU.max),
                reads=[b_stm], writes=[b_stm])
            dve(lambda e: e.tensor_scalar(t[:, 40:48], t[:, 32:40], t[:, 17:18], None, ALU.is_ge),
                reads=[b_stm], writes=[b_stm])
            dve(lambda e: e.tensor_add(t[:, 40:48], t[:, 40:48], t[:, 24:32]),
                reads=[b_stm], writes=[b_stm])
            dve(lambda e: e.tensor_mul(t[:, 48:56], t[:, 0:8], t[:, 40:48]),
                reads=[b_stm], writes=[b_stm])
            dve(lambda e: e.tensor_reduce(t[:, 18:19], t[:, 48:56], AX.X, ALU.add),
                reads=[b_stm], writes=[b_stm])
            dve(lambda e: e.tensor_scalar_add(t[:, 18:19], t[:, 18:19], 1e-9),
                reads=[b_stm], writes=[b_stm])
            dve(lambda e: e.reciprocal(t[:, 19:20], t[:, 18:19]),
                reads=[b_stm], writes=[b_stm])
            dve(lambda e: e.tensor_scalar(t[:, 56:64], t[:, 48:56], t[:, 19:20], None, ALU.mult),
                reads=[b_stm], writes=[b_stm])
            for e_ in range(NE):
                pb = 1 + (e_ % 2)
                pe(lambda e, e_=e_, pb=pb: e.transpose(
                    P[pb][0:1, 0:CH], stm[:, 56 + e_:57 + e_], iden_f[:, :]),
                   reads=[b_stm], writes=[b_P[pb]])
                dve(lambda e, e_=e_, pb=pb: e.tensor_copy(
                    combr[0:1, e_ * CH:(e_ + 1) * CH], P[pb][0:1, 0:CH]),
                    reads=[b_P[pb]], writes=[b_combr])
            for e_ in range(NE):
                pb = 1 + (e_ % 2)
                mm(P[pb][:, 0:CH], ones1b[:, :], combr[0:1, e_ * CH:(e_ + 1) * CH],
                   start=True, stop=True, reads=[b_combr], writes=[b_P[pb]])
                act(lambda e, e_=e_, pb=pb, c=c: e.activation(
                    combs[:, e_ * NT + c * CH: e_ * NT + (c + 1) * CH],
                    P[pb][:, 0:CH], AF.Copy),
                    reads=[b_P[pb]], writes=[b_combs[0], b_combs[1]])

    def emit_ffn(dest_is_out, tile):
        for half in range(2):
            for i in range(HSH // 2 // 128):
                hc = half * 32 + i
                blk = hc // 2
                if hc % 2 == 0:
                    s1, tok1 = stream_unit(wg, b_wg, ("sw1b", blk), sw1b[blk], "sq")
                    s2, tok2 = stream_unit(wg2, b_wg2, ("sw2b", blk), sw2b[blk], "aq")
                    emit_ffn.cur = (s1, s2)
                s1, s2 = emit_ffn.cur
                pg = P[2 * (i % 2)]
                pu = P[2 * (i % 2) + 1]
                for k in range(KC):
                    mm(pg[:, 0:NT], arena_tile(wg, s1, k, (hc % 2) * 128, 128),
                       ys[:, k * NT:(k + 1) * NT], start=(k == 0), stop=(k == KC - 1),
                       reads=[b_wg[s1], b_ys], writes=[b_P[2 * (i % 2)]])
                for k in range(KC):
                    mm(pu[:, 0:NT], arena_tile(wg2, s2, k, (hc % 2) * 128, 128),
                       ys[:, k * NT:(k + 1) * NT], start=(k == 0), stop=(k == KC - 1),
                       reads=[b_wg2[s2], b_ys], writes=[b_P[2 * (i % 2) + 1]])
                hreg = b_hsA if i < 16 else b_hsB
                hview = hs[:, i * NT: (i + 1) * NT] if i < 16 else \
                    hs[:, HSB + (i - 16) * NT: HSB + (i - 15) * NT]
                act(lambda e, pg=pg, i=i: e.activation(
                    attntm_silu[:, (i % 2) * 1024:(i % 2) * 1024 + NT],
                    pg[:, 0:NT], AF.Silu),
                    reads=[b_P[2 * (i % 2)]], writes=[b_att_sl[i % 2]])
                dve(lambda e, pu=pu, hv=hview, i=i: e.tensor_mul(
                    hv, attntm_silu[:, (i % 2) * 1024:(i % 2) * 1024 + NT], pu[:, 0:NT]),
                    reads=[b_att_sl[i % 2], b_P[2 * (i % 2) + 1]], writes=[hreg])
            for mp in range(KC // 2):
                sus = []
                for uu in range(2):
                    su, _ = stream_unit(wo3, b_wo3, ("sw3b", mp, half * 2 + uu),
                                        sw3b[mp, half * 2 + uu], "sq" if uu == 0 else "aq")
                    sus.append(su)
                for m in (2 * mp, 2 * mp + 1):
                    po = P[4 + (m % 2)]
                    for kk in range(32):
                        hv = hs[:, kk * NT:(kk + 1) * NT] if kk < 16 else \
                            hs[:, HSB + (kk - 16) * NT:HSB + (kk - 15) * NT]
                        hreg = b_hsA if kk < 16 else b_hsB
                        su = sus[kk // 16]
                        mm(po[:, 0:NT], arena_tile(wo3, su, kk % 16, (m % 2) * 128, 128),
                           hv, start=(kk == 0), stop=(kk == 31),
                           reads=[b_wo3[su], hreg], writes=[b_P[4 + (m % 2)]])
                    dve(lambda e, m=m, po=po: e.scalar_tensor_tensor(
                        xs[:, m * NT:(m + 1) * NT], po[:, 0:NT], 0.5,
                        xs[:, m * NT:(m + 1) * NT], ALU.mult, ALU.add),
                        reads=[b_P[4 + (m % 2)], b_xs], writes=[b_xs])
        for half in range(2):
            for ei in range(4):
                e_ = half * 4 + ei
                for jp in range(HR // 256):
                    s1, _ = stream_unit(wg, b_wg, ("w12b1", e_, jp), w12b1[e_, jp], "sq")
                    s2, _ = stream_unit(wg2, b_wg2, ("w12b2", e_, jp), w12b2[e_, jp], "aq")
                    for jj in range(2):
                        j2 = jp * 2 + jj
                        ph1 = P[2 * (j2 % 2)]
                        ph2 = P[2 * (j2 % 2) + 1]
                        for k in range(KC):
                            mm(ph1[:, 0:NT], arena_tile(wg, s1, k, jj * 128, 128),
                               ys[:, k * NT:(k + 1) * NT], start=(k == 0), stop=(k == KC - 1),
                               reads=[b_wg[s1], b_ys], writes=[b_P[2 * (j2 % 2)]])
                        for k in range(KC):
                            mm(ph2[:, 0:NT], arena_tile(wg2, s2, k, jj * 128, 128),
                               ys[:, k * NT:(k + 1) * NT], start=(k == 0), stop=(k == KC - 1),
                               reads=[b_wg2[s2], b_ys], writes=[b_P[2 * (j2 % 2) + 1]])
                        kk = ei * 8 + j2
                        hv = hs[:, kk * NT:(kk + 1) * NT] if kk < 16 else \
                            hs[:, HSB + (kk - 16) * NT:HSB + (kk - 15) * NT]
                        hreg = b_hsA if kk < 16 else b_hsB
                        act(lambda e, ph1=ph1, j2=j2: e.activation(
                            attntm_silu[:, (j2 % 2) * 1024:(j2 % 2) * 1024 + NT],
                            ph1[:, 0:NT], AF.Silu),
                            reads=[b_P[2 * (j2 % 2)]], writes=[b_att_sl[j2 % 2]])
                        dve(lambda e, ph2=ph2, j2=j2: e.tensor_mul(
                            silu2[:, (j2 % 2) * 1024:(j2 % 2) * 1024 + NT],
                            attntm_silu[:, (j2 % 2) * 1024:(j2 % 2) * 1024 + NT], ph2[:, 0:NT]),
                            reads=[b_att_sl[j2 % 2], b_P[2 * (j2 % 2) + 1]], writes=[b_kftm_sl[j2 % 2]])
                        dve(lambda e, hv=hv, j2=j2, e_=e_: e.tensor_mul(
                            hv, silu2[:, (j2 % 2) * 1024:(j2 % 2) * 1024 + NT],
                            combs[:, e_ * NT:(e_ + 1) * NT]),
                            reads=[b_kftm_sl[j2 % 2], b_combs[0], b_combs[1]], writes=[hreg])
            for mp in range(KC // 2):
                sus = []
                for uu in range(2):
                    su, _ = stream_unit(wo3, b_wo3, ("w3b", mp, half * 2 + uu),
                                        w3b[mp, half * 2 + uu], "sq" if uu == 0 else "aq")
                    sus.append(su)
                for m in (2 * mp, 2 * mp + 1):
                    po = P[4 + (m % 2)]
                    for kk in range(32):
                        hv = hs[:, kk * NT:(kk + 1) * NT] if kk < 16 else \
                            hs[:, HSB + (kk - 16) * NT:HSB + (kk - 15) * NT]
                        hreg = b_hsA if kk < 16 else b_hsB
                        su = sus[kk // 16]
                        mm(po[:, 0:NT], arena_tile(wo3, su, kk % 16, (m % 2) * 128, 128),
                           hv, start=(kk == 0), stop=(kk == 31),
                           reads=[b_wo3[su], hreg], writes=[b_P[4 + (m % 2)]])
                    dve(lambda e, m=m, po=po: e.scalar_tensor_tensor(
                        xs[:, m * NT:(m + 1) * NT], po[:, 0:NT], 0.5,
                        xs[:, m * NT:(m + 1) * NT], ALU.mult, ALU.add),
                        reads=[b_P[4 + (m % 2)], b_xs], writes=[b_xs])


    def emit_eluplus1(psum, b_psum, bias_col, dst_view, dst_buf):
        a = accs
        act(lambda e: e.activation(a[:, 0:NT], psum[:, 0:NT], AF.Relu, bias=bias_col),
            reads=[b_psum], writes=[b_accs], extra=tok_consts)
        act(lambda e: e.activation(a[:, NT:2 * NT], psum[:, 0:NT], AF.Identity, bias=bias_col),
            reads=[b_psum], writes=[b_accs])
        dve(lambda e: e.tensor_sub(a[:, NT:2 * NT], a[:, NT:2 * NT], a[:, 0:NT]),
            reads=[b_accs], writes=[b_accs])
        act(lambda e: e.activation(a[:, 2 * NT:3 * NT], a[:, NT:2 * NT], AF.Exp),
            reads=[b_accs], writes=[b_accs])
        dve(lambda e: e.tensor_add(dst_view, a[:, 0:NT], a[:, 2 * NT:3 * NT]),
            reads=[b_accs], writes=[dst_buf])

    for t in range(T):
        dma("sq", xs[:, :], x_in[t], writes=[b_xs])
        emit_ln(b_xs, xs)
        emit_router()
        emit_ffn(False, t)
        dma("gp", x1_d[t], xs[:, :], reads=[b_xs], writes=[b_x1d[t]])
        emit_ln(b_xs, xs)

        for which, wsrc, nmb, bias_sb, dstoff, dstbuf in (
                (0, wqb, "wqb", bqs, 0, b_hsA), (1, wkb, "wkb", bks, HSB, b_hsB)):
            for jp in range(KC // 2):
                s1, _ = stream_unit(wg, b_wg, (nmb, jp), wsrc[jp], "sq")
                for jj in range(2):
                    j = jp * 2 + jj
                    pq = P[2 + (j % 2)]
                    for k in range(KC):
                        mm(pq[:, 0:NT], arena_tile(wg, s1, k, jj * 128, 128),
                           ys[:, k * NT:(k + 1) * NT], start=(k == 0), stop=(k == KC - 1),
                           reads=[b_wg[s1], b_ys], writes=[b_P[2 + (j % 2)]])
                    emit_eluplus1(pq, b_P[2 + (j % 2)], bias_sb[:, j:j + 1],
                                  hs[:, dstoff + j * NT:dstoff + (j + 1) * NT], dstbuf)
        dma("gp", qf_d[t], hs[:, 0:HSB], reads=[b_hsA], writes=[b_qfd[t]])
        dma("gp", kf_d[t], hs[:, HSB:2 * HSB], reads=[b_hsB], writes=[b_kfd[t]])

        for cp in range(CPT // 2):
            for vh in range(2):
                for jp in range(4):
                    gj = vh * 4 + jp
                    s1, _ = stream_unit(wg2, b_wg2, ("wvb", gj), wvb[gj], "sq")
                    for jj in range(2):
                        j = gj * 2 + jj
                        lj = jp * 2 + jj
                        pv = P[2 + (j % 2)]
                        for k in range(KC):
                            mm(pv[:, 0:256], arena_tile(wg2, s1, k, jj * 128, 128),
                               ys[:, k * NT + cp * 256: k * NT + cp * 256 + 256],
                               start=(k == 0), stop=(k == KC - 1),
                               reads=[b_wg2[s1], b_ys], writes=[b_P[2 + (j % 2)]])
                        act(lambda e, pv=pv, j=j, lj=lj: e.activation(
                            vfm[:, lj * 256:(lj + 1) * 256], pv[:, 0:256], AF.Identity,
                            bias=bvs[:, j:j + 1]),
                            reads=[b_P[2 + (j % 2)]], writes=[b_vfm], extra=tok_consts)
                for ci in range(2):
                    c = cp * 2 + ci
                    vslot = c % 2
                    for lj in range(8):
                        pe(lambda e, lj=lj, ci=ci: e.transpose(
                            PT[:, ci * 1024 + lj * 128: ci * 1024 + (lj + 1) * 128],
                            vfm[:, lj * 256 + ci * 128: lj * 256 + ci * 128 + 128],
                            iden_b[:, :]),
                           reads=[b_vfm], writes=[b_PT[ci]])
                    dve(lambda e, ci=ci, vslot=vslot, vh=vh: e.tensor_copy(
                        combs[:, vslot * 2048 + vh * 1024: vslot * 2048 + (vh + 1) * 1024],
                        PT[:, ci * 1024: (ci + 1) * 1024]),
                        reads=[b_PT[ci]], writes=[b_combs[vslot]])
            for ci in range(2):
                c = cp * 2 + ci
                g = t * CPT + c
                vslot = c % 2
                dma("gp", v_d[g][:, :], combs[:, vslot * 2048: vslot * 2048 + DM],
                    reads=[b_combs[vslot]], writes=[b_vd[g]])
                for h in range(NHEAD):
                    pt = h // 8
                    pe(lambda e, h=h, c=c, pt=pt: e.transpose(
                        PT[:, pt * 1024 + (h % 8) * 128: pt * 1024 + (h % 8) * 128 + 128],
                        hs[:, HSB + h * NT + c * CH: HSB + h * NT + c * CH + 128],
                        iden_b[:, :]),
                       reads=[b_hsB], writes=[b_PT[pt]])
                    if h % 8 == 7:
                        dve(lambda e, pt=pt, h=h: e.tensor_copy(
                            kftm[:, (h - 7) * 128:(h + 1) * 128],
                            PT[:, pt * 1024: pt * 1024 + 1024]),
                            reads=[b_PT[pt]], writes=[b_kftm_sl[pt]])
                for h in range(NHEAD):
                    pb = (0, 1, 4, 5)[h // 4]
                    mm(P[pb][:, (h % 4) * 128:(h % 4) * 128 + 128],
                       kftm[:, h * 128:(h + 1) * 128],
                       combs[:, vslot * 2048 + h * 128: vslot * 2048 + (h + 1) * 128],
                       start=True, stop=True,
                       reads=[b_kftm_sl[h // 8], b_combs[vslot]], writes=[b_P[pb]])
                for g4 in range(4):
                    pb = (0, 1, 4, 5)[g4]
                    if t == 0 and c == 0:
                        dve(lambda e, pb=pb, g4=g4: e.tensor_copy(
                            kvs[:, g4 * 512:(g4 + 1) * 512], P[pb][:, 0:512]),
                            reads=[b_P[pb]], writes=[b_kvs])
                    else:
                        dve(lambda e, pb=pb, g4=g4: e.tensor_add(
                            kvs[:, g4 * 512:(g4 + 1) * 512],
                            kvs[:, g4 * 512:(g4 + 1) * 512], P[pb][:, 0:512]),
                            reads=[b_P[pb], b_kvs], writes=[b_kvs])
                if c == CPT - 1:
                    kf3 = hs[:, HSB:2 * HSB].rearrange("p (j n) -> p j n", j=NHEAD)
                    if t == 0:
                        dve(lambda e, kf3=kf3: e.tensor_reduce(ksums[:, :], kf3, AX.X, ALU.add),
                            reads=[b_hsB], writes=[b_ksums])
                    else:
                        dve(lambda e, kf3=kf3: e.tensor_reduce(stm[:, 0:NHEAD], kf3, AX.X, ALU.add),
                            reads=[b_hsB], writes=[b_stm])
                        dve(lambda e: e.tensor_add(ksums[:, :], ksums[:, :], stm[:, 0:NHEAD]),
                            reads=[b_stm, b_ksums], writes=[b_ksums])

    dma("gp", hs[:, HSB:2 * HSB], kf_d[0], reads=[b_kfd[0]], writes=[b_hsB])
    dma("gp", hs[:, 0:HSB], qf_d[0], reads=[b_qfd[0]], writes=[b_hsA])
    dma("gp", cc_in[:, 0:NHEAD * DH], kvs[:, :], reads=[b_kvs], writes=[b_ccin])
    dma("gp", cc_in[:, NHEAD * DH:], ksums[:, :], reads=[b_ksums], writes=[b_ccin])
    opp("gp", lambda e: e.collective_compute(
        "AllGather", ALU.bypass,
        replica_groups=[[2 * g, 2 * g + 1] for g in range(cfg.NC // 2)],
        ins=[cc_in[:, :]], outs=[cc_out[:, :]]),
        reads=[b_ccin], writes=[b_ccout], inc="coll")
    CCW = NHEAD * DH + NHEAD
    b_gslot = [Buf(), Buf()]
    for r in range(2):
        sl = r % 2
        gview = xs[:, sl * CCW: sl * CCW + CCW]
        dma("sq", gview, cc_out[r * 128:(r + 1) * 128, :],
            reads=[b_ccout], writes=[b_gslot[sl], b_xs])
        if r == 0:
            dve(lambda e, gv=gview, r=r: e.tensor_scalar(
                kvs[:, :], gv[:, 0:NHEAD * DH], prevm[:, r:r + 1], None, ALU.mult),
                reads=[b_gslot[sl], b_xs], writes=[b_kvs], extra=tok_consts)
            dve(lambda e, gv=gview, r=r: e.tensor_scalar(
                ksums[:, :], gv[:, NHEAD * DH:CCW], prevm[:, r:r + 1], None, ALU.mult),
                reads=[b_gslot[sl], b_xs], writes=[b_ksums])
        else:
            dve(lambda e, gv=gview, r=r: e.scalar_tensor_tensor(
                kvs[:, :], gv[:, 0:NHEAD * DH], prevm[:, r:r + 1], kvs[:, :],
                ALU.mult, ALU.add),
                reads=[b_gslot[sl], b_kvs, b_xs], writes=[b_kvs])
            dve(lambda e, gv=gview, r=r: e.scalar_tensor_tensor(
                ksums[:, :], gv[:, NHEAD * DH:CCW], prevm[:, r:r + 1], ksums[:, :],
                ALU.mult, ALU.add),
                reads=[b_gslot[sl], b_ksums, b_xs], writes=[b_ksums])
    dve(lambda e: e.tensor_copy(kvb[:, :], kvs[:, :]), reads=[b_kvs], writes=[b_kvb])
    dve(lambda e: e.tensor_copy(ksumb[:, :], ksums[:, :]), reads=[b_ksums], writes=[b_ksumb])

    for t in range(T):
        if t > 0:
            dma("gp", hs[:, HSB:2 * HSB], kf_d[t], reads=[b_kfd[t]], writes=[b_hsB])
            dma("gp", hs[:, 0:HSB], qf_d[t], reads=[b_qfd[t]], writes=[b_hsA])
        dma("gp", xs[:, :], x1_d[t], reads=[b_x1d[t]], writes=[b_xs])
        for c in range(CPT):
            g = t * CPT + c
            vslot = c % 2
            dma("gp", combs[:, vslot * 4 * NT: vslot * 4 * NT + DM], v_d[g][:, :],
                reads=[b_vd[g]], writes=[b_combs[vslot]])
            qf_h = lambda h: hs[:, h * NT + c * CH: h * NT + c * CH + 128]
            kf_h = lambda h: hs[:, HSB + h * NT + c * CH: HSB + h * NT + c * CH + 128]
            v_h = lambda h: combs[:, vslot * 4 * NT + h * 128: vslot * 4 * NT + (h + 1) * 128]
            for h in range(NHEAD):
                pb = (0, 1, 4, 5)[h // 4]
                mm(P[pb][:, (h % 4) * 128:(h % 4) * 128 + 128], kf_h(h), qf_h(h),
                   start=True, stop=True, reads=[b_hsA, b_hsB], writes=[b_P[pb]])
            for g4 in range(4):
                pb = (0, 1, 4, 5)[g4]
                dve(lambda e, pb=pb, g4=g4: e.tensor_mul(
                    atb[:, g4 * 512:(g4 + 1) * 512], P[pb][:, 0:512],
                    triu16[:, 0:512]),
                    reads=[b_P[pb]], writes=[b_atb], extra=tok_consts)
            for h in range(NHEAD):
                pt = h // 8
                pe(lambda e, h=h, pt=pt, src=kf_h(h): e.transpose(
                    PT[:, pt * 1024 + (h % 8) * 128: pt * 1024 + (h % 8) * 128 + 128],
                    src, iden_b[:, :]), reads=[b_hsB], writes=[b_PT[pt]])
                if h % 8 == 7:
                    dve(lambda e, pt=pt, h=h: e.tensor_copy(
                        kftm[:, (h - 7) * 128:(h + 1) * 128], PT[:, pt * 1024: pt * 1024 + 1024]),
                        reads=[b_PT[pt]], writes=[b_kftm_sl[pt]])
            for h in range(NHEAD):
                mm(P[3][:, h:h + 1], qf_h(h), ksumb[:, h:h + 1], start=True, stop=False,
                   reads=[b_hsA, b_ksumb], writes=[b_P[3]])
                mm(P[3][:, h:h + 1], atb[:, h * 128:(h + 1) * 128], colones[:, :],
                   start=False, stop=True, reads=[b_atb], writes=[b_P[3]])
            dve(lambda e: e.tensor_scalar_add(zs[:, :], P[3][:, 0:NHEAD], ATT_EPS),
                reads=[b_P[3]], writes=[b_z])
            dve(lambda e: e.reciprocal(zinv[:, :], zs[:, :]), reads=[b_z], writes=[b_zinv])
            for h in range(NHEAD):
                pb = 4 + (h // 4) % 2
                psl = P[pb][:, (h % 4) * 128:(h % 4) * 128 + 128]
                mm(psl, qf_h(h), kvb[:, h * 128:(h + 1) * 128], start=True, stop=False,
                   reads=[b_hsA, b_kvb], writes=[b_P[pb]])
                mm(psl, atb[:, h * 128:(h + 1) * 128], v_h(h), start=False, stop=True,
                   reads=[b_atb, b_combs[vslot]], writes=[b_P[pb]])
                act(lambda e, psl=psl, h=h: e.activation(
                    attntm[:, h * 128:(h + 1) * 128], psl, AF.Copy, scale=zinv[:, h:h + 1]),
                    reads=[b_P[pb], b_zinv], writes=[b_att_sl[h // 8]])
            for h in range(NHEAD):
                pt = h // 8
                pe(lambda e, h=h, pt=pt: e.transpose(
                    PT[:, pt * 1024 + (h % 8) * 128: pt * 1024 + (h % 8) * 128 + 128],
                    attntm[:, h * 128:(h + 1) * 128], iden_b[:, :]),
                   reads=[b_att_sl[h // 8]], writes=[b_PT[pt]])
                if h % 8 == 7:
                    for hh in range(h - 7, h + 1):
                        dve(lambda e, pt=pt, hh=hh, c=c: e.tensor_copy(
                            ys[:, hh * NT + c * CH: hh * NT + c * CH + 128],
                            PT[:, pt * 1024 + (hh % 8) * 128: pt * 1024 + (hh % 8) * 128 + 128]),
                            reads=[b_PT[pt]], writes=[b_ys])
            for h in range(NHEAD):
                pb = 4 + (h // 4) % 2
                mm(P[pb][:, (h % 4) * 128:(h % 4) * 128 + 128],
                   kftm[:, h * 128:(h + 1) * 128], v_h(h), start=True, stop=True,
                   reads=[b_kftm_sl[h // 8], b_combs[vslot]], writes=[b_P[pb]])
                if h % 4 == 3:
                    dve(lambda e, pb=pb, h=h: e.tensor_add(
                        kvs[:, (h - 3) * 128:(h + 1) * 128],
                        kvs[:, (h - 3) * 128:(h + 1) * 128], P[pb][:, 0:512]),
                        reads=[b_P[pb], b_kvs], writes=[b_kvs])
            dve(lambda e: e.tensor_copy(kvb[:, :], kvs[:, :]), reads=[b_kvs], writes=[b_kvb])
            kf3 = hs[:, HSB:2 * HSB].rearrange("p (j n) -> p j n", j=NHEAD)[:, :, c * CH:(c + 1) * CH]
            dve(lambda e, kf3=kf3: e.tensor_reduce(stm[:, 0:NHEAD], kf3, AX.X, ALU.add),
                reads=[b_hsB], writes=[b_stm])
            dve(lambda e: e.tensor_add(ksums[:, :], ksums[:, :], stm[:, 0:NHEAD]),
                reads=[b_stm, b_ksums], writes=[b_ksums])
            dve(lambda e: e.tensor_copy(ksumb[:, :], ksums[:, :]), reads=[b_ksums], writes=[b_ksumb])
        for mp in range(KC // 2):
            su, _ = stream_unit(wo3, b_wo3, ("wob", mp), wob[mp],
                                "sq" if mp % 2 == 0 else "aq")
            for m in (2 * mp, 2 * mp + 1):
                po = P[4 + (m % 2)]
                for j in range(KC):
                    mm(po[:, 0:NT], arena_tile(wo3, su, j, (m % 2) * 128, 128),
                       ys[:, j * NT:(j + 1) * NT], start=(j == 0), stop=(j == KC - 1),
                       reads=[b_wo3[su], b_ys], writes=[b_P[4 + (m % 2)]])
                dve(lambda e, m=m, po=po: e.scalar_tensor_tensor(
                    xs[:, m * NT:(m + 1) * NT], po[:, 0:NT], bos[:, m:m + 1],
                    xs[:, m * NT:(m + 1) * NT], ALU.add, ALU.add),
                    reads=[b_P[4 + (m % 2)], b_xs], writes=[b_xs])
        emit_ln(b_xs, xs)
        emit_router()
        emit_ffn(True, t)
        dma("aq", y_out[t], xs[:, :], reads=[b_xs], writes=[Buf()])

    with es:
        blk = es.enter_context(nc.Block())

        def make_runner(names, drain=()):
            names = set(names)
            def run(e):
                last = defaultdict(int)
                for (oe, fn, waits, inc_sem, inc_n) in S.ops:
                    if oe not in names:
                        continue
                    for s_, v_ in waits.items():
                        if oe == "pe" and s_ == "pe":
                            continue
                        if v_ > last[s_]:
                            e.wait_ge(sems[s_], v_)
                            last[s_] = v_
                    ins = fn(e)
                    ins.then_inc(sems[inc_sem], inc_n)
                for d in drain:
                    if S.cnt[d]:
                        e.wait_ge(sems[d], S.cnt[d])
            return run

        blk.tensor(make_runner(("pe",)))
        blk.scalar(make_runner(("act", "aq"), drain=[f"aq{i}" for i in range(NDSEM)]))
        blk.vector(make_runner(("dve",)))
        blk.sync(make_runner(("sq",)))
        blk.gpsimd(make_runner(("gp",), drain=[f"gp{i}" for i in range(NDSEM)]))
    build_nc.last_sched = S
    return nc


def _prep_inputs(cfg: CFG, I):
    f32 = lambda a: np.ascontiguousarray(np.asarray(a), dtype=np.float32)
    d = {}
    sw1 = np.asarray(I["L0_ffn_sw1"], dtype=np.float32)
    sw2 = np.asarray(I["L0_ffn_sw2"], dtype=np.float32)
    sw3 = np.asarray(I["L0_ffn_sw3"], dtype=np.float32)
    w12 = np.asarray(I["L0_ffn_w12"], dtype=np.float32)
    w3 = np.asarray(I["L0_ffn_w3"], dtype=np.float32)
    for nm, w in (("sw1p", sw1), ("sw2p", sw2)):
        d[nm] = f32(w.reshape(32, 256, 16, 128).transpose(0, 3, 2, 1)
                    .reshape(32, 128, 4096))
    d["sw3p"] = f32(sw3.reshape(8, 256, 4, 16, 128).transpose(0, 2, 4, 3, 1)
                    .reshape(8, 4, 128, 4096))
    d["w12p1"] = f32(w12[:, :HR, :].reshape(NE, 4, 256, 16, 128)
                     .transpose(0, 1, 4, 3, 2).reshape(NE, 4, 128, 4096))
    d["w12p2"] = f32(w12[:, HR:, :].reshape(NE, 4, 256, 16, 128)
                     .transpose(0, 1, 4, 3, 2).reshape(NE, 4, 128, 4096))
    d["w3p"] = f32(w3.transpose(0, 2, 1).reshape(8, 8, 128, 8, 256)
                   .transpose(3, 0, 1, 2, 4).reshape(8, 4, 16, 128, 256)
                   .transpose(0, 1, 3, 2, 4).reshape(8, 4, 128, 4096))
    for nm, key in (("wqp", "L0_attn_Wq"), ("wkp", "L0_attn_Wk"),
                    ("wvp", "L0_attn_Wv"), ("wop", "L0_attn_Wo")):
        w = np.asarray(I[key], dtype=np.float32)
        d[nm] = f32(w.reshape(16, 128, 8, 256).transpose(2, 1, 0, 3)
                    .reshape(8, 128, 4096))
    rt = f32(np.asarray(I["L0_ffn_router_w"]).T)
    d["routp"] = f32(rt.reshape(KC, 128, NE).transpose(1, 0, 2))
    d["ebbc"] = f32(np.broadcast_to(np.asarray(I["L0_ffn_expert_bias"]), (128, NE)))
    for nm, key in (("bq", "L0_attn_bq"), ("bk", "L0_attn_bk"),
                    ("bv", "L0_attn_bv"), ("bo", "L0_attn_bo")):
        d[nm] = f32(np.asarray(I[key]).reshape(KC, 128).T)
    import ml_dtypes
    d["iden_b"] = np.eye(128, dtype=ml_dtypes.bfloat16)
    d["iden_f"] = np.eye(128, dtype=np.float32)
    d["ones1"] = np.ones((1, 128), dtype=np.float32)
    d["ones1b"] = np.ones((1, 128), dtype=ml_dtypes.bfloat16)
    d["colones"] = np.ones((128, 1), dtype=ml_dtypes.bfloat16)
    d["triu"] = np.triu(np.ones((128, 128), dtype=np.float32))
    d["triu16"] = np.tile(d["triu"], (1, NHEAD)).astype(ml_dtypes.bfloat16)
    return d


def make_in_maps(cfg: CFG, inputs):
    shared = _prep_inputs(cfg, inputs)
    x = np.asarray(inputs["L0_attn_x"], dtype=np.float32)
    in_maps = []
    for c in range(cfg.NC):
        b = c // cfg.HPS
        h = c % cfg.HPS
        sl = x[b, h * cfg.TCORE:(h + 1) * cfg.TCORE, :]
        xt = np.ascontiguousarray(
            sl.reshape(cfg.NTILES, NT, KC, 128).transpose(0, 3, 2, 1)
            .reshape(cfg.NTILES, 128, KC * NT))
        pm = np.zeros((128, 2), dtype=np.float32)
        if c % cfg.HPS == 1:
            pm[:, 0] = 1.0
        m = dict(shared)
        m["x"] = xt
        m["prevm"] = pm
        in_maps.append(m)
    return in_maps


def assemble_out(cfg: CFG, x_shape, per_core_y):
    out = np.empty(x_shape, dtype=np.float32)
    for c in range(cfg.NC):
        b = c // cfg.HPS
        h = c % cfg.HPS
        yt = np.asarray(per_core_y[c])
        yt = (yt.reshape(cfg.NTILES, 128, KC, NT).transpose(0, 3, 2, 1)
              .reshape(cfg.TCORE, DM))
        out[b, h * cfg.TCORE:(h + 1) * cfg.TCORE, :] = yt
    return out


def run(cfg: CFG, inputs, **kw):
    in_maps = make_in_maps(cfg, inputs)
    nc = build_nc(cfg)
    res = run_bass_kernel_spmd(nc, in_maps, core_ids=list(range(cfg.NC)), **kw)
    x = np.asarray(inputs["L0_attn_x"])
    out = assemble_out(cfg, x.shape, [res.results[c]["y"] for c in range(cfg.NC)])
    return out, res


def kernel(**inputs):
    cfg = CFG(8, 4, 8192)
    out, _ = run(cfg, inputs)
    return out


# version 11
# speedup vs baseline: 1.1012x; 1.0333x over previous
import numpy as np
from collections import defaultdict
from contextlib import ExitStack

import concourse.bass as bass
import concourse.mybir as mybir
from concourse.bass_utils import run_bass_kernel_spmd

F32 = mybir.dt.float32
BF16 = mybir.dt.bfloat16
AF = mybir.ActivationFunctionType
ALU = mybir.AluOpType
AX = mybir.AxisListType

DM = 2048
KC = DM // 128
NHEAD = 16
DH = 128
HSH = 8192
HR = 1024
NE = 8
NT = 512
CH = 128
CPT = NT // CH
LN_EPS = 1e-5
ATT_EPS = 1e-6


class CFG:
    def __init__(self, n_cores, batch, seqlen):
        self.NC = n_cores
        self.B = batch
        self.S = seqlen
        self.TCORE = batch * seqlen // n_cores
        assert self.TCORE % NT == 0
        self.NTILES = self.TCORE // NT
        self.NCHUNK = self.TCORE // CH
        self.HPS = n_cores // batch
        assert seqlen % (self.HPS * CH) == 0


class Buf:
    __slots__ = ("w", "r")
    def __init__(self):
        self.w = {}
        self.r = {}


def _merge(dst, tok):
    for s, v in tok.items():
        if dst.get(s, -1) < v:
            dst[s] = v


_TRACE_SRC = False


class Sched:
    def __init__(self):
        self.ops = []
        self.cnt = defaultdict(int)
        self.src = []

    def op(self, eng, fn, reads=(), writes=(), extra=None, inc=None, inc_n=1):
        if _TRACE_SRC:
            import sys as _s
            f = _s._getframe(1)
            skip = ("op", "opp", "mm", "dma", "act", "dve", "pe", "cvt", "stream_unit")
            helper = None
            while f.f_code.co_name in skip and f.f_back is not None:
                helper = f.f_code.co_name
                f = f.f_back
            self.src.append((helper, f.f_code.co_name, f.f_lineno))
        waits = {}
        if extra:
            _merge(waits, extra)
        for b in reads:
            _merge(waits, b.w)
        for b in writes:
            _merge(waits, b.w)
            _merge(waits, b.r)
        sem = inc if inc is not None else eng
        self.cnt[sem] += inc_n
        tok = {sem: self.cnt[sem]}
        self.ops.append((eng, fn, waits, sem, inc_n))
        for b in reads:
            _merge(b.r, tok)
        for b in writes:
            b.w = dict(tok)
            b.r = {}
        return tok


def build_nc(cfg: CFG):
    nc = bass.Bass("TRN2", target_bir_lowering=False, debug=False)
    T = cfg.NTILES

    def ein(name, shape, dt=F32):
        return nc.dram_tensor(name, shape, dt, kind="ExternalInput").ap()

    U = KC * 256
    x_in = ein("x", [T, 128, KC * NT])
    sw1p = ein("sw1p", [HSH // 256, 128, U])
    sw2p = ein("sw2p", [HSH // 256, 128, U])
    sw3p = ein("sw3p", [KC // 2, 4, 128, U])
    w12p1 = ein("w12p1", [NE, HR // 256, 128, U])
    w12p2 = ein("w12p2", [NE, HR // 256, 128, U])
    w3p = ein("w3p", [KC // 2, 4, 128, U])
    wqp = ein("wqp", [KC // 2, 128, U])
    wkp = ein("wkp", [KC // 2, 128, U])
    wvp = ein("wvp", [KC // 2, 128, U])
    wop = ein("wop", [KC // 2, 128, U])
    routp = ein("routp", [128, KC, NE])
    ebbc_in = ein("ebbc", [128, NE])
    bq_in = ein("bq", [128, KC])
    bk_in = ein("bk", [128, KC])
    bv_in = ein("bv", [128, KC])
    bo_in = ein("bo", [128, KC])
    iden_b_in = ein("iden_b", [128, 128], BF16)
    iden_f_in = ein("iden_f", [128, 128])
    ones1_in = ein("ones1", [1, 128])
    ones1b_in = ein("ones1b", [1, 128], BF16)
    colones_in = ein("colones", [128, 1], BF16)
    triu_in = ein("triu", [128, 128])
    triu16_in = ein("triu16", [128, NHEAD * CH], BF16)
    prevm_in = ein("prevm", [128, 2])

    y_out = nc.dram_tensor("y", [T, 128, KC * NT], F32, kind="ExternalOutput").ap()

    def idram(name, shape, dt):
        return nc.dram_tensor(name, shape, dt).ap()

    sw1b = idram("sw1b", [HSH // 256, 128, U], BF16)
    sw2b = idram("sw2b", [HSH // 256, 128, U], BF16)
    sw3b = idram("sw3b", [KC // 2, 4, 128, U], BF16)
    w12b1 = idram("w12b1", [NE, HR // 256, 128, U], BF16)
    w12b2 = idram("w12b2", [NE, HR // 256, 128, U], BF16)
    w3b = idram("w3b", [KC // 2, 4, 128, U], BF16)
    wqb = idram("wqb", [KC // 2, 128, U], BF16)
    wkb = idram("wkb", [KC // 2, 128, U], BF16)
    wvb = idram("wvb", [KC // 2, 128, U], BF16)
    wob = idram("wob", [KC // 2, 128, U], BF16)

    x1_d = idram("x1_d", [T, 128, KC * NT], F32)
    qf_d = idram("qf_d", [T, 128, KC * NT], BF16)
    kf_d = idram("kf_d", [T, 128, KC * NT], BF16)
    v_d = idram("v_d", [cfg.NCHUNK, CH, DM], BF16)
    cc_in = idram("cc_in", [128, NHEAD * DH + NHEAD], F32)
    cc_out = idram("cc_out", [2 * 128, NHEAD * DH + NHEAD], F32)

    es = ExitStack()
    sb = lambda n, c, dt: es.enter_context(nc.sbuf_tensor(n, [128, c], dt))

    xs = sb("xs", KC * NT, F32)
    ys = sb("ys", KC * NT, BF16)
    hs = sb("hs", 2 * KC * NT, BF16)
    accs = sb("accs", 4 * NT, F32)
    combs = sb("combs", NE * NT, BF16)
    vfm = sb("vfm", NHEAD * CH, BF16)
    atb = vfm
    wg = sb("wg", 3 * KC * 256, BF16)
    wg2 = sb("wg2", 3 * KC * 256, BF16)
    wo3 = sb("wo3", 3 * KC * 256, BF16)
    kftm = sb("kftm", NHEAD * CH, BF16)
    attntm = sb("attntm", NHEAD * CH, BF16)
    kvs = sb("kvs", NHEAD * DH, F32)
    kvb = sb("kvb", NHEAD * DH, BF16)
    ksums = sb("ksums", NHEAD, F32)
    ksumb = sb("ksumb", NHEAD, BF16)
    routs = sb("routs", 2 * KC * NE, BF16)
    bqs = sb("bqs", KC, F32)
    bks = sb("bks", KC, F32)
    bvs = sb("bvs", KC, F32)
    bos = sb("bos", KC, F32)
    ebbc = sb("ebbcS", NE, F32)
    iden_b = sb("iden_bS", 128, BF16)
    iden_f = sb("iden_fS", 128, F32)
    colones = sb("colonesS", 1, BF16)
    triu16 = sb("triu16S", NHEAD * CH, BF16)
    prevm = sb("prevmS", 2, F32)
    stm = sb("stm", 64, F32)
    zs = sb("zs", NHEAD, F32)
    zinv = sb("zinv", NHEAD, F32)
    ones1 = es.enter_context(nc.sbuf_tensor("ones1S", [1, 128], F32))
    ones1b = es.enter_context(nc.sbuf_tensor("ones1bS", [1, 128], BF16))
    rowb = es.enter_context(nc.sbuf_tensor("rowb", [1, 2 * NT], F32))
    scor = es.enter_context(nc.sbuf_tensor("scor", [NE, NT], F32))
    combr = es.enter_context(nc.sbuf_tensor("combr", [1, NE * CH], BF16))
    attntm_silu = attntm
    silu2 = kftm

    P = [es.enter_context(nc.psum_tensor(f"P{i}", [128, 512], F32)) for i in range(6)]
    PT = es.enter_context(nc.psum_tensor("PT", [128, 2048], BF16))

    NDSEM = 6
    sem_names = (["pe", "act", "dve", "coll"]
                 + [f"sq{i}" for i in range(NDSEM)]
                 + [f"gp{i}" for i in range(NDSEM)]
                 + [f"aq{i}" for i in range(NDSEM)])
    sems = {n: es.enter_context(nc.semaphore(f"s_{n}")) for n in sem_names}

    S = Sched()
    dma_ring = {"sq": 0, "gp": 0, "aq": 0}

    b_xs, b_ys, b_hsA, b_hsB, b_accs, b_vfm = (Buf() for _ in range(6))
    b_atb = b_vfm
    b_combs = [Buf(), Buf()]
    b_wg = [Buf(), Buf(), Buf()]
    b_wg2 = [Buf(), Buf(), Buf()]
    b_wo3 = [Buf(), Buf(), Buf()]
    b_att_sl = [Buf(), Buf()]
    b_kftm_sl = [Buf(), Buf()]
    b_kvs, b_kvb, b_ksums, b_ksumb = Buf(), Buf(), Buf(), Buf()
    b_stm, b_row, b_scor, b_combr, b_z, b_zinv = (Buf() for _ in range(6))
    b_routs, b_rtmp = Buf(), Buf()
    b_P = [Buf() for _ in range(6)]
    b_PT = [Buf(), Buf()]
    b_x1d = [Buf() for _ in range(T)]
    b_qfd = [Buf() for _ in range(T)]
    b_kfd = [Buf() for _ in range(T)]
    b_vd = [Buf() for _ in range(cfg.NCHUNK)]
    b_ccin, b_ccout = Buf(), Buf()
    b_wcvt = defaultdict(Buf)

    HSB = KC * NT

    def opp(eng, fn, reads=(), writes=(), extra=None, inc=None, inc_n=1):
        return S.op(eng, fn, reads=reads, writes=writes, extra=extra, inc=inc, inc_n=inc_n)

    def dma(queue, out_ap, in_ap, reads=(), writes=(), extra=None):
        i = dma_ring[queue]
        dma_ring[queue] = (i + 1) % NDSEM
        sem = f"{queue}{i}"
        slot_free = {sem: S.cnt[sem]}
        if extra:
            slot_free = dict(slot_free)
            _merge(slot_free, extra)
        return opp(queue, lambda e: e.dma_start(out=out_ap, in_=in_ap),
                   reads=reads, writes=writes, extra=slot_free, inc=sem, inc_n=16)

    def mm(out_ap, lhsT_ap, rhs_ap, start, stop, reads=(), writes=(), extra=None, sgc=False):
        return opp("pe", lambda e, o=out_ap, l=lhsT_ap, r=rhs_ap, st=start, sp=stop, g=sgc:
                   e.matmul(o, l, r, start=st, stop=sp, skip_group_check=g),
                   reads=reads, writes=writes, extra=extra)

    def act(fn, reads=(), writes=(), extra=None):
        return opp("act", fn, reads=reads, writes=writes, extra=extra)

    def dve(fn, reads=(), writes=(), extra=None):
        return opp("dve", fn, reads=reads, writes=writes, extra=extra)

    def pe(fn, reads=(), writes=(), extra=None):
        return opp("pe", fn, reads=reads, writes=writes, extra=extra)

    tok_consts = {}
    for dst, srcap in ((bqs, bq_in), (bks, bk_in), (bvs, bv_in), (bos, bo_in),
                       (ebbc, ebbc_in), (iden_b, iden_b_in), (iden_f, iden_f_in),
                       (ones1, ones1_in), (ones1b, ones1b_in), (colones, colones_in),
                       (triu16, triu16_in), (prevm, prevm_in)):
        _merge(tok_consts, dma("sq", dst[:, :], srcap[:, :], writes=[Buf()]))

    RW = KC * NE
    dma("sq", accs[:, 0:RW], routp.rearrange("p k e -> p (k e)"), writes=[b_accs])
    dve(lambda e: e.tensor_copy(routs[:, 0:RW], accs[:, 0:RW]),
        reads=[b_accs], writes=[b_routs])
    dve(lambda e: e.tensor_copy(accs[:, RW:2 * RW], routs[:, 0:RW]),
        reads=[b_routs, b_accs], writes=[b_accs])
    dve(lambda e: e.tensor_sub(accs[:, RW:2 * RW], accs[:, 0:RW], accs[:, RW:2 * RW]),
        reads=[b_accs], writes=[b_accs])
    dve(lambda e: e.tensor_copy(routs[:, RW:2 * RW], accs[:, RW:2 * RW]),
        reads=[b_accs], writes=[b_routs])

    def cvt(key, out_ap, in_ap):
        return dma("gp", out_ap, in_ap, writes=[b_wcvt[key]])

    for blk in range(16):
        cvt(("sw1b", blk), sw1b[blk], sw1p[blk])
        cvt(("sw2b", blk), sw2b[blk], sw2p[blk])
    for mp in range(KC // 2):
        for u_ in (0, 1):
            cvt(("sw3b", mp, u_), sw3b[mp, u_], sw3p[mp, u_])
    for blk in range(16, 32):
        cvt(("sw1b", blk), sw1b[blk], sw1p[blk])
        cvt(("sw2b", blk), sw2b[blk], sw2p[blk])
    for mp in range(KC // 2):
        for u_ in (2, 3):
            cvt(("sw3b", mp, u_), sw3b[mp, u_], sw3p[mp, u_])
    for e_ in range(4):
        for jp in range(HR // 256):
            cvt(("w12b1", e_, jp), w12b1[e_, jp], w12p1[e_, jp])
            cvt(("w12b2", e_, jp), w12b2[e_, jp], w12p2[e_, jp])
    for mp in range(KC // 2):
        for u_ in (0, 1):
            cvt(("w3b", mp, u_), w3b[mp, u_], w3p[mp, u_])
    for e_ in range(4, NE):
        for jp in range(HR // 256):
            cvt(("w12b1", e_, jp), w12b1[e_, jp], w12p1[e_, jp])
            cvt(("w12b2", e_, jp), w12b2[e_, jp], w12p2[e_, jp])
    for mp in range(KC // 2):
        for u_ in (2, 3):
            cvt(("w3b", mp, u_), w3b[mp, u_], w3p[mp, u_])
    for nm, dst, src in (("wqb", wqb, wqp), ("wkb", wkb, wkp), ("wvb", wvb, wvp)):
        for jp in range(KC // 2):
            cvt((nm, jp), dst[jp], src[jp])
    for mp in range(KC // 2):
        cvt(("wob", mp), wob[mp], wop[mp])

    def arena_tile(arena, slot, kc, c0, w):
        off = slot * KC * 256 + kc * 256 + c0
        return arena[:, off:off + w]

    slot_state = {}

    def stream_unit(arena, b_arena, key, dram_slice, queue):
        k_ = id(arena)
        s_ = slot_state.get(k_, 0)
        slot_state[k_] = s_ + 1
        slot = s_ % len(b_arena)
        tok = dma(queue, arena[:, slot * U: (slot + 1) * U], dram_slice,
                  reads=[b_wcvt[key]], writes=[b_arena[slot]])
        return slot, tok

    def emit_ln(src_buf, src_sb):
        dve(lambda e: e.tensor_copy(hs[:, 0:HSB], src_sb[:, :]),
            reads=[src_buf], writes=[b_hsA])
        act(lambda e: e.activation(hs[:, HSB:2 * HSB], src_sb[:, :], AF.Square),
            reads=[src_buf], writes=[b_hsB])
        for k in range(KC):
            mm(P[3][0:1, 0:NT], colones[:, :], hs[:, k * NT:(k + 1) * NT],
               start=(k == 0), stop=(k == KC - 1), reads=[b_hsA], writes=[b_P[3]],
               extra=tok_consts)
        for k in range(KC):
            mm(P[4][0:1, 0:NT], colones[:, :], hs[:, HSB + k * NT:HSB + (k + 1) * NT],
               start=(k == 0), stop=(k == KC - 1), reads=[b_hsB], writes=[b_P[4]])
        r = rowb
        dve(lambda e: e.tensor_scalar_mul(r[:, 0:NT], P[3][0:1, 0:NT], 1.0 / DM),
            reads=[b_P[3]], writes=[b_row])
        dve(lambda e: e.tensor_scalar_mul(r[:, NT:2 * NT], P[4][0:1, 0:NT], 1.0 / DM),
            reads=[b_P[4]], writes=[b_row])
        dve(lambda e: e.tensor_mul(scor[0:1, 0:NT], r[:, 0:NT], r[:, 0:NT]),
            reads=[b_row], writes=[b_scor])
        dve(lambda e: e.tensor_sub(r[:, NT:2 * NT], r[:, NT:2 * NT], scor[0:1, 0:NT]),
            reads=[b_row, b_scor], writes=[b_row])
        dve(lambda e: e.tensor_scalar_add(r[:, NT:2 * NT], r[:, NT:2 * NT], LN_EPS),
            reads=[b_row], writes=[b_row])
        act(lambda e: e.activation(r[:, NT:2 * NT], r[:, NT:2 * NT], AF.Ln),
            reads=[b_row], writes=[b_row])
        dve(lambda e: e.tensor_scalar_mul(r[:, NT:2 * NT], r[:, NT:2 * NT], -0.5),
            reads=[b_row], writes=[b_row])
        act(lambda e: e.activation(r[:, NT:2 * NT], r[:, NT:2 * NT], AF.Exp),
            reads=[b_row], writes=[b_row])
        mm(P[4][:, 0:NT], ones1[:, :], r[:, 0:NT], start=True, stop=True,
           reads=[b_row], writes=[b_P[4]], extra=tok_consts)
        mm(P[5][:, 0:NT], ones1[:, :], r[:, NT:2 * NT], start=True, stop=True,
           reads=[b_row], writes=[b_P[5]])
        for g in range(4):
            sl = slice(g * 4 * NT, (g + 1) * 4 * NT)
            x4 = src_sb[:, sl].rearrange("p (k n) -> p k n", k=4)
            e4 = accs[:, 0:4 * NT].rearrange("p (k n) -> p k n", k=4)
            y4 = ys[:, sl].rearrange("p (k n) -> p k n", k=4)
            m_bc = P[4][:, None, 0:NT].broadcast_to([128, 4, NT])
            a_bc = P[5][:, None, 0:NT].broadcast_to([128, 4, NT])
            dve(lambda e, x4=x4, e4=e4, m_bc=m_bc: e.tensor_tensor(
                e4, x4, m_bc, ALU.subtract),
                reads=[src_buf, b_P[4]], writes=[b_accs])
            dve(lambda e, e4=e4, y4=y4, a_bc=a_bc: e.tensor_tensor(
                y4, e4, a_bc, ALU.mult),
                reads=[b_accs, b_P[5]], writes=[b_ys])

    def emit_router():
        for k in range(KC):
            mm(P[0][0:NE, 0:NT], routs[:, k * NE:(k + 1) * NE], ys[:, k * NT:(k + 1) * NT],
               start=(k == 0), stop=False, reads=[b_routs, b_ys], writes=[b_P[0]])
        for k in range(KC):
            mm(P[0][0:NE, 0:NT], routs[:, KC * NE + k * NE:KC * NE + (k + 1) * NE],
               ys[:, k * NT:(k + 1) * NT],
               start=False, stop=(k == KC - 1), reads=[b_routs, b_ys], writes=[b_P[0]])
        dve(lambda e: e.tensor_scalar_mul(scor[:, :], P[0][0:NE, 0:NT], 0.5),
            reads=[b_P[0]], writes=[b_scor])
        act(lambda e: e.activation(scor[:, :], scor[:, :], AF.Tanh),
            reads=[b_scor], writes=[b_scor])
        dve(lambda e: e.tensor_scalar(scor[:, :], scor[:, :], 0.5, 0.5, ALU.mult, ALU.add),
            reads=[b_scor], writes=[b_scor])
        for c in range(CPT):
            pe(lambda e, c=c: e.transpose(P[1][:, 0:NE], scor[:, c * CH:(c + 1) * CH],
                                          iden_f[0:NE, 0:NE]),
               reads=[b_scor], writes=[b_P[1]], extra=tok_consts)
            t = stm
            dve(lambda e: e.tensor_copy(t[:, 0:8], P[1][:, 0:NE]),
                reads=[b_P[1]], writes=[b_stm])
            dve(lambda e: e.tensor_add(t[:, 8:16], t[:, 0:8], ebbc[:, :]),
                reads=[b_stm], writes=[b_stm], extra=tok_consts)
            dve(lambda e: e.tensor_reduce(t[:, 16:17], t[:, 8:16], AX.X, ALU.max),
                reads=[b_stm], writes=[b_stm])
            dve(lambda e: e.tensor_scalar(t[:, 24:32], t[:, 8:16], t[:, 16:17], None, ALU.is_ge),
                reads=[b_stm], writes=[b_stm])
            dve(lambda e: e.scalar_tensor_tensor(t[:, 32:40], t[:, 24:32], -1e9, t[:, 8:16],
                                                 ALU.mult, ALU.add),
                reads=[b_stm], writes=[b_stm])
            dve(lambda e: e.tensor_reduce(t[:, 17:18], t[:, 32:40], AX.X, ALU.max),
                reads=[b_stm], writes=[b_stm])
            dve(lambda e: e.tensor_scalar(t[:, 40:48], t[:, 32:40], t[:, 17:18], None, ALU.is_ge),
                reads=[b_stm], writes=[b_stm])
            dve(lambda e: e.tensor_add(t[:, 40:48], t[:, 40:48], t[:, 24:32]),
                reads=[b_stm], writes=[b_stm])
            dve(lambda e: e.tensor_mul(t[:, 48:56], t[:, 0:8], t[:, 40:48]),
                reads=[b_stm], writes=[b_stm])
            dve(lambda e: e.tensor_reduce(t[:, 18:19], t[:, 48:56], AX.X, ALU.add),
                reads=[b_stm], writes=[b_stm])
            dve(lambda e: e.tensor_scalar_add(t[:, 18:19], t[:, 18:19], 1e-9),
                reads=[b_stm], writes=[b_stm])
            dve(lambda e: e.reciprocal(t[:, 19:20], t[:, 18:19]),
                reads=[b_stm], writes=[b_stm])
            dve(lambda e: e.tensor_scalar(t[:, 56:64], t[:, 48:56], t[:, 19:20], None, ALU.mult),
                reads=[b_stm], writes=[b_stm])
            for e_ in range(NE):
                pb = 1 + (e_ % 2)
                pe(lambda e, e_=e_, pb=pb: e.transpose(
                    P[pb][0:1, 0:CH], stm[:, 56 + e_:57 + e_], iden_f[:, :]),
                   reads=[b_stm], writes=[b_P[pb]])
                dve(lambda e, e_=e_, pb=pb: e.tensor_copy(
                    combr[0:1, e_ * CH:(e_ + 1) * CH], P[pb][0:1, 0:CH]),
                    reads=[b_P[pb]], writes=[b_combr])
            for e_ in range(NE):
                pb = 1 + (e_ % 2)
                mm(P[pb][:, 0:CH], ones1b[:, :], combr[0:1, e_ * CH:(e_ + 1) * CH],
                   start=True, stop=True, reads=[b_combr], writes=[b_P[pb]])
                act(lambda e, e_=e_, pb=pb, c=c: e.activation(
                    combs[:, e_ * NT + c * CH: e_ * NT + (c + 1) * CH],
                    P[pb][:, 0:CH], AF.Copy),
                    reads=[b_P[pb]], writes=[b_combs[0], b_combs[1]])

    def emit_ffn(dest_is_out, tile):
        for half in range(2):
            for i in range(HSH // 2 // 128):
                hc = half * 32 + i
                blk = hc // 2
                if hc % 2 == 0:
                    s1, tok1 = stream_unit(wg, b_wg, ("sw1b", blk), sw1b[blk], "sq")
                    s2, tok2 = stream_unit(wg2, b_wg2, ("sw2b", blk), sw2b[blk], "aq")
                    emit_ffn.cur = (s1, s2)
                s1, s2 = emit_ffn.cur
                pg = P[2 * (i % 2)]
                pu = P[2 * (i % 2) + 1]
                for k in range(KC):
                    mm(pg[:, 0:NT], arena_tile(wg, s1, k, (hc % 2) * 128, 128),
                       ys[:, k * NT:(k + 1) * NT], start=(k == 0), stop=(k == KC - 1),
                       reads=[b_wg[s1], b_ys], writes=[b_P[2 * (i % 2)]])
                for k in range(KC):
                    mm(pu[:, 0:NT], arena_tile(wg2, s2, k, (hc % 2) * 128, 128),
                       ys[:, k * NT:(k + 1) * NT], start=(k == 0), stop=(k == KC - 1),
                       reads=[b_wg2[s2], b_ys], writes=[b_P[2 * (i % 2) + 1]])
                hreg = b_hsA if i < 16 else b_hsB
                hview = hs[:, i * NT: (i + 1) * NT] if i < 16 else \
                    hs[:, HSB + (i - 16) * NT: HSB + (i - 15) * NT]
                act(lambda e, pg=pg, i=i: e.activation(
                    attntm_silu[:, (i % 2) * 1024:(i % 2) * 1024 + NT],
                    pg[:, 0:NT], AF.Silu),
                    reads=[b_P[2 * (i % 2)]], writes=[b_att_sl[i % 2]])
                dve(lambda e, pu=pu, hv=hview, i=i: e.tensor_mul(
                    hv, attntm_silu[:, (i % 2) * 1024:(i % 2) * 1024 + NT], pu[:, 0:NT]),
                    reads=[b_att_sl[i % 2], b_P[2 * (i % 2) + 1]], writes=[hreg])
            for mp in range(KC // 2):
                sus = []
                for uu in range(2):
                    su, _ = stream_unit(wo3, b_wo3, ("sw3b", mp, half * 2 + uu),
                                        sw3b[mp, half * 2 + uu], "sq" if uu == 0 else "aq")
                    sus.append(su)
                for m in (2 * mp, 2 * mp + 1):
                    po = P[4 + (m % 2)]
                    for kk in range(32):
                        hv = hs[:, kk * NT:(kk + 1) * NT] if kk < 16 else \
                            hs[:, HSB + (kk - 16) * NT:HSB + (kk - 15) * NT]
                        hreg = b_hsA if kk < 16 else b_hsB
                        su = sus[kk // 16]
                        mm(po[:, 0:NT], arena_tile(wo3, su, kk % 16, (m % 2) * 128, 128),
                           hv, start=(kk == 0), stop=(kk == 31),
                           reads=[b_wo3[su], hreg], writes=[b_P[4 + (m % 2)]])
                    dve(lambda e, m=m, po=po: e.scalar_tensor_tensor(
                        xs[:, m * NT:(m + 1) * NT], po[:, 0:NT], 0.5,
                        xs[:, m * NT:(m + 1) * NT], ALU.mult, ALU.add),
                        reads=[b_P[4 + (m % 2)], b_xs], writes=[b_xs])
        for half in range(2):
            for ei in range(4):
                e_ = half * 4 + ei
                for jp in range(HR // 256):
                    s1, _ = stream_unit(wg, b_wg, ("w12b1", e_, jp), w12b1[e_, jp], "sq")
                    s2, _ = stream_unit(wg2, b_wg2, ("w12b2", e_, jp), w12b2[e_, jp], "aq")
                    for jj in range(2):
                        j2 = jp * 2 + jj
                        ph1 = P[2 * (j2 % 2)]
                        ph2 = P[2 * (j2 % 2) + 1]
                        for k in range(KC):
                            mm(ph1[:, 0:NT], arena_tile(wg, s1, k, jj * 128, 128),
                               ys[:, k * NT:(k + 1) * NT], start=(k == 0), stop=(k == KC - 1),
                               reads=[b_wg[s1], b_ys], writes=[b_P[2 * (j2 % 2)]])
                        for k in range(KC):
                            mm(ph2[:, 0:NT], arena_tile(wg2, s2, k, jj * 128, 128),
                               ys[:, k * NT:(k + 1) * NT], start=(k == 0), stop=(k == KC - 1),
                               reads=[b_wg2[s2], b_ys], writes=[b_P[2 * (j2 % 2) + 1]])
                        kk = ei * 8 + j2
                        hv = hs[:, kk * NT:(kk + 1) * NT] if kk < 16 else \
                            hs[:, HSB + (kk - 16) * NT:HSB + (kk - 15) * NT]
                        hreg = b_hsA if kk < 16 else b_hsB
                        act(lambda e, ph1=ph1, j2=j2: e.activation(
                            attntm_silu[:, (j2 % 2) * 1024:(j2 % 2) * 1024 + NT],
                            ph1[:, 0:NT], AF.Silu),
                            reads=[b_P[2 * (j2 % 2)]], writes=[b_att_sl[j2 % 2]])
                        dve(lambda e, ph2=ph2, j2=j2: e.tensor_mul(
                            silu2[:, (j2 % 2) * 1024:(j2 % 2) * 1024 + NT],
                            attntm_silu[:, (j2 % 2) * 1024:(j2 % 2) * 1024 + NT], ph2[:, 0:NT]),
                            reads=[b_att_sl[j2 % 2], b_P[2 * (j2 % 2) + 1]], writes=[b_kftm_sl[j2 % 2]])
                        dve(lambda e, hv=hv, j2=j2, e_=e_: e.tensor_mul(
                            hv, silu2[:, (j2 % 2) * 1024:(j2 % 2) * 1024 + NT],
                            combs[:, e_ * NT:(e_ + 1) * NT]),
                            reads=[b_kftm_sl[j2 % 2], b_combs[0], b_combs[1]], writes=[hreg])
            for mp in range(KC // 2):
                sus = []
                for uu in range(2):
                    su, _ = stream_unit(wo3, b_wo3, ("w3b", mp, half * 2 + uu),
                                        w3b[mp, half * 2 + uu], "sq" if uu == 0 else "aq")
                    sus.append(su)
                for m in (2 * mp, 2 * mp + 1):
                    po = P[4 + (m % 2)]
                    for kk in range(32):
                        hv = hs[:, kk * NT:(kk + 1) * NT] if kk < 16 else \
                            hs[:, HSB + (kk - 16) * NT:HSB + (kk - 15) * NT]
                        hreg = b_hsA if kk < 16 else b_hsB
                        su = sus[kk // 16]
                        mm(po[:, 0:NT], arena_tile(wo3, su, kk % 16, (m % 2) * 128, 128),
                           hv, start=(kk == 0), stop=(kk == 31),
                           reads=[b_wo3[su], hreg], writes=[b_P[4 + (m % 2)]])
                    dve(lambda e, m=m, po=po: e.scalar_tensor_tensor(
                        xs[:, m * NT:(m + 1) * NT], po[:, 0:NT], 0.5,
                        xs[:, m * NT:(m + 1) * NT], ALU.mult, ALU.add),
                        reads=[b_P[4 + (m % 2)], b_xs], writes=[b_xs])


    def emit_eluplus1(psum, b_psum, bias_col, dst_view, dst_buf):
        a = accs
        act(lambda e: e.activation(a[:, 0:NT], psum[:, 0:NT], AF.Relu, bias=bias_col),
            reads=[b_psum], writes=[b_accs], extra=tok_consts)
        act(lambda e: e.activation(a[:, NT:2 * NT], psum[:, 0:NT], AF.Identity, bias=bias_col),
            reads=[b_psum], writes=[b_accs])
        dve(lambda e: e.tensor_sub(a[:, NT:2 * NT], a[:, NT:2 * NT], a[:, 0:NT]),
            reads=[b_accs], writes=[b_accs])
        act(lambda e: e.activation(a[:, 2 * NT:3 * NT], a[:, NT:2 * NT], AF.Exp),
            reads=[b_accs], writes=[b_accs])
        dve(lambda e: e.tensor_add(dst_view, a[:, 0:NT], a[:, 2 * NT:3 * NT]),
            reads=[b_accs], writes=[dst_buf])

    for t in range(T):
        dma("sq", xs[:, :], x_in[t], writes=[b_xs])
        emit_ln(b_xs, xs)
        emit_router()
        emit_ffn(False, t)
        dma("gp", x1_d[t], xs[:, :], reads=[b_xs], writes=[b_x1d[t]])
        emit_ln(b_xs, xs)

        for which, wsrc, nmb, bias_sb, dstoff, dstbuf in (
                (0, wqb, "wqb", bqs, 0, b_hsA), (1, wkb, "wkb", bks, HSB, b_hsB)):
            for jp in range(KC // 2):
                s1, _ = stream_unit(wg, b_wg, (nmb, jp), wsrc[jp], "sq")
                for jj in range(2):
                    j = jp * 2 + jj
                    pq = P[2 + (j % 2)]
                    for k in range(KC):
                        mm(pq[:, 0:NT], arena_tile(wg, s1, k, jj * 128, 128),
                           ys[:, k * NT:(k + 1) * NT], start=(k == 0), stop=(k == KC - 1),
                           reads=[b_wg[s1], b_ys], writes=[b_P[2 + (j % 2)]])
                    emit_eluplus1(pq, b_P[2 + (j % 2)], bias_sb[:, j:j + 1],
                                  hs[:, dstoff + j * NT:dstoff + (j + 1) * NT], dstbuf)
        dma("gp", qf_d[t], hs[:, 0:HSB], reads=[b_hsA], writes=[b_qfd[t]])
        dma("gp", kf_d[t], hs[:, HSB:2 * HSB], reads=[b_hsB], writes=[b_kfd[t]])

        for cp in range(CPT // 2):
            for vh in range(2):
                for jp in range(4):
                    gj = vh * 4 + jp
                    s1, _ = stream_unit(wg2, b_wg2, ("wvb", gj), wvb[gj], "sq")
                    for jj in range(2):
                        j = gj * 2 + jj
                        lj = jp * 2 + jj
                        pv = P[2 + (j % 2)]
                        for k in range(KC):
                            mm(pv[:, 0:256], arena_tile(wg2, s1, k, jj * 128, 128),
                               ys[:, k * NT + cp * 256: k * NT + cp * 256 + 256],
                               start=(k == 0), stop=(k == KC - 1),
                               reads=[b_wg2[s1], b_ys], writes=[b_P[2 + (j % 2)]])
                        act(lambda e, pv=pv, j=j, lj=lj: e.activation(
                            vfm[:, lj * 256:(lj + 1) * 256], pv[:, 0:256], AF.Identity,
                            bias=bvs[:, j:j + 1]),
                            reads=[b_P[2 + (j % 2)]], writes=[b_vfm], extra=tok_consts)
                for ci in range(2):
                    c = cp * 2 + ci
                    vslot = c % 2
                    for lj in range(8):
                        pe(lambda e, lj=lj, ci=ci: e.transpose(
                            PT[:, ci * 1024 + lj * 128: ci * 1024 + (lj + 1) * 128],
                            vfm[:, lj * 256 + ci * 128: lj * 256 + ci * 128 + 128],
                            iden_b[:, :]),
                           reads=[b_vfm], writes=[b_PT[ci]])
                    dve(lambda e, ci=ci, vslot=vslot, vh=vh: e.tensor_copy(
                        combs[:, vslot * 2048 + vh * 1024: vslot * 2048 + (vh + 1) * 1024],
                        PT[:, ci * 1024: (ci + 1) * 1024]),
                        reads=[b_PT[ci]], writes=[b_combs[vslot]])
            for ci in range(2):
                c = cp * 2 + ci
                g = t * CPT + c
                vslot = c % 2
                dma("gp", v_d[g][:, :], combs[:, vslot * 2048: vslot * 2048 + DM],
                    reads=[b_combs[vslot]], writes=[b_vd[g]])
                for h in range(NHEAD):
                    pt = h // 8
                    pe(lambda e, h=h, c=c, pt=pt: e.transpose(
                        PT[:, pt * 1024 + (h % 8) * 128: pt * 1024 + (h % 8) * 128 + 128],
                        hs[:, HSB + h * NT + c * CH: HSB + h * NT + c * CH + 128],
                        iden_b[:, :]),
                       reads=[b_hsB], writes=[b_PT[pt]])
                    if h % 8 == 7:
                        dve(lambda e, pt=pt, h=h: e.tensor_copy(
                            kftm[:, (h - 7) * 128:(h + 1) * 128],
                            PT[:, pt * 1024: pt * 1024 + 1024]),
                            reads=[b_PT[pt]], writes=[b_kftm_sl[pt]])
                for h in range(NHEAD):
                    pb = (0, 1, 4, 5)[h // 4]
                    mm(P[pb][:, (h % 4) * 128:(h % 4) * 128 + 128],
                       kftm[:, h * 128:(h + 1) * 128],
                       combs[:, vslot * 2048 + h * 128: vslot * 2048 + (h + 1) * 128],
                       start=True, stop=True,
                       reads=[b_kftm_sl[h // 8], b_combs[vslot]], writes=[b_P[pb]])
                for g4 in range(4):
                    pb = (0, 1, 4, 5)[g4]
                    if t == 0 and c == 0:
                        dve(lambda e, pb=pb, g4=g4: e.tensor_copy(
                            kvs[:, g4 * 512:(g4 + 1) * 512], P[pb][:, 0:512]),
                            reads=[b_P[pb]], writes=[b_kvs])
                    else:
                        dve(lambda e, pb=pb, g4=g4: e.tensor_add(
                            kvs[:, g4 * 512:(g4 + 1) * 512],
                            kvs[:, g4 * 512:(g4 + 1) * 512], P[pb][:, 0:512]),
                            reads=[b_P[pb], b_kvs], writes=[b_kvs])
                if c == CPT - 1:
                    kf3 = hs[:, HSB:2 * HSB].rearrange("p (j n) -> p j n", j=NHEAD)
                    if t == 0:
                        dve(lambda e, kf3=kf3: e.tensor_reduce(ksums[:, :], kf3, AX.X, ALU.add),
                            reads=[b_hsB], writes=[b_ksums])
                    else:
                        dve(lambda e, kf3=kf3: e.tensor_reduce(stm[:, 0:NHEAD], kf3, AX.X, ALU.add),
                            reads=[b_hsB], writes=[b_stm])
                        dve(lambda e: e.tensor_add(ksums[:, :], ksums[:, :], stm[:, 0:NHEAD]),
                            reads=[b_stm, b_ksums], writes=[b_ksums])

    dma("gp", hs[:, HSB:2 * HSB], kf_d[0], reads=[b_kfd[0]], writes=[b_hsB])
    dma("gp", hs[:, 0:HSB], qf_d[0], reads=[b_qfd[0]], writes=[b_hsA])
    dma("gp", cc_in[:, 0:NHEAD * DH], kvs[:, :], reads=[b_kvs], writes=[b_ccin])
    dma("gp", cc_in[:, NHEAD * DH:], ksums[:, :], reads=[b_ksums], writes=[b_ccin])
    opp("gp", lambda e: e.collective_compute(
        "AllGather", ALU.bypass,
        replica_groups=[[2 * g, 2 * g + 1] for g in range(cfg.NC // 2)],
        ins=[cc_in[:, :]], outs=[cc_out[:, :]]),
        reads=[b_ccin], writes=[b_ccout], inc="coll")
    CCW = NHEAD * DH + NHEAD
    b_gslot = [Buf(), Buf()]
    for r in range(2):
        sl = r % 2
        gview = xs[:, sl * CCW: sl * CCW + CCW]
        dma("sq", gview, cc_out[r * 128:(r + 1) * 128, :],
            reads=[b_ccout], writes=[b_gslot[sl], b_xs])
        if r == 0:
            dve(lambda e, gv=gview, r=r: e.tensor_scalar(
                kvs[:, :], gv[:, 0:NHEAD * DH], prevm[:, r:r + 1], None, ALU.mult),
                reads=[b_gslot[sl], b_xs], writes=[b_kvs], extra=tok_consts)
            dve(lambda e, gv=gview, r=r: e.tensor_scalar(
                ksums[:, :], gv[:, NHEAD * DH:CCW], prevm[:, r:r + 1], None, ALU.mult),
                reads=[b_gslot[sl], b_xs], writes=[b_ksums])
        else:
            dve(lambda e, gv=gview, r=r: e.scalar_tensor_tensor(
                kvs[:, :], gv[:, 0:NHEAD * DH], prevm[:, r:r + 1], kvs[:, :],
                ALU.mult, ALU.add),
                reads=[b_gslot[sl], b_kvs, b_xs], writes=[b_kvs])
            dve(lambda e, gv=gview, r=r: e.scalar_tensor_tensor(
                ksums[:, :], gv[:, NHEAD * DH:CCW], prevm[:, r:r + 1], ksums[:, :],
                ALU.mult, ALU.add),
                reads=[b_gslot[sl], b_ksums, b_xs], writes=[b_ksums])
    dve(lambda e: e.tensor_copy(kvb[:, :], kvs[:, :]), reads=[b_kvs], writes=[b_kvb])
    dve(lambda e: e.tensor_copy(ksumb[:, :], ksums[:, :]), reads=[b_ksums], writes=[b_ksumb])

    for t in range(T):
        if t > 0:
            dma("gp", hs[:, HSB:2 * HSB], kf_d[t], reads=[b_kfd[t]], writes=[b_hsB])
            dma("gp", hs[:, 0:HSB], qf_d[t], reads=[b_qfd[t]], writes=[b_hsA])
        dma("gp", xs[:, :], x1_d[t], reads=[b_x1d[t]], writes=[b_xs])
        for c in range(CPT):
            g = t * CPT + c
            vslot = c % 2
            dma("gp", combs[:, vslot * 4 * NT: vslot * 4 * NT + DM], v_d[g][:, :],
                reads=[b_vd[g]], writes=[b_combs[vslot]])
            qf_h = lambda h: hs[:, h * NT + c * CH: h * NT + c * CH + 128]
            kf_h = lambda h: hs[:, HSB + h * NT + c * CH: HSB + h * NT + c * CH + 128]
            v_h = lambda h: combs[:, vslot * 4 * NT + h * 128: vslot * 4 * NT + (h + 1) * 128]
            for h in range(NHEAD):
                pb = (0, 1, 4, 5)[h // 4]
                mm(P[pb][:, (h % 4) * 128:(h % 4) * 128 + 128], kf_h(h), qf_h(h),
                   start=True, stop=True, reads=[b_hsA, b_hsB], writes=[b_P[pb]])
            for g4 in range(4):
                pb = (0, 1, 4, 5)[g4]
                dve(lambda e, pb=pb, g4=g4: e.tensor_mul(
                    atb[:, g4 * 512:(g4 + 1) * 512], P[pb][:, 0:512],
                    triu16[:, 0:512]),
                    reads=[b_P[pb]], writes=[b_atb], extra=tok_consts)
            for h in range(NHEAD):
                pt = h // 8
                pe(lambda e, h=h, pt=pt, src=kf_h(h): e.transpose(
                    PT[:, pt * 1024 + (h % 8) * 128: pt * 1024 + (h % 8) * 128 + 128],
                    src, iden_b[:, :]), reads=[b_hsB], writes=[b_PT[pt]])
                if h % 8 == 7:
                    dve(lambda e, pt=pt, h=h: e.tensor_copy(
                        kftm[:, (h - 7) * 128:(h + 1) * 128], PT[:, pt * 1024: pt * 1024 + 1024]),
                        reads=[b_PT[pt]], writes=[b_kftm_sl[pt]])
            for h in range(NHEAD):
                mm(P[3][:, h:h + 1], qf_h(h), ksumb[:, h:h + 1], start=True, stop=False,
                   reads=[b_hsA, b_ksumb], writes=[b_P[3]])
                mm(P[3][:, h:h + 1], atb[:, h * 128:(h + 1) * 128], colones[:, :],
                   start=False, stop=True, reads=[b_atb], writes=[b_P[3]])
            dve(lambda e: e.tensor_scalar_add(zs[:, :], P[3][:, 0:NHEAD], ATT_EPS),
                reads=[b_P[3]], writes=[b_z])
            dve(lambda e: e.reciprocal(zinv[:, :], zs[:, :]), reads=[b_z], writes=[b_zinv])
            for h in range(NHEAD):
                pb = 4 + (h // 4) % 2
                psl = P[pb][:, (h % 4) * 128:(h % 4) * 128 + 128]
                mm(psl, qf_h(h), kvb[:, h * 128:(h + 1) * 128], start=True, stop=False,
                   reads=[b_hsA, b_kvb], writes=[b_P[pb]])
                mm(psl, atb[:, h * 128:(h + 1) * 128], v_h(h), start=False, stop=True,
                   reads=[b_atb, b_combs[vslot]], writes=[b_P[pb]])
                act(lambda e, psl=psl, h=h: e.activation(
                    attntm[:, h * 128:(h + 1) * 128], psl, AF.Copy, scale=zinv[:, h:h + 1]),
                    reads=[b_P[pb], b_zinv], writes=[b_att_sl[h // 8]])
            for h in range(NHEAD):
                pt = h // 8
                pe(lambda e, h=h, pt=pt: e.transpose(
                    PT[:, pt * 1024 + (h % 8) * 128: pt * 1024 + (h % 8) * 128 + 128],
                    attntm[:, h * 128:(h + 1) * 128], iden_b[:, :]),
                   reads=[b_att_sl[h // 8]], writes=[b_PT[pt]])
                if h % 8 == 7:
                    for hh in range(h - 7, h + 1):
                        dve(lambda e, pt=pt, hh=hh, c=c: e.tensor_copy(
                            ys[:, hh * NT + c * CH: hh * NT + c * CH + 128],
                            PT[:, pt * 1024 + (hh % 8) * 128: pt * 1024 + (hh % 8) * 128 + 128]),
                            reads=[b_PT[pt]], writes=[b_ys])
            for h in range(NHEAD):
                pb = 4 + (h // 4) % 2
                mm(P[pb][:, (h % 4) * 128:(h % 4) * 128 + 128],
                   kftm[:, h * 128:(h + 1) * 128], v_h(h), start=True, stop=True,
                   reads=[b_kftm_sl[h // 8], b_combs[vslot]], writes=[b_P[pb]])
                if h % 4 == 3:
                    dve(lambda e, pb=pb, h=h: e.tensor_add(
                        kvs[:, (h - 3) * 128:(h + 1) * 128],
                        kvs[:, (h - 3) * 128:(h + 1) * 128], P[pb][:, 0:512]),
                        reads=[b_P[pb], b_kvs], writes=[b_kvs])
            act(lambda e: e.activation(kvb[:, :], kvs[:, :], AF.Copy), reads=[b_kvs], writes=[b_kvb])
            kf3 = hs[:, HSB:2 * HSB].rearrange("p (j n) -> p j n", j=NHEAD)[:, :, c * CH:(c + 1) * CH]
            dve(lambda e, kf3=kf3: e.tensor_reduce(stm[:, 0:NHEAD], kf3, AX.X, ALU.add),
                reads=[b_hsB], writes=[b_stm])
            dve(lambda e: e.tensor_add(ksums[:, :], ksums[:, :], stm[:, 0:NHEAD]),
                reads=[b_stm, b_ksums], writes=[b_ksums])
            dve(lambda e: e.tensor_copy(ksumb[:, :], ksums[:, :]), reads=[b_ksums], writes=[b_ksumb])
        for mp in range(KC // 2):
            su, _ = stream_unit(wo3, b_wo3, ("wob", mp), wob[mp],
                                "sq" if mp % 2 == 0 else "aq")
            for m in (2 * mp, 2 * mp + 1):
                po = P[4 + (m % 2)]
                for j in range(KC):
                    mm(po[:, 0:NT], arena_tile(wo3, su, j, (m % 2) * 128, 128),
                       ys[:, j * NT:(j + 1) * NT], start=(j == 0), stop=(j == KC - 1),
                       reads=[b_wo3[su], b_ys], writes=[b_P[4 + (m % 2)]])
                dve(lambda e, m=m, po=po: e.scalar_tensor_tensor(
                    xs[:, m * NT:(m + 1) * NT], po[:, 0:NT], bos[:, m:m + 1],
                    xs[:, m * NT:(m + 1) * NT], ALU.add, ALU.add),
                    reads=[b_P[4 + (m % 2)], b_xs], writes=[b_xs])
        emit_ln(b_xs, xs)
        emit_router()
        emit_ffn(True, t)
        dma("aq", y_out[t], xs[:, :], reads=[b_xs], writes=[Buf()])

    with es:
        blk = es.enter_context(nc.Block())

        def make_runner(names, drain=()):
            names = set(names)
            def run(e):
                last = defaultdict(int)
                for (oe, fn, waits, inc_sem, inc_n) in S.ops:
                    if oe not in names:
                        continue
                    for s_, v_ in waits.items():
                        if oe == "pe" and s_ == "pe":
                            continue
                        if v_ > last[s_]:
                            e.wait_ge(sems[s_], v_)
                            last[s_] = v_
                    ins = fn(e)
                    ins.then_inc(sems[inc_sem], inc_n)
                for d in drain:
                    if S.cnt[d]:
                        e.wait_ge(sems[d], S.cnt[d])
            return run

        blk.tensor(make_runner(("pe",)))
        blk.scalar(make_runner(("act", "aq"), drain=[f"aq{i}" for i in range(NDSEM)]))
        blk.vector(make_runner(("dve",)))
        blk.sync(make_runner(("sq",)))
        blk.gpsimd(make_runner(("gp",), drain=[f"gp{i}" for i in range(NDSEM)]))
    build_nc.last_sched = S
    return nc


def _prep_inputs(cfg: CFG, I):
    f32 = lambda a: np.ascontiguousarray(np.asarray(a), dtype=np.float32)
    d = {}
    sw1 = np.asarray(I["L0_ffn_sw1"], dtype=np.float32)
    sw2 = np.asarray(I["L0_ffn_sw2"], dtype=np.float32)
    sw3 = np.asarray(I["L0_ffn_sw3"], dtype=np.float32)
    w12 = np.asarray(I["L0_ffn_w12"], dtype=np.float32)
    w3 = np.asarray(I["L0_ffn_w3"], dtype=np.float32)
    for nm, w in (("sw1p", sw1), ("sw2p", sw2)):
        d[nm] = f32(w.reshape(32, 256, 16, 128).transpose(0, 3, 2, 1)
                    .reshape(32, 128, 4096))
    d["sw3p"] = f32(sw3.reshape(8, 256, 4, 16, 128).transpose(0, 2, 4, 3, 1)
                    .reshape(8, 4, 128, 4096))
    d["w12p1"] = f32(w12[:, :HR, :].reshape(NE, 4, 256, 16, 128)
                     .transpose(0, 1, 4, 3, 2).reshape(NE, 4, 128, 4096))
    d["w12p2"] = f32(w12[:, HR:, :].reshape(NE, 4, 256, 16, 128)
                     .transpose(0, 1, 4, 3, 2).reshape(NE, 4, 128, 4096))
    d["w3p"] = f32(w3.transpose(0, 2, 1).reshape(8, 8, 128, 8, 256)
                   .transpose(3, 0, 1, 2, 4).reshape(8, 4, 16, 128, 256)
                   .transpose(0, 1, 3, 2, 4).reshape(8, 4, 128, 4096))
    for nm, key in (("wqp", "L0_attn_Wq"), ("wkp", "L0_attn_Wk"),
                    ("wvp", "L0_attn_Wv"), ("wop", "L0_attn_Wo")):
        w = np.asarray(I[key], dtype=np.float32)
        d[nm] = f32(w.reshape(16, 128, 8, 256).transpose(2, 1, 0, 3)
                    .reshape(8, 128, 4096))
    rt = f32(np.asarray(I["L0_ffn_router_w"]).T)
    d["routp"] = f32(rt.reshape(KC, 128, NE).transpose(1, 0, 2))
    d["ebbc"] = f32(np.broadcast_to(np.asarray(I["L0_ffn_expert_bias"]), (128, NE)))
    for nm, key in (("bq", "L0_attn_bq"), ("bk", "L0_attn_bk"),
                    ("bv", "L0_attn_bv"), ("bo", "L0_attn_bo")):
        d[nm] = f32(np.asarray(I[key]).reshape(KC, 128).T)
    import ml_dtypes
    d["iden_b"] = np.eye(128, dtype=ml_dtypes.bfloat16)
    d["iden_f"] = np.eye(128, dtype=np.float32)
    d["ones1"] = np.ones((1, 128), dtype=np.float32)
    d["ones1b"] = np.ones((1, 128), dtype=ml_dtypes.bfloat16)
    d["colones"] = np.ones((128, 1), dtype=ml_dtypes.bfloat16)
    d["triu"] = np.triu(np.ones((128, 128), dtype=np.float32))
    d["triu16"] = np.tile(d["triu"], (1, NHEAD)).astype(ml_dtypes.bfloat16)
    return d


def make_in_maps(cfg: CFG, inputs):
    shared = _prep_inputs(cfg, inputs)
    x = np.asarray(inputs["L0_attn_x"], dtype=np.float32)
    in_maps = []
    for c in range(cfg.NC):
        b = c // cfg.HPS
        h = c % cfg.HPS
        sl = x[b, h * cfg.TCORE:(h + 1) * cfg.TCORE, :]
        xt = np.ascontiguousarray(
            sl.reshape(cfg.NTILES, NT, KC, 128).transpose(0, 3, 2, 1)
            .reshape(cfg.NTILES, 128, KC * NT))
        pm = np.zeros((128, 2), dtype=np.float32)
        if c % cfg.HPS == 1:
            pm[:, 0] = 1.0
        m = dict(shared)
        m["x"] = xt
        m["prevm"] = pm
        in_maps.append(m)
    return in_maps


def assemble_out(cfg: CFG, x_shape, per_core_y):
    out = np.empty(x_shape, dtype=np.float32)
    for c in range(cfg.NC):
        b = c // cfg.HPS
        h = c % cfg.HPS
        yt = np.asarray(per_core_y[c])
        yt = (yt.reshape(cfg.NTILES, 128, KC, NT).transpose(0, 3, 2, 1)
              .reshape(cfg.TCORE, DM))
        out[b, h * cfg.TCORE:(h + 1) * cfg.TCORE, :] = yt
    return out


def run(cfg: CFG, inputs, **kw):
    in_maps = make_in_maps(cfg, inputs)
    nc = build_nc(cfg)
    res = run_bass_kernel_spmd(nc, in_maps, core_ids=list(range(cfg.NC)), **kw)
    x = np.asarray(inputs["L0_attn_x"])
    out = assemble_out(cfg, x.shape, [res.results[c]["y"] for c in range(cfg.NC)])
    return out, res


def kernel(**inputs):
    cfg = CFG(8, 4, 8192)
    out, _ = run(cfg, inputs)
    return out


# version 12
# speedup vs baseline: 1.1103x; 1.0083x over previous
import numpy as np
from collections import defaultdict
from contextlib import ExitStack

import concourse.bass as bass
import concourse.mybir as mybir
from concourse.bass_utils import run_bass_kernel_spmd

F32 = mybir.dt.float32
BF16 = mybir.dt.bfloat16
AF = mybir.ActivationFunctionType
ALU = mybir.AluOpType
AX = mybir.AxisListType

DM = 2048
KC = DM // 128
NHEAD = 16
DH = 128
HSH = 8192
HR = 1024
NE = 8
NT = 512
CH = 128
CPT = NT // CH
LN_EPS = 1e-5
ATT_EPS = 1e-6


class CFG:
    def __init__(self, n_cores, batch, seqlen):
        self.NC = n_cores
        self.B = batch
        self.S = seqlen
        self.TCORE = batch * seqlen // n_cores
        assert self.TCORE % NT == 0
        self.NTILES = self.TCORE // NT
        self.NCHUNK = self.TCORE // CH
        self.HPS = n_cores // batch
        assert seqlen % (self.HPS * CH) == 0


class Buf:
    __slots__ = ("w", "r")
    def __init__(self):
        self.w = {}
        self.r = {}


def _merge(dst, tok):
    for s, v in tok.items():
        if dst.get(s, -1) < v:
            dst[s] = v


_TRACE_SRC = False


class Sched:
    def __init__(self):
        self.ops = []
        self.cnt = defaultdict(int)
        self.src = []

    def op(self, eng, fn, reads=(), writes=(), extra=None, inc=None, inc_n=1):
        if _TRACE_SRC:
            import sys as _s
            f = _s._getframe(1)
            skip = ("op", "opp", "mm", "dma", "act", "dve", "pe", "cvt", "stream_unit")
            helper = None
            while f.f_code.co_name in skip and f.f_back is not None:
                helper = f.f_code.co_name
                f = f.f_back
            self.src.append((helper, f.f_code.co_name, f.f_lineno))
        waits = {}
        if extra:
            _merge(waits, extra)
        for b in reads:
            _merge(waits, b.w)
        for b in writes:
            _merge(waits, b.w)
            _merge(waits, b.r)
        sem = inc if inc is not None else eng
        self.cnt[sem] += inc_n
        tok = {sem: self.cnt[sem]}
        self.ops.append((eng, fn, waits, sem, inc_n))
        for b in reads:
            _merge(b.r, tok)
        for b in writes:
            b.w = dict(tok)
            b.r = {}
        return tok


def build_nc(cfg: CFG):
    nc = bass.Bass("TRN2", target_bir_lowering=False, debug=False)
    T = cfg.NTILES

    def ein(name, shape, dt=F32):
        return nc.dram_tensor(name, shape, dt, kind="ExternalInput").ap()

    U = KC * 256
    x_in = ein("x", [T, 128, KC * NT])
    sw1p = ein("sw1p", [HSH // 256, 128, U])
    sw2p = ein("sw2p", [HSH // 256, 128, U])
    sw3p = ein("sw3p", [KC // 2, 4, 128, U])
    w12p1 = ein("w12p1", [NE, HR // 256, 128, U])
    w12p2 = ein("w12p2", [NE, HR // 256, 128, U])
    w3p = ein("w3p", [KC // 2, 4, 128, U])
    wqp = ein("wqp", [KC // 2, 128, U])
    wkp = ein("wkp", [KC // 2, 128, U])
    wvp = ein("wvp", [KC // 2, 128, U])
    wop = ein("wop", [KC // 2, 128, U])
    routp = ein("routp", [128, KC, NE])
    ebbc_in = ein("ebbc", [128, NE])
    bq_in = ein("bq", [128, KC])
    bk_in = ein("bk", [128, KC])
    bv_in = ein("bv", [128, KC])
    bo_in = ein("bo", [128, KC])
    iden_b_in = ein("iden_b", [128, 128], BF16)
    iden_f_in = ein("iden_f", [128, 128])
    ones1_in = ein("ones1", [1, 128])
    ones1b_in = ein("ones1b", [1, 128], BF16)
    colones_in = ein("colones", [128, 1], BF16)
    triu_in = ein("triu", [128, 128])
    triu16_in = ein("triu16", [128, NHEAD * CH], BF16)
    prevm_in = ein("prevm", [128, 2])

    y_out = nc.dram_tensor("y", [T, 128, KC * NT], F32, kind="ExternalOutput").ap()

    def idram(name, shape, dt):
        return nc.dram_tensor(name, shape, dt).ap()

    sw1b = idram("sw1b", [HSH // 256, 128, U], BF16)
    sw2b = idram("sw2b", [HSH // 256, 128, U], BF16)
    sw3b = idram("sw3b", [KC // 2, 4, 128, U], BF16)
    w12b1 = idram("w12b1", [NE, HR // 256, 128, U], BF16)
    w12b2 = idram("w12b2", [NE, HR // 256, 128, U], BF16)
    w3b = idram("w3b", [KC // 2, 4, 128, U], BF16)
    wqb = idram("wqb", [KC // 2, 128, U], BF16)
    wkb = idram("wkb", [KC // 2, 128, U], BF16)
    wvb = idram("wvb", [KC // 2, 128, U], BF16)
    wob = idram("wob", [KC // 2, 128, U], BF16)

    x1_d = idram("x1_d", [T, 128, KC * NT], F32)
    qf_d = idram("qf_d", [T, 128, KC * NT], BF16)
    kf_d = idram("kf_d", [T, 128, KC * NT], BF16)
    v_d = idram("v_d", [cfg.NCHUNK, CH, DM], BF16)
    cc_in = idram("cc_in", [128, NHEAD * DH + NHEAD], F32)
    cc_out = idram("cc_out", [2 * 128, NHEAD * DH + NHEAD], F32)

    es = ExitStack()
    sb = lambda n, c, dt: es.enter_context(nc.sbuf_tensor(n, [128, c], dt))

    xs = sb("xs", KC * NT, F32)
    ys = sb("ys", KC * NT, BF16)
    hs = sb("hs", 2 * KC * NT, BF16)
    accs = sb("accs", 4 * NT, F32)
    combs = sb("combs", NE * NT, BF16)
    vfm = sb("vfm", NHEAD * CH, BF16)
    atb = vfm
    wg = sb("wg", 3 * KC * 256, BF16)
    wg2 = sb("wg2", 3 * KC * 256, BF16)
    wo3 = sb("wo3", 3 * KC * 256, BF16)
    kftm = sb("kftm", NHEAD * CH, BF16)
    attntm = sb("attntm", NHEAD * CH, BF16)
    kvs = sb("kvs", NHEAD * DH, F32)
    kvb = sb("kvb", NHEAD * DH, BF16)
    ksums = sb("ksums", NHEAD, F32)
    ksumb = sb("ksumb", NHEAD, BF16)
    routs = sb("routs", 2 * KC * NE, BF16)
    bqs = sb("bqs", KC, F32)
    bks = sb("bks", KC, F32)
    bvs = sb("bvs", KC, F32)
    bos = sb("bos", KC, F32)
    ebbc = sb("ebbcS", NE, F32)
    iden_b = sb("iden_bS", 128, BF16)
    iden_f = sb("iden_fS", 128, F32)
    colones = sb("colonesS", 1, BF16)
    triu16 = sb("triu16S", NHEAD * CH, BF16)
    prevm = sb("prevmS", 2, F32)
    stm = sb("stm", 64, F32)
    zs = sb("zs", NHEAD, F32)
    zinv = sb("zinv", NHEAD, F32)
    ones1 = es.enter_context(nc.sbuf_tensor("ones1S", [1, 128], F32))
    ones1b = es.enter_context(nc.sbuf_tensor("ones1bS", [1, 128], BF16))
    rowb = es.enter_context(nc.sbuf_tensor("rowb", [1, 2 * NT], F32))
    scor = es.enter_context(nc.sbuf_tensor("scor", [NE, NT], F32))
    combr = es.enter_context(nc.sbuf_tensor("combr", [1, NE * CH], BF16))
    attntm_silu = attntm
    silu2 = kftm

    P = [es.enter_context(nc.psum_tensor(f"P{i}", [128, 512], F32)) for i in range(6)]
    PT = es.enter_context(nc.psum_tensor("PT", [128, 2048], BF16))

    NDSEM = 6
    sem_names = (["pe", "act", "dve", "coll"]
                 + [f"sq{i}" for i in range(NDSEM)]
                 + [f"gp{i}" for i in range(NDSEM)]
                 + [f"aq{i}" for i in range(NDSEM)])
    sems = {n: es.enter_context(nc.semaphore(f"s_{n}")) for n in sem_names}

    S = Sched()
    dma_ring = {"sq": 0, "gp": 0, "aq": 0}

    b_xs, b_ys, b_hsA, b_hsB, b_accs, b_vfm = (Buf() for _ in range(6))
    b_atb = b_vfm
    b_combs = [Buf(), Buf()]
    b_wg = [Buf(), Buf(), Buf()]
    b_wg2 = [Buf(), Buf(), Buf()]
    b_wo3 = [Buf(), Buf(), Buf()]
    b_att_sl = [Buf(), Buf()]
    b_kftm_sl = [Buf(), Buf()]
    b_kvs, b_kvb, b_ksums, b_ksumb = Buf(), Buf(), Buf(), Buf()
    b_stm, b_row, b_scor, b_combr, b_z, b_zinv = (Buf() for _ in range(6))
    b_routs, b_rtmp = Buf(), Buf()
    b_P = [Buf() for _ in range(6)]
    b_PT = [Buf(), Buf()]
    b_x1d = [Buf() for _ in range(T)]
    b_qfd = [Buf() for _ in range(T)]
    b_kfd = [Buf() for _ in range(T)]
    b_vd = [Buf() for _ in range(cfg.NCHUNK)]
    b_ccin, b_ccout = Buf(), Buf()
    b_wcvt = defaultdict(Buf)

    HSB = KC * NT

    def opp(eng, fn, reads=(), writes=(), extra=None, inc=None, inc_n=1):
        return S.op(eng, fn, reads=reads, writes=writes, extra=extra, inc=inc, inc_n=inc_n)

    def dma(queue, out_ap, in_ap, reads=(), writes=(), extra=None):
        i = dma_ring[queue]
        dma_ring[queue] = (i + 1) % NDSEM
        sem = f"{queue}{i}"
        slot_free = {sem: S.cnt[sem]}
        if extra:
            slot_free = dict(slot_free)
            _merge(slot_free, extra)
        return opp(queue, lambda e: e.dma_start(out=out_ap, in_=in_ap),
                   reads=reads, writes=writes, extra=slot_free, inc=sem, inc_n=16)

    def mm(out_ap, lhsT_ap, rhs_ap, start, stop, reads=(), writes=(), extra=None, sgc=False):
        return opp("pe", lambda e, o=out_ap, l=lhsT_ap, r=rhs_ap, st=start, sp=stop, g=sgc:
                   e.matmul(o, l, r, start=st, stop=sp, skip_group_check=g),
                   reads=reads, writes=writes, extra=extra)

    def act(fn, reads=(), writes=(), extra=None):
        return opp("act", fn, reads=reads, writes=writes, extra=extra)

    def dve(fn, reads=(), writes=(), extra=None):
        return opp("dve", fn, reads=reads, writes=writes, extra=extra)

    def pe(fn, reads=(), writes=(), extra=None):
        return opp("pe", fn, reads=reads, writes=writes, extra=extra)

    tok_consts = {}
    for dst, srcap in ((bqs, bq_in), (bks, bk_in), (bvs, bv_in), (bos, bo_in),
                       (ebbc, ebbc_in), (iden_b, iden_b_in), (iden_f, iden_f_in),
                       (ones1, ones1_in), (ones1b, ones1b_in), (colones, colones_in),
                       (triu16, triu16_in), (prevm, prevm_in)):
        _merge(tok_consts, dma("sq", dst[:, :], srcap[:, :], writes=[Buf()]))

    RW = KC * NE
    dma("sq", accs[:, 0:RW], routp.rearrange("p k e -> p (k e)"), writes=[b_accs])
    dve(lambda e: e.tensor_copy(routs[:, 0:RW], accs[:, 0:RW]),
        reads=[b_accs], writes=[b_routs])
    dve(lambda e: e.tensor_copy(accs[:, RW:2 * RW], routs[:, 0:RW]),
        reads=[b_routs, b_accs], writes=[b_accs])
    dve(lambda e: e.tensor_sub(accs[:, RW:2 * RW], accs[:, 0:RW], accs[:, RW:2 * RW]),
        reads=[b_accs], writes=[b_accs])
    dve(lambda e: e.tensor_copy(routs[:, RW:2 * RW], accs[:, RW:2 * RW]),
        reads=[b_accs], writes=[b_routs])

    def cvt(key, out_ap, in_ap):
        return dma("gp", out_ap, in_ap, writes=[b_wcvt[key]])

    def cvt2(k1, k2, out_ap, in_ap):
        return dma("gp", out_ap, in_ap, writes=[b_wcvt[k1], b_wcvt[k2]])

    for blk in range(0, 16, 2):
        cvt2(("sw1b", blk), ("sw1b", blk + 1), sw1b[blk:blk + 2], sw1p[blk:blk + 2])
        cvt2(("sw2b", blk), ("sw2b", blk + 1), sw2b[blk:blk + 2], sw2p[blk:blk + 2])
    for mp in range(KC // 2):
        cvt2(("sw3b", mp, 0), ("sw3b", mp, 1), sw3b[mp, 0:2], sw3p[mp, 0:2])
    for blk in range(16, 32, 2):
        cvt2(("sw1b", blk), ("sw1b", blk + 1), sw1b[blk:blk + 2], sw1p[blk:blk + 2])
        cvt2(("sw2b", blk), ("sw2b", blk + 1), sw2b[blk:blk + 2], sw2p[blk:blk + 2])
    for mp in range(KC // 2):
        cvt2(("sw3b", mp, 2), ("sw3b", mp, 3), sw3b[mp, 2:4], sw3p[mp, 2:4])
    for e_ in range(4):
        for jp in range(0, HR // 256, 2):
            cvt2(("w12b1", e_, jp), ("w12b1", e_, jp + 1), w12b1[e_, jp:jp + 2], w12p1[e_, jp:jp + 2])
            cvt2(("w12b2", e_, jp), ("w12b2", e_, jp + 1), w12b2[e_, jp:jp + 2], w12p2[e_, jp:jp + 2])
    for mp in range(KC // 2):
        cvt2(("w3b", mp, 0), ("w3b", mp, 1), w3b[mp, 0:2], w3p[mp, 0:2])
    for e_ in range(4, NE):
        for jp in range(0, HR // 256, 2):
            cvt2(("w12b1", e_, jp), ("w12b1", e_, jp + 1), w12b1[e_, jp:jp + 2], w12p1[e_, jp:jp + 2])
            cvt2(("w12b2", e_, jp), ("w12b2", e_, jp + 1), w12b2[e_, jp:jp + 2], w12p2[e_, jp:jp + 2])
    for mp in range(KC // 2):
        cvt2(("w3b", mp, 2), ("w3b", mp, 3), w3b[mp, 2:4], w3p[mp, 2:4])
    for nm, dst, src in (("wqb", wqb, wqp), ("wkb", wkb, wkp), ("wvb", wvb, wvp)):
        for jp in range(0, KC // 2, 2):
            cvt2((nm, jp), (nm, jp + 1), dst[jp:jp + 2], src[jp:jp + 2])
    for mp in range(0, KC // 2, 2):
        cvt2(("wob", mp), ("wob", mp + 1), wob[mp:mp + 2], wop[mp:mp + 2])

    def arena_tile(arena, slot, kc, c0, w):
        off = slot * KC * 256 + kc * 256 + c0
        return arena[:, off:off + w]

    slot_state = {}

    def stream_unit(arena, b_arena, key, dram_slice, queue):
        k_ = id(arena)
        s_ = slot_state.get(k_, 0)
        slot_state[k_] = s_ + 1
        slot = s_ % len(b_arena)
        tok = dma(queue, arena[:, slot * U: (slot + 1) * U], dram_slice,
                  reads=[b_wcvt[key]], writes=[b_arena[slot]])
        return slot, tok

    def emit_ln(src_buf, src_sb):
        dve(lambda e: e.tensor_copy(hs[:, 0:HSB], src_sb[:, :]),
            reads=[src_buf], writes=[b_hsA])
        act(lambda e: e.activation(hs[:, HSB:2 * HSB], src_sb[:, :], AF.Square),
            reads=[src_buf], writes=[b_hsB])
        for k in range(KC):
            mm(P[3][0:1, 0:NT], colones[:, :], hs[:, k * NT:(k + 1) * NT],
               start=(k == 0), stop=(k == KC - 1), reads=[b_hsA], writes=[b_P[3]],
               extra=tok_consts)
        for k in range(KC):
            mm(P[4][0:1, 0:NT], colones[:, :], hs[:, HSB + k * NT:HSB + (k + 1) * NT],
               start=(k == 0), stop=(k == KC - 1), reads=[b_hsB], writes=[b_P[4]])
        r = rowb
        dve(lambda e: e.tensor_scalar_mul(r[:, 0:NT], P[3][0:1, 0:NT], 1.0 / DM),
            reads=[b_P[3]], writes=[b_row])
        dve(lambda e: e.tensor_scalar_mul(r[:, NT:2 * NT], P[4][0:1, 0:NT], 1.0 / DM),
            reads=[b_P[4]], writes=[b_row])
        dve(lambda e: e.tensor_mul(scor[0:1, 0:NT], r[:, 0:NT], r[:, 0:NT]),
            reads=[b_row], writes=[b_scor])
        dve(lambda e: e.tensor_sub(r[:, NT:2 * NT], r[:, NT:2 * NT], scor[0:1, 0:NT]),
            reads=[b_row, b_scor], writes=[b_row])
        dve(lambda e: e.tensor_scalar_add(r[:, NT:2 * NT], r[:, NT:2 * NT], LN_EPS),
            reads=[b_row], writes=[b_row])
        act(lambda e: e.activation(r[:, NT:2 * NT], r[:, NT:2 * NT], AF.Ln),
            reads=[b_row], writes=[b_row])
        dve(lambda e: e.tensor_scalar_mul(r[:, NT:2 * NT], r[:, NT:2 * NT], -0.5),
            reads=[b_row], writes=[b_row])
        act(lambda e: e.activation(r[:, NT:2 * NT], r[:, NT:2 * NT], AF.Exp),
            reads=[b_row], writes=[b_row])
        mm(P[4][:, 0:NT], ones1[:, :], r[:, 0:NT], start=True, stop=True,
           reads=[b_row], writes=[b_P[4]], extra=tok_consts)
        mm(P[5][:, 0:NT], ones1[:, :], r[:, NT:2 * NT], start=True, stop=True,
           reads=[b_row], writes=[b_P[5]])
        for g in range(4):
            sl = slice(g * 4 * NT, (g + 1) * 4 * NT)
            x4 = src_sb[:, sl].rearrange("p (k n) -> p k n", k=4)
            e4 = accs[:, 0:4 * NT].rearrange("p (k n) -> p k n", k=4)
            y4 = ys[:, sl].rearrange("p (k n) -> p k n", k=4)
            m_bc = P[4][:, None, 0:NT].broadcast_to([128, 4, NT])
            a_bc = P[5][:, None, 0:NT].broadcast_to([128, 4, NT])
            dve(lambda e, x4=x4, e4=e4, m_bc=m_bc: e.tensor_tensor(
                e4, x4, m_bc, ALU.subtract),
                reads=[src_buf, b_P[4]], writes=[b_accs])
            dve(lambda e, e4=e4, y4=y4, a_bc=a_bc: e.tensor_tensor(
                y4, e4, a_bc, ALU.mult),
                reads=[b_accs, b_P[5]], writes=[b_ys])

    def emit_router():
        for k in range(KC):
            mm(P[0][0:NE, 0:NT], routs[:, k * NE:(k + 1) * NE], ys[:, k * NT:(k + 1) * NT],
               start=(k == 0), stop=False, reads=[b_routs, b_ys], writes=[b_P[0]])
        for k in range(KC):
            mm(P[0][0:NE, 0:NT], routs[:, KC * NE + k * NE:KC * NE + (k + 1) * NE],
               ys[:, k * NT:(k + 1) * NT],
               start=False, stop=(k == KC - 1), reads=[b_routs, b_ys], writes=[b_P[0]])
        dve(lambda e: e.tensor_scalar_mul(scor[:, :], P[0][0:NE, 0:NT], 0.5),
            reads=[b_P[0]], writes=[b_scor])
        act(lambda e: e.activation(scor[:, :], scor[:, :], AF.Tanh),
            reads=[b_scor], writes=[b_scor])
        dve(lambda e: e.tensor_scalar(scor[:, :], scor[:, :], 0.5, 0.5, ALU.mult, ALU.add),
            reads=[b_scor], writes=[b_scor])
        for c in range(CPT):
            pe(lambda e, c=c: e.transpose(P[1][:, 0:NE], scor[:, c * CH:(c + 1) * CH],
                                          iden_f[0:NE, 0:NE]),
               reads=[b_scor], writes=[b_P[1]], extra=tok_consts)
            t = stm
            dve(lambda e: e.tensor_copy(t[:, 0:8], P[1][:, 0:NE]),
                reads=[b_P[1]], writes=[b_stm])
            dve(lambda e: e.tensor_add(t[:, 8:16], t[:, 0:8], ebbc[:, :]),
                reads=[b_stm], writes=[b_stm], extra=tok_consts)
            dve(lambda e: e.tensor_reduce(t[:, 16:17], t[:, 8:16], AX.X, ALU.max),
                reads=[b_stm], writes=[b_stm])
            dve(lambda e: e.tensor_scalar(t[:, 24:32], t[:, 8:16], t[:, 16:17], None, ALU.is_ge),
                reads=[b_stm], writes=[b_stm])
            dve(lambda e: e.scalar_tensor_tensor(t[:, 32:40], t[:, 24:32], -1e9, t[:, 8:16],
                                                 ALU.mult, ALU.add),
                reads=[b_stm], writes=[b_stm])
            dve(lambda e: e.tensor_reduce(t[:, 17:18], t[:, 32:40], AX.X, ALU.max),
                reads=[b_stm], writes=[b_stm])
            dve(lambda e: e.tensor_scalar(t[:, 40:48], t[:, 32:40], t[:, 17:18], None, ALU.is_ge),
                reads=[b_stm], writes=[b_stm])
            dve(lambda e: e.tensor_add(t[:, 40:48], t[:, 40:48], t[:, 24:32]),
                reads=[b_stm], writes=[b_stm])
            dve(lambda e: e.tensor_mul(t[:, 48:56], t[:, 0:8], t[:, 40:48]),
                reads=[b_stm], writes=[b_stm])
            dve(lambda e: e.tensor_reduce(t[:, 18:19], t[:, 48:56], AX.X, ALU.add),
                reads=[b_stm], writes=[b_stm])
            dve(lambda e: e.tensor_scalar_add(t[:, 18:19], t[:, 18:19], 1e-9),
                reads=[b_stm], writes=[b_stm])
            dve(lambda e: e.reciprocal(t[:, 19:20], t[:, 18:19]),
                reads=[b_stm], writes=[b_stm])
            dve(lambda e: e.tensor_scalar(t[:, 56:64], t[:, 48:56], t[:, 19:20], None, ALU.mult),
                reads=[b_stm], writes=[b_stm])
            for e_ in range(NE):
                pb = 1 + (e_ % 2)
                pe(lambda e, e_=e_, pb=pb: e.transpose(
                    P[pb][0:1, 0:CH], stm[:, 56 + e_:57 + e_], iden_f[:, :]),
                   reads=[b_stm], writes=[b_P[pb]])
                dve(lambda e, e_=e_, pb=pb: e.tensor_copy(
                    combr[0:1, e_ * CH:(e_ + 1) * CH], P[pb][0:1, 0:CH]),
                    reads=[b_P[pb]], writes=[b_combr])
            for e_ in range(NE):
                pb = 1 + (e_ % 2)
                mm(P[pb][:, 0:CH], ones1b[:, :], combr[0:1, e_ * CH:(e_ + 1) * CH],
                   start=True, stop=True, reads=[b_combr], writes=[b_P[pb]])
                act(lambda e, e_=e_, pb=pb, c=c: e.activation(
                    combs[:, e_ * NT + c * CH: e_ * NT + (c + 1) * CH],
                    P[pb][:, 0:CH], AF.Copy),
                    reads=[b_P[pb]], writes=[b_combs[0], b_combs[1]])

    def emit_ffn(dest_is_out, tile):
        for half in range(2):
            for i in range(HSH // 2 // 128):
                hc = half * 32 + i
                blk = hc // 2
                if hc % 2 == 0:
                    s1, tok1 = stream_unit(wg, b_wg, ("sw1b", blk), sw1b[blk], "sq")
                    s2, tok2 = stream_unit(wg2, b_wg2, ("sw2b", blk), sw2b[blk], "aq")
                    emit_ffn.cur = (s1, s2)
                s1, s2 = emit_ffn.cur
                pg = P[2 * (i % 2)]
                pu = P[2 * (i % 2) + 1]
                for k in range(KC):
                    mm(pg[:, 0:NT], arena_tile(wg, s1, k, (hc % 2) * 128, 128),
                       ys[:, k * NT:(k + 1) * NT], start=(k == 0), stop=(k == KC - 1),
                       reads=[b_wg[s1], b_ys], writes=[b_P[2 * (i % 2)]])
                for k in range(KC):
                    mm(pu[:, 0:NT], arena_tile(wg2, s2, k, (hc % 2) * 128, 128),
                       ys[:, k * NT:(k + 1) * NT], start=(k == 0), stop=(k == KC - 1),
                       reads=[b_wg2[s2], b_ys], writes=[b_P[2 * (i % 2) + 1]])
                hreg = b_hsA if i < 16 else b_hsB
                hview = hs[:, i * NT: (i + 1) * NT] if i < 16 else \
                    hs[:, HSB + (i - 16) * NT: HSB + (i - 15) * NT]
                act(lambda e, pg=pg, i=i: e.activation(
                    attntm_silu[:, (i % 2) * 1024:(i % 2) * 1024 + NT],
                    pg[:, 0:NT], AF.Silu),
                    reads=[b_P[2 * (i % 2)]], writes=[b_att_sl[i % 2]])
                dve(lambda e, pu=pu, hv=hview, i=i: e.tensor_mul(
                    hv, attntm_silu[:, (i % 2) * 1024:(i % 2) * 1024 + NT], pu[:, 0:NT]),
                    reads=[b_att_sl[i % 2], b_P[2 * (i % 2) + 1]], writes=[hreg])
            for mp in range(KC // 2):
                sus = []
                for uu in range(2):
                    su, _ = stream_unit(wo3, b_wo3, ("sw3b", mp, half * 2 + uu),
                                        sw3b[mp, half * 2 + uu], "sq" if uu == 0 else "aq")
                    sus.append(su)
                for m in (2 * mp, 2 * mp + 1):
                    po = P[4 + (m % 2)]
                    for kk in range(32):
                        hv = hs[:, kk * NT:(kk + 1) * NT] if kk < 16 else \
                            hs[:, HSB + (kk - 16) * NT:HSB + (kk - 15) * NT]
                        hreg = b_hsA if kk < 16 else b_hsB
                        su = sus[kk // 16]
                        mm(po[:, 0:NT], arena_tile(wo3, su, kk % 16, (m % 2) * 128, 128),
                           hv, start=(kk == 0), stop=(kk == 31),
                           reads=[b_wo3[su], hreg], writes=[b_P[4 + (m % 2)]])
                    dve(lambda e, m=m, po=po: e.scalar_tensor_tensor(
                        xs[:, m * NT:(m + 1) * NT], po[:, 0:NT], 0.5,
                        xs[:, m * NT:(m + 1) * NT], ALU.mult, ALU.add),
                        reads=[b_P[4 + (m % 2)], b_xs], writes=[b_xs])
        for half in range(2):
            for ei in range(4):
                e_ = half * 4 + ei
                for jp in range(HR // 256):
                    s1, _ = stream_unit(wg, b_wg, ("w12b1", e_, jp), w12b1[e_, jp], "sq")
                    s2, _ = stream_unit(wg2, b_wg2, ("w12b2", e_, jp), w12b2[e_, jp], "aq")
                    for jj in range(2):
                        j2 = jp * 2 + jj
                        ph1 = P[2 * (j2 % 2)]
                        ph2 = P[2 * (j2 % 2) + 1]
                        for k in range(KC):
                            mm(ph1[:, 0:NT], arena_tile(wg, s1, k, jj * 128, 128),
                               ys[:, k * NT:(k + 1) * NT], start=(k == 0), stop=(k == KC - 1),
                               reads=[b_wg[s1], b_ys], writes=[b_P[2 * (j2 % 2)]])
                        for k in range(KC):
                            mm(ph2[:, 0:NT], arena_tile(wg2, s2, k, jj * 128, 128),
                               ys[:, k * NT:(k + 1) * NT], start=(k == 0), stop=(k == KC - 1),
                               reads=[b_wg2[s2], b_ys], writes=[b_P[2 * (j2 % 2) + 1]])
                        kk = ei * 8 + j2
                        hv = hs[:, kk * NT:(kk + 1) * NT] if kk < 16 else \
                            hs[:, HSB + (kk - 16) * NT:HSB + (kk - 15) * NT]
                        hreg = b_hsA if kk < 16 else b_hsB
                        act(lambda e, ph1=ph1, j2=j2: e.activation(
                            attntm_silu[:, (j2 % 2) * 1024:(j2 % 2) * 1024 + NT],
                            ph1[:, 0:NT], AF.Silu),
                            reads=[b_P[2 * (j2 % 2)]], writes=[b_att_sl[j2 % 2]])
                        dve(lambda e, ph2=ph2, j2=j2: e.tensor_mul(
                            silu2[:, (j2 % 2) * 1024:(j2 % 2) * 1024 + NT],
                            attntm_silu[:, (j2 % 2) * 1024:(j2 % 2) * 1024 + NT], ph2[:, 0:NT]),
                            reads=[b_att_sl[j2 % 2], b_P[2 * (j2 % 2) + 1]], writes=[b_kftm_sl[j2 % 2]])
                        dve(lambda e, hv=hv, j2=j2, e_=e_: e.tensor_mul(
                            hv, silu2[:, (j2 % 2) * 1024:(j2 % 2) * 1024 + NT],
                            combs[:, e_ * NT:(e_ + 1) * NT]),
                            reads=[b_kftm_sl[j2 % 2], b_combs[0], b_combs[1]], writes=[hreg])
            for mp in range(KC // 2):
                sus = []
                for uu in range(2):
                    su, _ = stream_unit(wo3, b_wo3, ("w3b", mp, half * 2 + uu),
                                        w3b[mp, half * 2 + uu], "sq" if uu == 0 else "aq")
                    sus.append(su)
                for m in (2 * mp, 2 * mp + 1):
                    po = P[4 + (m % 2)]
                    for kk in range(32):
                        hv = hs[:, kk * NT:(kk + 1) * NT] if kk < 16 else \
                            hs[:, HSB + (kk - 16) * NT:HSB + (kk - 15) * NT]
                        hreg = b_hsA if kk < 16 else b_hsB
                        su = sus[kk // 16]
                        mm(po[:, 0:NT], arena_tile(wo3, su, kk % 16, (m % 2) * 128, 128),
                           hv, start=(kk == 0), stop=(kk == 31),
                           reads=[b_wo3[su], hreg], writes=[b_P[4 + (m % 2)]])
                    dve(lambda e, m=m, po=po: e.scalar_tensor_tensor(
                        xs[:, m * NT:(m + 1) * NT], po[:, 0:NT], 0.5,
                        xs[:, m * NT:(m + 1) * NT], ALU.mult, ALU.add),
                        reads=[b_P[4 + (m % 2)], b_xs], writes=[b_xs])


    def emit_eluplus1(psum, b_psum, bias_col, dst_view, dst_buf):
        a = accs
        act(lambda e: e.activation(a[:, 0:NT], psum[:, 0:NT], AF.Relu, bias=bias_col),
            reads=[b_psum], writes=[b_accs], extra=tok_consts)
        act(lambda e: e.activation(a[:, NT:2 * NT], psum[:, 0:NT], AF.Identity, bias=bias_col),
            reads=[b_psum], writes=[b_accs])
        dve(lambda e: e.tensor_sub(a[:, NT:2 * NT], a[:, NT:2 * NT], a[:, 0:NT]),
            reads=[b_accs], writes=[b_accs])
        act(lambda e: e.activation(a[:, 2 * NT:3 * NT], a[:, NT:2 * NT], AF.Exp),
            reads=[b_accs], writes=[b_accs])
        dve(lambda e: e.tensor_add(dst_view, a[:, 0:NT], a[:, 2 * NT:3 * NT]),
            reads=[b_accs], writes=[dst_buf])

    for t in range(T):
        dma("sq", xs[:, :], x_in[t], writes=[b_xs])
        emit_ln(b_xs, xs)
        emit_router()
        emit_ffn(False, t)
        dma("gp", x1_d[t], xs[:, :], reads=[b_xs], writes=[b_x1d[t]])
        emit_ln(b_xs, xs)

        for which, wsrc, nmb, bias_sb, dstoff, dstbuf in (
                (0, wqb, "wqb", bqs, 0, b_hsA), (1, wkb, "wkb", bks, HSB, b_hsB)):
            for jp in range(KC // 2):
                s1, _ = stream_unit(wg, b_wg, (nmb, jp), wsrc[jp], "sq")
                for jj in range(2):
                    j = jp * 2 + jj
                    pq = P[2 + (j % 2)]
                    for k in range(KC):
                        mm(pq[:, 0:NT], arena_tile(wg, s1, k, jj * 128, 128),
                           ys[:, k * NT:(k + 1) * NT], start=(k == 0), stop=(k == KC - 1),
                           reads=[b_wg[s1], b_ys], writes=[b_P[2 + (j % 2)]])
                    emit_eluplus1(pq, b_P[2 + (j % 2)], bias_sb[:, j:j + 1],
                                  hs[:, dstoff + j * NT:dstoff + (j + 1) * NT], dstbuf)
        dma("gp", qf_d[t], hs[:, 0:HSB], reads=[b_hsA], writes=[b_qfd[t]])
        dma("gp", kf_d[t], hs[:, HSB:2 * HSB], reads=[b_hsB], writes=[b_kfd[t]])

        for cp in range(CPT // 2):
            for vh in range(2):
                for jp in range(4):
                    gj = vh * 4 + jp
                    s1, _ = stream_unit(wg2, b_wg2, ("wvb", gj), wvb[gj], "sq")
                    for jj in range(2):
                        j = gj * 2 + jj
                        lj = jp * 2 + jj
                        pv = P[2 + (j % 2)]
                        for k in range(KC):
                            mm(pv[:, 0:256], arena_tile(wg2, s1, k, jj * 128, 128),
                               ys[:, k * NT + cp * 256: k * NT + cp * 256 + 256],
                               start=(k == 0), stop=(k == KC - 1),
                               reads=[b_wg2[s1], b_ys], writes=[b_P[2 + (j % 2)]])
                        act(lambda e, pv=pv, j=j, lj=lj: e.activation(
                            vfm[:, lj * 256:(lj + 1) * 256], pv[:, 0:256], AF.Identity,
                            bias=bvs[:, j:j + 1]),
                            reads=[b_P[2 + (j % 2)]], writes=[b_vfm], extra=tok_consts)
                for ci in range(2):
                    c = cp * 2 + ci
                    vslot = c % 2
                    for lj in range(8):
                        pe(lambda e, lj=lj, ci=ci: e.transpose(
                            PT[:, ci * 1024 + lj * 128: ci * 1024 + (lj + 1) * 128],
                            vfm[:, lj * 256 + ci * 128: lj * 256 + ci * 128 + 128],
                            iden_b[:, :]),
                           reads=[b_vfm], writes=[b_PT[ci]])
                    dve(lambda e, ci=ci, vslot=vslot, vh=vh: e.tensor_copy(
                        combs[:, vslot * 2048 + vh * 1024: vslot * 2048 + (vh + 1) * 1024],
                        PT[:, ci * 1024: (ci + 1) * 1024]),
                        reads=[b_PT[ci]], writes=[b_combs[vslot]])
            for ci in range(2):
                c = cp * 2 + ci
                g = t * CPT + c
                vslot = c % 2
                dma("gp", v_d[g][:, :], combs[:, vslot * 2048: vslot * 2048 + DM],
                    reads=[b_combs[vslot]], writes=[b_vd[g]])
                for h in range(NHEAD):
                    pt = h // 8
                    pe(lambda e, h=h, c=c, pt=pt: e.transpose(
                        PT[:, pt * 1024 + (h % 8) * 128: pt * 1024 + (h % 8) * 128 + 128],
                        hs[:, HSB + h * NT + c * CH: HSB + h * NT + c * CH + 128],
                        iden_b[:, :]),
                       reads=[b_hsB], writes=[b_PT[pt]])
                    if h % 8 == 7:
                        act(lambda e, pt=pt, h=h: e.activation(
                            kftm[:, (h - 7) * 128:(h + 1) * 128],
                            PT[:, pt * 1024: pt * 1024 + 1024], AF.Copy),
                            reads=[b_PT[pt]], writes=[b_kftm_sl[pt]])
                for h in range(NHEAD):
                    pb = (0, 1, 4, 5)[h // 4]
                    mm(P[pb][:, (h % 4) * 128:(h % 4) * 128 + 128],
                       kftm[:, h * 128:(h + 1) * 128],
                       combs[:, vslot * 2048 + h * 128: vslot * 2048 + (h + 1) * 128],
                       start=True, stop=True,
                       reads=[b_kftm_sl[h // 8], b_combs[vslot]], writes=[b_P[pb]])
                for g4 in range(4):
                    pb = (0, 1, 4, 5)[g4]
                    if t == 0 and c == 0:
                        dve(lambda e, pb=pb, g4=g4: e.tensor_copy(
                            kvs[:, g4 * 512:(g4 + 1) * 512], P[pb][:, 0:512]),
                            reads=[b_P[pb]], writes=[b_kvs])
                    else:
                        dve(lambda e, pb=pb, g4=g4: e.tensor_add(
                            kvs[:, g4 * 512:(g4 + 1) * 512],
                            kvs[:, g4 * 512:(g4 + 1) * 512], P[pb][:, 0:512]),
                            reads=[b_P[pb], b_kvs], writes=[b_kvs])
                if c == CPT - 1:
                    kf3 = hs[:, HSB:2 * HSB].rearrange("p (j n) -> p j n", j=NHEAD)
                    if t == 0:
                        dve(lambda e, kf3=kf3: e.tensor_reduce(ksums[:, :], kf3, AX.X, ALU.add),
                            reads=[b_hsB], writes=[b_ksums])
                    else:
                        dve(lambda e, kf3=kf3: e.tensor_reduce(stm[:, 0:NHEAD], kf3, AX.X, ALU.add),
                            reads=[b_hsB], writes=[b_stm])
                        dve(lambda e: e.tensor_add(ksums[:, :], ksums[:, :], stm[:, 0:NHEAD]),
                            reads=[b_stm, b_ksums], writes=[b_ksums])

    dma("gp", hs[:, HSB:2 * HSB], kf_d[0], reads=[b_kfd[0]], writes=[b_hsB])
    dma("gp", hs[:, 0:HSB], qf_d[0], reads=[b_qfd[0]], writes=[b_hsA])
    dma("gp", cc_in[:, 0:NHEAD * DH], kvs[:, :], reads=[b_kvs], writes=[b_ccin])
    dma("gp", cc_in[:, NHEAD * DH:], ksums[:, :], reads=[b_ksums], writes=[b_ccin])
    opp("gp", lambda e: e.collective_compute(
        "AllGather", ALU.bypass,
        replica_groups=[[2 * g, 2 * g + 1] for g in range(cfg.NC // 2)],
        ins=[cc_in[:, :]], outs=[cc_out[:, :]]),
        reads=[b_ccin], writes=[b_ccout], inc="coll")
    CCW = NHEAD * DH + NHEAD
    b_gslot = [Buf(), Buf()]
    for r in range(2):
        sl = r % 2
        gview = xs[:, sl * CCW: sl * CCW + CCW]
        dma("sq", gview, cc_out[r * 128:(r + 1) * 128, :],
            reads=[b_ccout], writes=[b_gslot[sl], b_xs])
        if r == 0:
            dve(lambda e, gv=gview, r=r: e.tensor_scalar(
                kvs[:, :], gv[:, 0:NHEAD * DH], prevm[:, r:r + 1], None, ALU.mult),
                reads=[b_gslot[sl], b_xs], writes=[b_kvs], extra=tok_consts)
            dve(lambda e, gv=gview, r=r: e.tensor_scalar(
                ksums[:, :], gv[:, NHEAD * DH:CCW], prevm[:, r:r + 1], None, ALU.mult),
                reads=[b_gslot[sl], b_xs], writes=[b_ksums])
        else:
            dve(lambda e, gv=gview, r=r: e.scalar_tensor_tensor(
                kvs[:, :], gv[:, 0:NHEAD * DH], prevm[:, r:r + 1], kvs[:, :],
                ALU.mult, ALU.add),
                reads=[b_gslot[sl], b_kvs, b_xs], writes=[b_kvs])
            dve(lambda e, gv=gview, r=r: e.scalar_tensor_tensor(
                ksums[:, :], gv[:, NHEAD * DH:CCW], prevm[:, r:r + 1], ksums[:, :],
                ALU.mult, ALU.add),
                reads=[b_gslot[sl], b_ksums, b_xs], writes=[b_ksums])
    dve(lambda e: e.tensor_copy(kvb[:, :], kvs[:, :]), reads=[b_kvs], writes=[b_kvb])
    dve(lambda e: e.tensor_copy(ksumb[:, :], ksums[:, :]), reads=[b_ksums], writes=[b_ksumb])

    for t in range(T):
        if t > 0:
            dma("gp", hs[:, HSB:2 * HSB], kf_d[t], reads=[b_kfd[t]], writes=[b_hsB])
            dma("gp", hs[:, 0:HSB], qf_d[t], reads=[b_qfd[t]], writes=[b_hsA])
        dma("gp", xs[:, :], x1_d[t], reads=[b_x1d[t]], writes=[b_xs])
        for c in range(CPT):
            g = t * CPT + c
            vslot = c % 2
            dma("gp", combs[:, vslot * 4 * NT: vslot * 4 * NT + DM], v_d[g][:, :],
                reads=[b_vd[g]], writes=[b_combs[vslot]])
            qf_h = lambda h: hs[:, h * NT + c * CH: h * NT + c * CH + 128]
            kf_h = lambda h: hs[:, HSB + h * NT + c * CH: HSB + h * NT + c * CH + 128]
            v_h = lambda h: combs[:, vslot * 4 * NT + h * 128: vslot * 4 * NT + (h + 1) * 128]
            for h in range(NHEAD):
                pb = (0, 1, 4, 5)[h // 4]
                mm(P[pb][:, (h % 4) * 128:(h % 4) * 128 + 128], kf_h(h), qf_h(h),
                   start=True, stop=True, reads=[b_hsA, b_hsB], writes=[b_P[pb]])
            for g4 in range(4):
                pb = (0, 1, 4, 5)[g4]
                dve(lambda e, pb=pb, g4=g4: e.tensor_mul(
                    atb[:, g4 * 512:(g4 + 1) * 512], P[pb][:, 0:512],
                    triu16[:, 0:512]),
                    reads=[b_P[pb]], writes=[b_atb], extra=tok_consts)
            for h in range(NHEAD):
                pt = h // 8
                pe(lambda e, h=h, pt=pt, src=kf_h(h): e.transpose(
                    PT[:, pt * 1024 + (h % 8) * 128: pt * 1024 + (h % 8) * 128 + 128],
                    src, iden_b[:, :]), reads=[b_hsB], writes=[b_PT[pt]])
                if h % 8 == 7:
                    act(lambda e, pt=pt, h=h: e.activation(
                        kftm[:, (h - 7) * 128:(h + 1) * 128],
                        PT[:, pt * 1024: pt * 1024 + 1024], AF.Copy),
                        reads=[b_PT[pt]], writes=[b_kftm_sl[pt]])
            for h in range(NHEAD):
                mm(P[3][:, h:h + 1], qf_h(h), ksumb[:, h:h + 1], start=True, stop=False,
                   reads=[b_hsA, b_ksumb], writes=[b_P[3]])
                mm(P[3][:, h:h + 1], atb[:, h * 128:(h + 1) * 128], colones[:, :],
                   start=False, stop=True, reads=[b_atb], writes=[b_P[3]])
            dve(lambda e: e.tensor_scalar_add(zs[:, :], P[3][:, 0:NHEAD], ATT_EPS),
                reads=[b_P[3]], writes=[b_z])
            dve(lambda e: e.reciprocal(zinv[:, :], zs[:, :]), reads=[b_z], writes=[b_zinv])
            for h in range(NHEAD):
                pb = 4 + (h // 4) % 2
                psl = P[pb][:, (h % 4) * 128:(h % 4) * 128 + 128]
                mm(psl, qf_h(h), kvb[:, h * 128:(h + 1) * 128], start=True, stop=False,
                   reads=[b_hsA, b_kvb], writes=[b_P[pb]])
                mm(psl, atb[:, h * 128:(h + 1) * 128], v_h(h), start=False, stop=True,
                   reads=[b_atb, b_combs[vslot]], writes=[b_P[pb]])
                act(lambda e, psl=psl, h=h: e.activation(
                    attntm[:, h * 128:(h + 1) * 128], psl, AF.Copy, scale=zinv[:, h:h + 1]),
                    reads=[b_P[pb], b_zinv], writes=[b_att_sl[h // 8]])
            for h in range(NHEAD):
                pt = h // 8
                pe(lambda e, h=h, pt=pt: e.transpose(
                    PT[:, pt * 1024 + (h % 8) * 128: pt * 1024 + (h % 8) * 128 + 128],
                    attntm[:, h * 128:(h + 1) * 128], iden_b[:, :]),
                   reads=[b_att_sl[h // 8]], writes=[b_PT[pt]])
                if h % 8 == 7:
                    for hh in range(h - 7, h + 1):
                        dve(lambda e, pt=pt, hh=hh, c=c: e.tensor_copy(
                            ys[:, hh * NT + c * CH: hh * NT + c * CH + 128],
                            PT[:, pt * 1024 + (hh % 8) * 128: pt * 1024 + (hh % 8) * 128 + 128]),
                            reads=[b_PT[pt]], writes=[b_ys])
            for h in range(NHEAD):
                pb = 4 + (h // 4) % 2
                mm(P[pb][:, (h % 4) * 128:(h % 4) * 128 + 128],
                   kftm[:, h * 128:(h + 1) * 128], v_h(h), start=True, stop=True,
                   reads=[b_kftm_sl[h // 8], b_combs[vslot]], writes=[b_P[pb]])
                if h % 4 == 3:
                    dve(lambda e, pb=pb, h=h: e.tensor_add(
                        kvs[:, (h - 3) * 128:(h + 1) * 128],
                        kvs[:, (h - 3) * 128:(h + 1) * 128], P[pb][:, 0:512]),
                        reads=[b_P[pb], b_kvs], writes=[b_kvs])
            act(lambda e: e.activation(kvb[:, :], kvs[:, :], AF.Copy), reads=[b_kvs], writes=[b_kvb])
            kf3 = hs[:, HSB:2 * HSB].rearrange("p (j n) -> p j n", j=NHEAD)[:, :, c * CH:(c + 1) * CH]
            dve(lambda e, kf3=kf3: e.tensor_reduce(stm[:, 0:NHEAD], kf3, AX.X, ALU.add),
                reads=[b_hsB], writes=[b_stm])
            dve(lambda e: e.tensor_add(ksums[:, :], ksums[:, :], stm[:, 0:NHEAD]),
                reads=[b_stm, b_ksums], writes=[b_ksums])
            dve(lambda e: e.tensor_copy(ksumb[:, :], ksums[:, :]), reads=[b_ksums], writes=[b_ksumb])
        for mp in range(KC // 2):
            su, _ = stream_unit(wo3, b_wo3, ("wob", mp), wob[mp],
                                "sq" if mp % 2 == 0 else "aq")
            for m in (2 * mp, 2 * mp + 1):
                po = P[4 + (m % 2)]
                for j in range(KC):
                    mm(po[:, 0:NT], arena_tile(wo3, su, j, (m % 2) * 128, 128),
                       ys[:, j * NT:(j + 1) * NT], start=(j == 0), stop=(j == KC - 1),
                       reads=[b_wo3[su], b_ys], writes=[b_P[4 + (m % 2)]])
                dve(lambda e, m=m, po=po: e.scalar_tensor_tensor(
                    xs[:, m * NT:(m + 1) * NT], po[:, 0:NT], bos[:, m:m + 1],
                    xs[:, m * NT:(m + 1) * NT], ALU.add, ALU.add),
                    reads=[b_P[4 + (m % 2)], b_xs], writes=[b_xs])
        emit_ln(b_xs, xs)
        emit_router()
        emit_ffn(True, t)
        dma("aq", y_out[t], xs[:, :], reads=[b_xs], writes=[Buf()])

    with es:
        blk = es.enter_context(nc.Block())

        def make_runner(names, drain=()):
            names = set(names)
            def run(e):
                last = defaultdict(int)
                for (oe, fn, waits, inc_sem, inc_n) in S.ops:
                    if oe not in names:
                        continue
                    for s_, v_ in waits.items():
                        if oe == "pe" and s_ == "pe":
                            continue
                        if v_ > last[s_]:
                            e.wait_ge(sems[s_], v_)
                            last[s_] = v_
                    ins = fn(e)
                    ins.then_inc(sems[inc_sem], inc_n)
                for d in drain:
                    if S.cnt[d]:
                        e.wait_ge(sems[d], S.cnt[d])
            return run

        blk.tensor(make_runner(("pe",)))
        blk.scalar(make_runner(("act", "aq"), drain=[f"aq{i}" for i in range(NDSEM)]))
        blk.vector(make_runner(("dve",)))
        blk.sync(make_runner(("sq",)))
        blk.gpsimd(make_runner(("gp",), drain=[f"gp{i}" for i in range(NDSEM)]))
    build_nc.last_sched = S
    return nc


def _prep_inputs(cfg: CFG, I):
    f32 = lambda a: np.ascontiguousarray(np.asarray(a), dtype=np.float32)
    d = {}
    sw1 = np.asarray(I["L0_ffn_sw1"], dtype=np.float32)
    sw2 = np.asarray(I["L0_ffn_sw2"], dtype=np.float32)
    sw3 = np.asarray(I["L0_ffn_sw3"], dtype=np.float32)
    w12 = np.asarray(I["L0_ffn_w12"], dtype=np.float32)
    w3 = np.asarray(I["L0_ffn_w3"], dtype=np.float32)
    for nm, w in (("sw1p", sw1), ("sw2p", sw2)):
        d[nm] = f32(w.reshape(32, 256, 16, 128).transpose(0, 3, 2, 1)
                    .reshape(32, 128, 4096))
    d["sw3p"] = f32(sw3.reshape(8, 256, 4, 16, 128).transpose(0, 2, 4, 3, 1)
                    .reshape(8, 4, 128, 4096))
    d["w12p1"] = f32(w12[:, :HR, :].reshape(NE, 4, 256, 16, 128)
                     .transpose(0, 1, 4, 3, 2).reshape(NE, 4, 128, 4096))
    d["w12p2"] = f32(w12[:, HR:, :].reshape(NE, 4, 256, 16, 128)
                     .transpose(0, 1, 4, 3, 2).reshape(NE, 4, 128, 4096))
    d["w3p"] = f32(w3.transpose(0, 2, 1).reshape(8, 8, 128, 8, 256)
                   .transpose(3, 0, 1, 2, 4).reshape(8, 4, 16, 128, 256)
                   .transpose(0, 1, 3, 2, 4).reshape(8, 4, 128, 4096))
    for nm, key in (("wqp", "L0_attn_Wq"), ("wkp", "L0_attn_Wk"),
                    ("wvp", "L0_attn_Wv"), ("wop", "L0_attn_Wo")):
        w = np.asarray(I[key], dtype=np.float32)
        d[nm] = f32(w.reshape(16, 128, 8, 256).transpose(2, 1, 0, 3)
                    .reshape(8, 128, 4096))
    rt = f32(np.asarray(I["L0_ffn_router_w"]).T)
    d["routp"] = f32(rt.reshape(KC, 128, NE).transpose(1, 0, 2))
    d["ebbc"] = f32(np.broadcast_to(np.asarray(I["L0_ffn_expert_bias"]), (128, NE)))
    for nm, key in (("bq", "L0_attn_bq"), ("bk", "L0_attn_bk"),
                    ("bv", "L0_attn_bv"), ("bo", "L0_attn_bo")):
        d[nm] = f32(np.asarray(I[key]).reshape(KC, 128).T)
    import ml_dtypes
    d["iden_b"] = np.eye(128, dtype=ml_dtypes.bfloat16)
    d["iden_f"] = np.eye(128, dtype=np.float32)
    d["ones1"] = np.ones((1, 128), dtype=np.float32)
    d["ones1b"] = np.ones((1, 128), dtype=ml_dtypes.bfloat16)
    d["colones"] = np.ones((128, 1), dtype=ml_dtypes.bfloat16)
    d["triu"] = np.triu(np.ones((128, 128), dtype=np.float32))
    d["triu16"] = np.tile(d["triu"], (1, NHEAD)).astype(ml_dtypes.bfloat16)
    return d


def make_in_maps(cfg: CFG, inputs):
    shared = _prep_inputs(cfg, inputs)
    x = np.asarray(inputs["L0_attn_x"], dtype=np.float32)
    in_maps = []
    for c in range(cfg.NC):
        b = c // cfg.HPS
        h = c % cfg.HPS
        sl = x[b, h * cfg.TCORE:(h + 1) * cfg.TCORE, :]
        xt = np.ascontiguousarray(
            sl.reshape(cfg.NTILES, NT, KC, 128).transpose(0, 3, 2, 1)
            .reshape(cfg.NTILES, 128, KC * NT))
        pm = np.zeros((128, 2), dtype=np.float32)
        if c % cfg.HPS == 1:
            pm[:, 0] = 1.0
        m = dict(shared)
        m["x"] = xt
        m["prevm"] = pm
        in_maps.append(m)
    return in_maps


def assemble_out(cfg: CFG, x_shape, per_core_y):
    out = np.empty(x_shape, dtype=np.float32)
    for c in range(cfg.NC):
        b = c // cfg.HPS
        h = c % cfg.HPS
        yt = np.asarray(per_core_y[c])
        yt = (yt.reshape(cfg.NTILES, 128, KC, NT).transpose(0, 3, 2, 1)
              .reshape(cfg.TCORE, DM))
        out[b, h * cfg.TCORE:(h + 1) * cfg.TCORE, :] = yt
    return out


def run(cfg: CFG, inputs, **kw):
    in_maps = make_in_maps(cfg, inputs)
    nc = build_nc(cfg)
    res = run_bass_kernel_spmd(nc, in_maps, core_ids=list(range(cfg.NC)), **kw)
    x = np.asarray(inputs["L0_attn_x"])
    out = assemble_out(cfg, x.shape, [res.results[c]["y"] for c in range(cfg.NC)])
    return out, res


def kernel(**inputs):
    cfg = CFG(8, 4, 8192)
    out, _ = run(cfg, inputs)
    return out
